# version 02
# baseline (speedup 1.0000x reference)
import numpy as np
import ml_dtypes
import concourse.bass as bass
import concourse.mybir as mybir
from concourse.bass_utils import run_bass_kernel_spmd

F32 = mybir.dt.float32
BF16 = mybir.dt.bfloat16
I32 = mybir.dt.int32
AF = mybir.ActivationFunctionType
ALU = mybir.AluOpType
BF16NP = ml_dtypes.bfloat16

NCORE = 8
D = 2048
TOK = 2048
HALO = 3
TOKH = TOK + HALO
KCH = D // 128
INTER = 5504
NOT = INTER // 128
DIN = 4096
EPS_LN = 1e-5


def tok_tiles(T):
    if T == TOKH:
        return [(0, 515, [(0, 256), (256, 259)]), (515, 512, [(0, 512)]),
                (1027, 512, [(0, 512)]), (1539, 512, [(0, 512)])]
    assert T % 512 == 0
    return [(o, 512, [(0, 512)]) for o in range(0, T, 512)]


ENGS = ("tensor", "vector", "scalar", "gpsimd", "sync")


class Sem:
    __slots__ = ("h", "n", "name")


class B:

    def __init__(self, nc):
        self.nc = nc
        self.q = {e: [] for e in ENGS}
        self.sems_all = []

    def sem(self, name):
        s = Sem()
        s.h = self.nc.alloc_semaphore(name)
        s.n = 0
        s.name = name
        self.sems_all.append(s)
        return s

    def _incs(self, i, i16):
        incs = []
        if i is not None:
            i.n += 1
            incs.append((i, 1))
        if i16 is not None:
            i16.n += 16
            incs.append((i16, 16))
        return incs

    def _waits(self, w):
        wl = []
        for (s, v) in w:
            v = int(v)
            assert v <= s.n, f"wait {s.name}>={v} but schedule max {s.n}"
            wl.append((s, v))
        return tuple(wl)

    def op(self, eng, meth, *args, w=(), i=None, i16=None, **kw):
        self.q[eng].append((meth, args, kw, self._waits(w), self._incs(i, i16)))

    def raw(self, eng, fn, w=(), i=None, i16=None):
        self.q[eng].append((fn, None, None, self._waits(w), self._incs(i, i16)))

    @staticmethod
    def _runner(ops):
        def run(e):
            for (meth, args, kw, waits, incs) in ops:
                for (s, v) in waits:
                    e.wait_ge(s.h, v)
                if args is None:
                    ins = meth(e)
                else:
                    ins = getattr(e, meth)(*args, **kw)
                for (s, n) in incs:
                    ins.then_inc(s.h, n)
        return run

    def emit(self):
        with self.nc.Block() as blk:
            for eng in ENGS:
                if self.q[eng]:
                    getattr(blk, eng)(self._runner(self.q[eng]))


def mk_consts():
    return {
        "c_ones_f": np.ones((128, 1), np.float32),
        "c_ones_bf": np.ones((1, 128), BF16NP),
        "c_id_bf": np.eye(128, dtype=BF16NP),
        "c_id_f": np.eye(128, dtype=np.float32),
        "c_triu_bf": np.triu(np.ones((64, 64))).astype(BF16NP),
        "c_onescol_bf": np.ones((128, 1), BF16NP),
        "c_scal": np.tile(np.array([[1e-5, 1e-6, -20.0, 1.0, 0.5, 0.0, -200.0, -0.36787944117144233]],
                                   np.float32), (128, 1)),
    }


class Consts:

    def __init__(self, b, sL):
        nc = b.nc
        self.sem = sL
        self.ones_f = nc.alloc_sbuf_tensor("k_ones_f", [128, 1], F32)
        self.ones_bf = nc.alloc_sbuf_tensor("k_ones_bf", [1, 128], BF16)
        self.id_bf = nc.alloc_sbuf_tensor("k_id_bf", [128, 128], BF16)
        self.id_f = nc.alloc_sbuf_tensor("k_id_f", [128, 128], F32)
        self.triu_bf = nc.alloc_sbuf_tensor("k_triu", [64, 64], BF16)
        self.scal = nc.alloc_sbuf_tensor("k_scal", [128, 8], F32)
        self.ones_bf128 = nc.alloc_sbuf_tensor("k_onescol", [128, 1], BF16)
        d = {}
        d["ones_f"] = nc.dram_tensor("c_ones_f", [128, 1], F32, kind="ExternalInput").ap()
        d["ones_bf"] = nc.dram_tensor("c_ones_bf", [1, 128], BF16, kind="ExternalInput").ap()
        d["id_bf"] = nc.dram_tensor("c_id_bf", [128, 128], BF16, kind="ExternalInput").ap()
        d["id_f"] = nc.dram_tensor("c_id_f", [128, 128], F32, kind="ExternalInput").ap()
        d["triu_bf"] = nc.dram_tensor("c_triu_bf", [64, 64], BF16, kind="ExternalInput").ap()
        d["scal"] = nc.dram_tensor("c_scal", [128, 8], F32, kind="ExternalInput").ap()
        d["onescol"] = nc.dram_tensor("c_onescol_bf", [128, 1], BF16, kind="ExternalInput").ap()
        b.op("sync", "dma_start", out=self.ones_f[:], in_=d["ones_f"], i16=sL)
        b.op("sync", "dma_start", out=self.ones_bf[:], in_=d["ones_bf"], i16=sL)
        b.op("sync", "dma_start", out=self.id_bf[:], in_=d["id_bf"], i16=sL)
        b.op("sync", "dma_start", out=self.id_f[:], in_=d["id_f"], i16=sL)
        b.op("sync", "dma_start", out=self.triu_bf[:], in_=d["triu_bf"], i16=sL)
        b.op("sync", "dma_start", out=self.scal[:], in_=d["scal"], i16=sL)
        b.op("sync", "dma_start", out=self.ones_bf128[:], in_=d["onescol"], i16=sL)
        self.tick = sL.n

    def w(self):
        return (self.sem, self.tick)


def emit_ghostw(b, C, sems, gidx, uidx, didx, lut, wgt_out, wut_out, wdt_out):
    nc = b.nc
    sL, sE, sM, sS = sems
    lutb = nc.alloc_sbuf_tensor("g_lutb", [128, 16], F32)
    b.op("gpsimd", "dma_start",
         out=lutb[:], in_=lut[None, :].broadcast_to((128, 16)), i16=sL)

    for (mat, idx, out_ap, n_ot, hwid, kT) in (
        ("g", gidx, wgt_out, 6, 2048, 16),
        ("u", uidx, wut_out, 6, 2048, 16),
        ("d", didx, wdt_out, 2, 5504, 43),
    ):
        idxf = nc.alloc_sbuf_tensor(f"g_idxf_{mat}", [128, hwid], F32)
        acc = nc.alloc_sbuf_tensor(f"g_acc_{mat}", [128, hwid], BF16)
        tmp = nc.alloc_sbuf_tensor(f"g_tmp_{mat}", [128, hwid], BF16)
        wtb = nc.alloc_sbuf_tensor(f"g_wtb_{mat}", [128, 512], BF16)
        pt = nc.alloc_psum_tensor(f"g_pt_{mat}", [128, 512], BF16)
        for t in range(n_ot):
            b.op("gpsimd", "dma_start", out=idxf[:],
                 in_=idx[t * 128:(t + 1) * 128, :],
                 w=[(sE, sE.n)], i16=sL)
            ldt = sL.n
            b.op("vector", "memset", acc[:], 0.0, w=[(sM, sM.n)])
            for v in range(16):
                wv = [(sL, ldt)] if v == 0 else []
                b.op("vector", "tensor_scalar", tmp[:], idxf[:], float(v),
                     lutb[:, v:v + 1], ALU.is_equal, ALU.mult, w=wv)
                b.op("vector", "tensor_add", acc[:], acc[:], tmp[:],
                     i=(sE if v == 15 else None))
            gat = sE.n
            for k4 in range((kT + 3) // 4):
                ks = [k4 * 4 + j for j in range(4) if k4 * 4 + j < kT]
                for jj, k in enumerate(ks):
                    b.raw("tensor",
                          (lambda e, _a=acc[:, k * 128:(k + 1) * 128],
                           _o=pt[:, jj * 128:(jj + 1) * 128], _i=C.id_bf:
                           e.transpose(_o, _a, _i[:])),
                          w=([(sE, gat), (sE, sE.n)] if jj == 0 else []),
                          i=(sM if jj == len(ks) - 1 else None))
                mmt = sM.n
                b.op("vector", "tensor_copy", wtb[:, 0:len(ks) * 128],
                     pt[:, 0:len(ks) * 128], w=[(sM, mmt), (sS, sS.n)], i=sE)
                evt = sE.n
                for jj, k in enumerate(ks):
                    b.op("sync", "dma_start",
                         out=out_ap[t, :, k, :],
                         in_=wtb[:, jj * 128:(jj + 1) * 128],
                         w=([(sE, evt)] if jj == 0 else []), i16=sS)
    b.op("sync", "nop", w=[(sS, sS.n)])


class LNState:
    def __init__(self, nc, tag, maxw=515):
        self.maxw = maxw
        self.x2 = nc.alloc_sbuf_tensor(f"ln_x2_{tag}", [128, KCH * maxw], F32)
        self.r0 = nc.alloc_sbuf_tensor(f"ln_r0_{tag}", [1, maxw], F32)
        self.r1 = nc.alloc_sbuf_tensor(f"ln_r1_{tag}", [1, maxw], F32)
        self.r2 = nc.alloc_sbuf_tensor(f"ln_r2_{tag}", [1, maxw], F32)
        self.mu_bf = nc.alloc_sbuf_tensor(f"ln_mu_{tag}", [1, maxw], BF16)
        self.rs_bf = nc.alloc_sbuf_tensor(f"ln_rs_{tag}", [1, maxw], BF16)
        self.mubc = nc.alloc_sbuf_tensor(f"ln_mubc_{tag}", [128, maxw], BF16)
        self.rsbc = nc.alloc_sbuf_tensor(f"ln_rsbc_{tag}", [128, maxw], BF16)


def emit_ln(b, C, st, sems, psA, psB, res_tile, xt, Tt, subs, res_tick, out_sem,
            xt_stride=None, xt_off=0):
    sL, sE, sA, sM, sS = sems
    if xt_stride is None:
        xt_stride = Tt
    for k in range(KCH):
        b.op("scalar", "activation", st.x2[:, k * Tt:(k + 1) * Tt],
             res_tile[:, k * Tt:(k + 1) * Tt], AF.Square,
             w=([res_tick, (sM, sM.n)] if k == 0 else []),
             i=(sA if k == KCH - 1 else None))
    x2t = sA.n
    for si, (so, sw) in enumerate(subs):
        for k in range(KCH):
            b.raw("tensor",
                  (lambda e, _o=psA[0:1, 0:sw],
                   _r=res_tile[:, k * Tt + so:k * Tt + so + sw], _k=k:
                   e.matmul(_o, C.ones_f[:], _r, start=(_k == 0),
                            stop=(_k == KCH - 1))),
                  w=([res_tick, (sE, sE.n)] if k == 0 else []))
        for k in range(KCH):
            b.raw("tensor",
                  (lambda e, _o=psB[0:1, 0:sw],
                   _r=st.x2[:, k * Tt + so:k * Tt + so + sw], _k=k:
                   e.matmul(_o, C.ones_f[:], _r, start=(_k == 0),
                            stop=(_k == KCH - 1))),
                  w=([(sA, x2t)] if k == 0 else []),
                  i=(sM if k == KCH - 1 else None))
        smt = sM.n
        b.op("vector", "tensor_scalar", st.r0[0:1, so:so + sw], psA[0:1, 0:sw],
             1.0 / D, None, ALU.mult, w=[(sM, smt)])
        b.op("vector", "tensor_scalar", st.r1[0:1, so:so + sw], psB[0:1, 0:sw],
             1.0 / D, None, ALU.mult)
        b.op("vector", "tensor_mul", st.r2[0:1, so:so + sw],
             st.r0[0:1, so:so + sw], st.r0[0:1, so:so + sw])
        b.op("vector", "tensor_sub", st.r1[0:1, so:so + sw],
             st.r1[0:1, so:so + sw], st.r2[0:1, so:so + sw])
        b.op("vector", "tensor_copy", st.mu_bf[0:1, so:so + sw],
             st.r0[0:1, so:so + sw], i=sE)
        vet = sE.n
        b.op("scalar", "activation", st.r2[0:1, so:so + sw],
             st.r1[0:1, so:so + sw], AF.Ln, bias=C.scal[0:1, 0:1], w=[(sE, vet)])
        b.op("scalar", "activation", st.r2[0:1, so:so + sw],
             st.r2[0:1, so:so + sw], AF.Exp, scale=-0.5, i=sA)
        rst = sA.n
        b.op("vector", "tensor_copy", st.rs_bf[0:1, so:so + sw],
             st.r2[0:1, so:so + sw], w=[(sA, rst)], i=sE)
        cvt = sE.n
        b.raw("tensor",
              (lambda e, _o=psA[:, 0:sw], _r=st.mu_bf[0:1, so:so + sw]:
               e.matmul(_o, C.ones_bf[:], _r, start=True, stop=True)),
              w=[(sE, cvt)])
        b.raw("tensor",
              (lambda e, _o=psB[:, 0:sw], _r=st.rs_bf[0:1, so:so + sw]:
               e.matmul(_o, C.ones_bf[:], _r, start=True, stop=True)), i=sM)
        bct = sM.n
        b.op("vector", "tensor_copy", st.mubc[:, so:so + sw], psA[:, 0:sw],
             w=[(sM, bct)])
        b.op("vector", "tensor_copy", st.rsbc[:, so:so + sw], psB[:, 0:sw], i=sE)
        bce = sE.n
        for k in range(KCH):
            b.op("vector", "tensor_sub", st.x2[:, k * Tt + so:k * Tt + so + sw],
                 res_tile[:, k * Tt + so:k * Tt + so + sw],
                 st.mubc[:, so:so + sw],
                 w=([(sE, bce)] if k == 0 else []))
            b.op("vector", "tensor_mul",
                 xt[:, k * xt_stride + xt_off + so:k * xt_stride + xt_off + so + sw],
                 st.x2[:, k * Tt + so:k * Tt + so + sw],
                 st.rsbc[:, so:so + sw],
                 i=(out_sem if (k == KCH - 1 and si == len(subs) - 1) else None))


def emit_ffn0(b, C, tag, T, resin, resout, wgt, wut, wdt):
    nc = b.nc
    sL = b.sem(f"{tag}_L")
    sW = b.sem(f"{tag}_W")
    sE = b.sem(f"{tag}_E")
    sA = b.sem(f"{tag}_A")
    sM = b.sem(f"{tag}_M")
    sS = b.sem(f"{tag}_S")
    sems = (sL, sE, sA, sM, sS)

    maxw = 515 if T == TOKH else 512
    RES = nc.alloc_sbuf_tensor(f"{tag}_res", [128, KCH * maxw], F32)
    XT = nc.alloc_sbuf_tensor(f"{tag}_xt", [128, KCH * maxw], BF16)
    H = nc.alloc_sbuf_tensor(f"{tag}_h", [128, NOT * maxw], BF16)
    SG = nc.alloc_sbuf_tensor(f"{tag}_sg", [128, maxw], BF16)
    WG = [nc.alloc_sbuf_tensor(f"{tag}_wg{i}", [128, 16 * 128], BF16) for i in range(2)]
    WU = [nc.alloc_sbuf_tensor(f"{tag}_wu{i}", [128, 16 * 128], BF16) for i in range(2)]
    WD = [nc.alloc_sbuf_tensor(f"{tag}_wd{i}", [128, 43 * 128], BF16) for i in range(2)]
    ln = LNState(nc, tag, maxw)
    RO = ln.x2
    PG = [nc.alloc_psum_tensor(f"{tag}_pg{i}", [128, 512], F32) for i in range(2)]
    PU = [nc.alloc_psum_tensor(f"{tag}_pu{i}", [128, 512], F32) for i in range(2)]
    PD = [nc.alloc_psum_tensor(f"{tag}_pd{i}", [128, 512], F32) for i in range(2)]

    prev_store = 0
    for (toff, Tt, subs) in tok_tiles(T):
        for k in range(KCH):
            b.op("sync", "dma_start", out=RES[:, k * Tt:(k + 1) * Tt],
                 in_=resin[k * 128:(k + 1) * 128, toff:toff + Tt],
                 w=([(sS, prev_store), C.w()] if k == 0 else []), i16=sL)
        res_tick = (sL, sL.n)
        sXT = b.sem(f"{tag}_x{toff}")
        emit_ln(b, C, ln, sems, PG[0][:, :], PU[0][:, :], RES, XT, Tt, subs,
                res_tick, sXT)
        xt_tick = (sXT, sXT.n)

        for t in range(NOT):
            par = t % 2
            wv = [(sM, sM.n)] if t >= 2 else []
            b.op("gpsimd", "dma_start", out=WG[par][:],
                 in_=wgt[t, :, :, :].rearrange("p k o -> p (k o)"),
                 w=wv, i16=sW)
            b.op("gpsimd", "dma_start", out=WU[par][:],
                 in_=wut[t, :, :, :].rearrange("p k o -> p (k o)"), i16=sW)
            wt = sW.n
            for (so, sw) in subs:
                for k in range(KCH):
                    b.raw("tensor",
                          (lambda e, _o=PG[par][:, 0:sw],
                           _l=WG[par][:, k * 128:(k + 1) * 128],
                           _r=XT[:, k * Tt + so:k * Tt + so + sw],
                           _k=k: e.matmul(_o, _l, _r, start=(_k == 0),
                                          stop=(_k == KCH - 1))),
                          w=([(sW, wt), xt_tick, (sA, sA.n), (sE, sE.n)]
                             if k == 0 else []))
                for k in range(KCH):
                    b.raw("tensor",
                          (lambda e, _o=PU[par][:, 0:sw],
                           _l=WU[par][:, k * 128:(k + 1) * 128],
                           _r=XT[:, k * Tt + so:k * Tt + so + sw],
                           _k=k: e.matmul(_o, _l, _r, start=(_k == 0),
                                          stop=(_k == KCH - 1))),
                          i=(sM if k == KCH - 1 else None))
                mmt = sM.n
                b.op("scalar", "activation", SG[:, so:so + sw], PG[par][:, 0:sw],
                     AF.Silu, w=[(sM, mmt)], i=sA)
                b.op("vector", "tensor_mul", H[:, t * Tt + so:t * Tt + so + sw],
                     SG[:, so:so + sw], PU[par][:, 0:sw],
                     w=[(sA, sA.n)], i=sE)
        h_tick = (sE, sE.n)

        for m in range(KCH):
            par = m % 2
            wv = [(sM, sM.n)] if m >= 2 else []
            b.op("gpsimd", "dma_start", out=WD[par][:],
                 in_=wdt[m, :, :, :].rearrange("p k o -> p (k o)"), w=wv, i16=sW)
            wt = sW.n
            for (so, sw) in subs:
                for k in range(NOT):
                    b.raw("tensor",
                          (lambda e, _o=PD[par][:, 0:sw],
                           _l=WD[par][:, k * 128:(k + 1) * 128],
                           _r=H[:, k * Tt + so:k * Tt + so + sw],
                           _k=k: e.matmul(_o, _l, _r, start=(_k == 0),
                                          stop=(_k == NOT - 1))),
                          w=([(sW, wt), h_tick, (sE, sE.n)] if k == 0 else []),
                          i=(sM if k == NOT - 1 else None))
                mmt = sM.n
                b.op("vector", "scalar_tensor_tensor",
                     RO[:, m * Tt + so:m * Tt + so + sw], PD[par][:, 0:sw], 0.5,
                     RES[:, m * Tt + so:m * Tt + so + sw], ALU.mult, ALU.add,
                     w=[(sM, mmt)], i=sE)
            b.op("sync", "dma_start",
                 out=resout[m * 128:(m + 1) * 128, toff:toff + Tt],
                 in_=RO[:, m * Tt:(m + 1) * Tt], w=[(sE, sE.n)], i16=sS)
        prev_store = sS.n
    b.op("sync", "nop", w=[(sS, sS.n)])


def build_s0():
    nc = bass.Bass("TRN2", target_bir_lowering=False, debug=False)
    gidx = nc.dram_tensor("gidx", [768, 2048], I32, kind="ExternalInput").ap()
    uidx = nc.dram_tensor("uidx", [768, 2048], I32, kind="ExternalInput").ap()
    didx = nc.dram_tensor("didx", [256, 5504], I32, kind="ExternalInput").ap()
    lut = nc.dram_tensor("lut", [16], F32, kind="ExternalInput").ap()
    wgt = nc.dram_tensor("wgt_s", [6, 128, 16, 128], BF16, kind="ExternalOutput").ap()
    wut = nc.dram_tensor("wut_s", [6, 128, 16, 128], BF16, kind="ExternalOutput").ap()
    wdt = nc.dram_tensor("wdt_s", [2, 128, 43, 128], BF16, kind="ExternalOutput").ap()
    b = B(nc)
    sL, sE, sM, sS = b.sem("L"), b.sem("E"), b.sem("M"), b.sem("S")
    C = Consts(b, sL)
    emit_ghostw(b, C, (sL, sE, sM, sS), gidx, uidx, didx, lut, wgt, wut, wdt)
    b.emit()
    return nc


def build_ffn0_stage(T):
    nc = bass.Bass("TRN2", target_bir_lowering=False, debug=False)
    resin = nc.dram_tensor("resin", [D, T], F32, kind="ExternalInput").ap()
    resout = nc.dram_tensor("resout", [D, T], F32, kind="ExternalOutput").ap()
    wgt = nc.dram_tensor("wgt", [48, 128, 16, 128], BF16, kind="ExternalInput").ap()
    wut = nc.dram_tensor("wut", [48, 128, 16, 128], BF16, kind="ExternalInput").ap()
    wdt = nc.dram_tensor("wdt", [16, 128, 43, 128], BF16, kind="ExternalInput").ap()
    b = B(nc)
    sC = b.sem("C")
    C = Consts(b, sC)
    emit_ffn0(b, C, "f", T, resin, resout, wgt, wut, wdt)
    b.emit()
    return nc


def _shard_x0(x0):
    shards = []
    for c in range(NCORE):
        bb, h = c // 2, c % 2
        if h == 0:
            halo = np.zeros((HALO, D), np.float32)
        else:
            halo = np.asarray(x0[bb, TOK - HALO:TOK, :], np.float32)
        s = np.concatenate([halo, np.asarray(x0[bb, h * TOK:(h + 1) * TOK, :],
                                             np.float32)], axis=0)
        shards.append(np.ascontiguousarray(s.T))
    return shards


def _pad_idx(idx, rows):
    out = np.zeros((rows, idx.shape[1]), np.int32)
    out[:idx.shape[0]] = idx
    return out


def run_ghostw(inputs, consts):
    gidx = _pad_idx(np.asarray(inputs["L0_ffn_gate_idx"], np.int32), 6144)
    uidx = _pad_idx(np.asarray(inputs["L0_ffn_up_idx"], np.int32), 6144)
    didx = np.asarray(inputs["L0_ffn_down_idx"], np.int32)
    lut = np.asarray(inputs["L0_ffn_lut"], np.float32)
    nc0 = build_s0()
    maps0 = [dict(consts,
                  gidx=gidx[c * 768:(c + 1) * 768],
                  uidx=uidx[c * 768:(c + 1) * 768],
                  didx=didx[c * 256:(c + 1) * 256],
                  lut=lut) for c in range(NCORE)]
    r0 = run_bass_kernel_spmd(nc0, maps0, core_ids=list(range(NCORE)))
    wgt = np.ascontiguousarray(np.concatenate(
        [r0.results[c]["wgt_s"] for c in range(NCORE)], axis=0))
    wut = np.ascontiguousarray(np.concatenate(
        [r0.results[c]["wut_s"] for c in range(NCORE)], axis=0))
    wdt = np.ascontiguousarray(np.concatenate(
        [r0.results[c]["wdt_s"] for c in range(NCORE)], axis=0))
    return wgt, wut, wdt


class LN2State:
    def __init__(self, nc, tag):
        self.rs = [nc.alloc_sbuf_tensor(f"l2_rs{i}_{tag}", [128, 512], F32)
                   for i in range(2)]
        self.x2 = [nc.alloc_sbuf_tensor(f"l2_x2{i}_{tag}", [128, 512], F32)
                   for i in range(2)]
        self.tm = nc.alloc_sbuf_tensor(f"l2_tm_{tag}", [128, 512], F32)
        self.r0 = nc.alloc_sbuf_tensor(f"l2_r0_{tag}", [1, 512], F32)
        self.r1 = nc.alloc_sbuf_tensor(f"l2_r1_{tag}", [1, 512], F32)
        self.r2 = nc.alloc_sbuf_tensor(f"l2_r2_{tag}", [1, 512], F32)
        self.mu_bf = nc.alloc_sbuf_tensor(f"l2_mu_{tag}", [1, 512], BF16)
        self.rs_bf = nc.alloc_sbuf_tensor(f"l2_rsb_{tag}", [1, 512], BF16)
        self.mubc = nc.alloc_sbuf_tensor(f"l2_mubc_{tag}", [128, 512], BF16)
        self.rsbc = nc.alloc_sbuf_tensor(f"l2_rsbc_{tag}", [128, 512], BF16)


def emit_ln2(b, C, st, sems, psA, psB, resin, col0, Tt, subs, xt, xt_stride,
             xt_off, out_sem):
    sL, sE, sA, sM, sS = sems
    for si, (so, sw) in enumerate(subs):
        c0 = col0 + so
        ticks = {}
        for k in range(KCH):
            par = k % 2
            b.op("sync", "dma_start", out=st.rs[par][:, 0:sw],
                 in_=resin[k * 128:(k + 1) * 128, c0:c0 + sw],
                 w=[(sE, sE.n), (sM, sM.n if k < 2 else ticks[k - 2]),
                    C.w()], i16=sL)
            ldt = sL.n
            b.op("scalar", "activation", st.x2[par][:, 0:sw], st.rs[par][:, 0:sw],
                 AF.Square,
                 w=[(sL, ldt), (sM, sM.n if k < 2 else ticks[k - 2])], i=sA)
            sqt = sA.n
            b.raw("tensor",
                  (lambda e, _o=psA[0:1, 0:sw], _r=st.rs[par][:, 0:sw],
                   _s=(k == 0), _t=(k == KCH - 1):
                   e.matmul(_o, C.ones_f[:], _r, start=_s, stop=_t)),
                  w=[(sL, ldt)] + ([(sE, sE.n)] if k == 0 else []))
            b.raw("tensor",
                  (lambda e, _o=psB[0:1, 0:sw], _r=st.x2[par][:, 0:sw],
                   _s=(k == 0), _t=(k == KCH - 1):
                   e.matmul(_o, C.ones_f[:], _r, start=_s, stop=_t)),
                  w=[(sA, sqt)], i=sM)
            ticks[k] = sM.n
        smt = sM.n
        b.op("vector", "tensor_scalar", st.r0[0:1, 0:sw], psA[0:1, 0:sw],
             1.0 / D, None, ALU.mult, w=[(sM, smt)])
        b.op("vector", "tensor_scalar", st.r1[0:1, 0:sw], psB[0:1, 0:sw],
             1.0 / D, None, ALU.mult)
        b.op("vector", "tensor_mul", st.r2[0:1, 0:sw],
             st.r0[0:1, 0:sw], st.r0[0:1, 0:sw])
        b.op("vector", "tensor_sub", st.r1[0:1, 0:sw],
             st.r1[0:1, 0:sw], st.r2[0:1, 0:sw])
        b.op("vector", "tensor_copy", st.mu_bf[0:1, 0:sw],
             st.r0[0:1, 0:sw], i=sE)
        vet = sE.n
        b.op("scalar", "activation", st.r2[0:1, 0:sw],
             st.r1[0:1, 0:sw], AF.Ln, bias=C.scal[0:1, 0:1], w=[(sE, vet)])
        b.op("scalar", "activation", st.r2[0:1, 0:sw],
             st.r2[0:1, 0:sw], AF.Exp, scale=-0.5, i=sA)
        rst = sA.n
        b.op("vector", "tensor_copy", st.rs_bf[0:1, 0:sw],
             st.r2[0:1, 0:sw], w=[(sA, rst)], i=sE)
        cvt = sE.n
        b.raw("tensor",
              (lambda e, _o=psA[:, 0:sw], _r=st.mu_bf[0:1, 0:sw]:
               e.matmul(_o, C.ones_bf[:], _r, start=True, stop=True)),
              w=[(sE, cvt)])
        b.raw("tensor",
              (lambda e, _o=psB[:, 0:sw], _r=st.rs_bf[0:1, 0:sw]:
               e.matmul(_o, C.ones_bf[:], _r, start=True, stop=True)), i=sM)
        bct = sM.n
        b.op("vector", "tensor_copy", st.mubc[:, 0:sw], psA[:, 0:sw],
             w=[(sM, bct)])
        b.op("vector", "tensor_copy", st.rsbc[:, 0:sw], psB[:, 0:sw], i=sE)
        bce = sE.n
        for k in range(KCH):
            par = k % 2
            b.op("sync", "dma_start", out=st.rs[par][:, 0:sw],
                 in_=resin[k * 128:(k + 1) * 128, c0:c0 + sw],
                 w=[(sE, sE.n)], i16=sL)
            ldt = sL.n
            b.op("vector", "tensor_sub", st.tm[:, 0:sw], st.rs[par][:, 0:sw],
                 st.mubc[:, 0:sw], w=[(sL, ldt), (sE, bce)])
            xo = k * xt_stride + xt_off + so
            b.op("vector", "tensor_mul", xt[:, xo:xo + sw],
                 st.tm[:, 0:sw], st.rsbc[:, 0:sw],
                 i=(out_sem if (k == KCH - 1 and si == len(subs) - 1) else sE))


SUBS_A = [(0, 512), (512, 512), (1024, 512), (1536, 256), (1792, 259)]
SUBS_R = [(0, 512), (512, 512), (1024, 512), (1536, 512)]


def emit_mamba(b, C, tag, resin, resout, Wm, hpout, scr, stop_after=99,
               nblocks=4, dbg=(None, None, None, None)):
    nc = b.nc
    sL = b.sem(f"{tag}_L")
    sW = b.sem(f"{tag}_W")
    sE = b.sem(f"{tag}_E")
    sG = b.sem(f"{tag}_G")
    sA = b.sem(f"{tag}_A")
    sM = b.sem(f"{tag}_M")
    sS = b.sem(f"{tag}_S")
    sems = (sL, sE, sA, sM, sS)
    xraw, yscr, szscr = scr

    XT = nc.alloc_sbuf_tensor(f"{tag}_xt", [128, 32880], BF16)
    XB = XT[:, 0:16480]
    VF = XT[:, 16480:32864]
    YFM = XT[:, 0:16384]
    RH = XT[:, 0:32768]
    Bmat = nc.alloc_sbuf_tensor(f"{tag}_bm", [64, TOKH], BF16)
    Cmat = nc.alloc_sbuf_tensor(f"{tag}_cm", [64, TOKH], BF16)
    LA = nc.alloc_sbuf_tensor(f"{tag}_la", [64, TOKH], F32)
    VT0 = nc.alloc_sbuf_tensor(f"{tag}_vt", [64, 4096], BF16)
    VT = [VT0, VT0]
    HP = nc.alloc_sbuf_tensor(f"{tag}_hp", [64, 4096], F32)
    GA = nc.alloc_sbuf_tensor(f"{tag}_ga", [64, 4096], BF16)
    YSB = nc.alloc_sbuf_tensor(f"{tag}_ysb", [64, 4096], BF16)
    DID = nc.alloc_sbuf_tensor(f"{tag}_did", [64, 4096], BF16)
    DB64 = nc.alloc_sbuf_tensor(f"{tag}_db64", [64, 64], BF16)
    CSF = nc.alloc_sbuf_tensor(f"{tag}_csf", [64, 64], F32)
    RSF = nc.alloc_sbuf_tensor(f"{tag}_rsf", [64, 64], F32)
    RST = nc.alloc_sbuf_tensor(f"{tag}_rst", [64, 64], F32)
    T16 = nc.alloc_sbuf_tensor(f"{tag}_t16", [64, 1024], F32)
    EXPT = nc.alloc_sbuf_tensor(f"{tag}_expt", [64, 64], F32)
    DEC2 = nc.alloc_sbuf_tensor(f"{tag}_dec2", [64, 64], BF16)
    BTM = nc.alloc_sbuf_tensor(f"{tag}_btm", [64, 64], BF16)
    MB = nc.alloc_sbuf_tensor(f"{tag}_mb", [64, 64], BF16)
    TRIU2 = nc.alloc_sbuf_tensor(f"{tag}_triu2", [64, 64], F32)
    TOTC = nc.alloc_sbuf_tensor(f"{tag}_totc", [64, 1], F32)
    DRW2 = nc.alloc_sbuf_tensor(f"{tag}_drw2", [1, 64], BF16)
    DBCC = nc.alloc_sbuf_tensor(f"{tag}_dbcc", [64, 64], BF16)
    CT = [nc.alloc_sbuf_tensor(f"{tag}_ct{i}", [128, 512], F32) for i in range(2)]
    E1 = CT[0][0:64, :]
    E3 = CT[1][0:64, :]
    E2 = T16[0:64, 0:512]
    DTT = T16[0:64, 512:1024]
    ONES64 = nc.alloc_sbuf_tensor(f"{tag}_on64", [1, 64], BF16)
    ONESB = nc.alloc_sbuf_tensor(f"{tag}_onb", [128, 1], BF16)
    CW = nc.alloc_sbuf_tensor(f"{tag}_cw", [128, 128], F32)
    CB = nc.alloc_sbuf_tensor(f"{tag}_cb", [128, 32], F32)
    NS = nc.alloc_sbuf_tensor(f"{tag}_ns", [128, 32], F32)
    NA = nc.alloc_sbuf_tensor(f"{tag}_na", [64, 1], F32)
    DTB = nc.alloc_sbuf_tensor(f"{tag}_dtb", [64, 1], F32)
    DRW = nc.alloc_sbuf_tensor(f"{tag}_drw", [1, 64], F32)
    RSTDBC = nc.alloc_sbuf_tensor(f"{tag}_rsbc", [128, TOK], BF16)
    RR1 = nc.alloc_sbuf_tensor(f"{tag}_rr1", [1, 64], BF16)
    WB = [nc.alloc_sbuf_tensor(f"{tag}_wb{i}", [128, 4096], BF16) for i in range(2)]
    XRT = nc.alloc_sbuf_tensor(f"{tag}_xrt", [128, TOKH + 1], BF16)
    SZT = XRT[:, 0:2048]
    T1Q = XRT[0:64, 0:2048].bitcast(F32)
    RSROWB = nc.alloc_sbuf_tensor(f"{tag}_rsrow", [1, 4096], BF16)
    SJT = [nc.alloc_sbuf_tensor(f"{tag}_sjt{i}", [128, 1024], BF16) for i in range(2)]
    YJT = [nc.alloc_sbuf_tensor(f"{tag}_yjt{i}", [128, 1024], BF16) for i in range(2)]
    RESCH = nc.alloc_sbuf_tensor(f"{tag}_resch", [128, 512], F32)
    ROCH = nc.alloc_sbuf_tensor(f"{tag}_roch", [128, 512], F32)
    TOTB = nc.alloc_sbuf_tensor(f"{tag}_totb", [1, 64], BF16)
    SSQ = nc.alloc_sbuf_tensor(f"{tag}_ssq", [64, 5], F32)
    RSC = nc.alloc_sbuf_tensor(f"{tag}_rsc", [64, 1], F32)
    l2 = LN2State(nc, tag)

    PZ = [nc.alloc_psum_tensor(f"{tag}_pz{i}", [128, 512], F32) for i in range(2)]
    PYA = nc.alloc_psum_tensor(f"{tag}_pya", [64, 1024], F32)
    PYB = nc.alloc_psum_tensor(f"{tag}_pyb", [64, 1024], F32)
    PBF = nc.alloc_psum_tensor(f"{tag}_pbf", [64, 1024], BF16)
    PYT = nc.alloc_psum_tensor(f"{tag}_pyt", [128, 1024], BF16)

    b.op("sync", "dma_start", out=CW[:],
         in_=Wm["cwq"].rearrange("p q j -> p (q j)"), i16=sL)
    b.op("sync", "dma_start", out=CB[:], in_=Wm["cbv"], i16=sL)
    b.op("sync", "dma_start", out=NS[:], in_=Wm["nsv"], i16=sL)
    b.op("sync", "dma_start", out=NA[:], in_=Wm["alog"], i16=sL)
    b.op("sync", "dma_start", out=DTB[:], in_=Wm["dtb"], i16=sL)
    b.op("sync", "dma_start", out=DRW[:], in_=Wm["drow"], i16=sL)
    b.op("sync", "dma_start", out=HP[:], in_=Wm["hpin"], i16=sL)
    pld = sL.n
    b.op("scalar", "activation", NA[:], NA[:], AF.Exp, w=[(sL, pld), C.w()])
    b.op("scalar", "activation", DRW2[:], DRW[:], AF.Identity, i=sA)
    b.op("vector", "tensor_scalar", NA[:], NA[:], -1.0, None, ALU.mult,
         w=[(sA, sA.n)])
    b.op("vector", "memset", ONES64[:], 1.0)
    b.op("vector", "memset", ONESB[:], 1.0, i=sE)
    o64t = sE.n
    b.raw("tensor", (lambda e: e.matmul(PZ[0][0:64, 0:64], ONES64[:], DRW2[:],
                                        start=True, stop=True)),
          w=[(sE, o64t), (sA, sA.n)], i=sM)
    b.op("vector", "tensor_copy", DB64[:], PZ[0][0:64, 0:64], w=[(sM, sM.n)], i=sE)
    b.op("vector", "tensor_scalar", TRIU2[:], C.triu_bf[:], 200.0, None,
         ALU.mult, w=[C.w()])
    b.op("vector", "tensor_tensor",
         DID[:].rearrange("p (h s) -> p h s", h=64),
         C.id_bf[0:64, 0:64][:, None, :].broadcast_to((64, 64, 64)),
         DB64[:, :, None].broadcast_to((64, 64, 64)),
         ALU.mult, w=[(sE, sE.n)], i=sE)
    did_t = sE.n

    sXT = b.sem(f"{tag}_sxt")
    emit_ln2(b, C, l2, sems, PZ[0][:, :], PZ[1][:, :], resin, 0, TOKH, SUBS_A,
             XT, TOKH, 0, sXT)
    xt_tick = (sXT, sXT.n)
    if dbg[2] is not None:
        for k in range(KCH):
            b.op("sync", "dma_start",
                 out=dbg[2][k * 128:(k + 1) * 128, :],
                 in_=XT[:, k * TOKH:(k + 1) * TOKH],
                 w=([xt_tick] if k == 0 else []), i16=sS)

    a_tiles = [("x", 4096 + j * 128, 128, j) for j in range(32)]
    a_tiles += [("B", 8192, 64, None), ("C", 8256, 64, None), ("dt", 8320, 64, None)]
    for idx_t, (kind, c0, ow, j) in enumerate(a_tiles):
        par = idx_t % 2
        wv = [(sM, sM.n)] if idx_t >= 2 else []
        b.op("gpsimd", "dma_start",
             out=WB[par][:, 0:16 * ow].rearrange("p (kk o) -> p kk o", o=ow),
             in_=Wm["ipw"][0:2048, c0:c0 + ow].rearrange("(kk p) o -> p kk o", p=128),
             w=wv, i16=sW)
        wt = sW.n
        for (so, sw) in SUBS_A:
            pp = PZ[par]
            first = (so, sw) == SUBS_A[0]
            for kk in range(KCH):
                b.raw("tensor",
                      (lambda e, _o=pp[0:ow, 0:sw],
                       _l=WB[par][:, kk * ow:(kk + 1) * ow],
                       _r=XT[:, kk * TOKH + so:kk * TOKH + so + sw],
                       _s=(kk == 0), _t=(kk == KCH - 1):
                       e.matmul(_o, _l, _r, start=_s, stop=_t)),
                      w=(([(sW, wt), xt_tick] if first else []) +
                         ([(sE, sE.n), (sA, sA.n)] if kk == 0 else [])),
                      i=(sM if kk == KCH - 1 else None))
            mmt = sM.n
            if kind == "x":
                b.op("vector", "tensor_copy", XRT[:, so:so + sw], pp[:, 0:sw],
                     w=[(sM, mmt)] + ([(sS, sS.n)] if first else []), i=sE)
            elif kind in ("B", "C"):
                dst = Bmat if kind == "B" else Cmat
                b.op("vector", "tensor_scalar_min", E1[:, 0:sw], pp[0:64, 0:sw],
                     0.0, w=[(sM, mmt)], i=sE)
                b.op("scalar", "activation", E2[:, 0:sw], E1[:, 0:sw],
                     AF.Exp, w=[(sE, sE.n)], i=sA)
                b.op("vector", "tensor_relu", E3[:, 0:sw], pp[0:64, 0:sw])
                b.op("vector", "tensor_add", dst[0:64, so:so + sw],
                     E2[:, 0:sw], E3[:, 0:sw], w=[(sA, sA.n)], i=sE)
            else:
                b.op("scalar", "activation", E2[:, 0:sw], pp[0:64, 0:sw],
                     AF.Abs, bias=DTB[:, 0:1], w=[(sM, mmt), (sE, sE.n)])
                b.op("scalar", "activation", E2[:, 0:sw], E2[:, 0:sw],
                     AF.Exp, scale=-1.0)
                b.op("scalar", "activation", E2[:, 0:sw], E2[:, 0:sw],
                     AF.Ln, bias=C.scal[0:64, 3:4])
                b.op("scalar", "activation", E3[:, 0:sw], pp[0:64, 0:sw],
                     AF.Relu, bias=DTB[:, 0:1], i=sA)
                b.op("vector", "tensor_add", DTT[:, 0:sw], E2[:, 0:sw],
                     E3[:, 0:sw], w=[(sA, sA.n)])
                b.op("vector", "tensor_scalar", LA[0:64, so:so + sw],
                     DTT[:, 0:sw], NA[:, 0:1], None, ALU.mult, i=sE)
        if kind == "x":
            b.op("sync", "dma_start", out=xraw[j, :, :], in_=XRT[:, 0:TOKH],
                 w=[(sE, sE.n)], i16=sS)
    m1_mm = sM.n
    m1_ve = sE.n
    xraw_st = sS.n
    if stop_after <= 1:
        return

    yfm_st = xraw_st
    scan_add_t = sE.n
    for sb in range(nblocks):
        for j in range(32):
            b.op("sync", "dma_start", out=XB[:, j * 515:j * 515 + 515],
                 in_=xraw[j, :, sb * 512:sb * 512 + 515],
                 w=([(sS, yfm_st), (sM, m1_mm), (sE, m1_ve), (sA, sA.n)]
                    if j == 0 else []), i16=sL)
        xb_t = sL.n
        conv_ticks = {}
        for j in range(32):
            eng = "vector"
            ssem = sE if eng == "vector" else sG
            par = j % 2
            b.op(eng, "tensor_scalar", CT[par][:, 0:512],
                 XB[:, j * 515 + 0:j * 515 + 512], CW[:, j:j + 1], None,
                 ALU.mult, w=[(sL, xb_t), (sA, sA.n), (sM, sM.n)])
            for q in (1, 2, 3):
                b.op(eng, "scalar_tensor_tensor", CT[par][:, 0:512],
                     XB[:, j * 515 + q:j * 515 + q + 512],
                     CW[:, q * 32 + j:q * 32 + j + 1], CT[par][:, 0:512],
                     ALU.mult, ALU.add, i=(ssem if q == 3 else None))
            conv_ticks[j] = (ssem, ssem.n)
            b.op("scalar", "activation", VF[:, j * 512:(j + 1) * 512],
                 CT[par][:, 0:512], AF.Silu, bias=CB[:, j:j + 1],
                 w=[conv_ticks[j]], i=sA)
        conv_t = sA.n
        if dbg[3] is not None and sb == 0:
            b.op("sync", "dma_start", out=dbg[3][:, 0:16384], in_=VF[:, :],
                 w=[(sA, conv_t)], i16=sS)
        if stop_after <= 2:
            return
        for c in range(8):
            gc = sb * 8 + c
            gcol = 3 + gc * 64
            pv = c % 2
            for g in range(4):
                for jj in range(8):
                    j = g * 8 + jj
                    b.raw("tensor",
                          (lambda e, _o=PBF[0:64, jj * 128:(jj + 1) * 128],
                           _i=VF[:, j * 512 + c * 64:j * 512 + c * 64 + 64]:
                           e.transpose(_o, _i, C.id_bf[:])),
                          w=([(sA, conv_t), (sE, sE.n)] if jj == 0 else []),
                          i=(sM if jj == 7 else None))
                b.op("vector", "tensor_copy",
                     VT[pv][:, g * 1024:(g + 1) * 1024], PBF[0:64, 0:1024],
                     w=[(sM, sM.n)], i=sE)
            vt_t = (sE, sE.n)
            b.op("vector", "tensor_tensor_scan", CSF[:],
                 ONESB[0:64, 0:1].broadcast_to((64, 64)),
                 LA[0:64, gcol:gcol + 64], 0.0, ALU.mult, ALU.add,
                 w=[(sA, sA.n), (sM, sM.n)], i=sE)
            b.op("vector", "tensor_copy", TOTC[:], CSF[:, 63:64],
                 w=[(sE, sE.n)], i=sE)
            b.op("vector", "tensor_tensor", RSF[:], CSF[:],
                 TOTC[:, 0:1].broadcast_to((64, 64)), ALU.subtract,
                 w=[(sE, sE.n)], i=sE)
            cs_t = sE.n
            b.op("gpsimd", "dma_start", out=RSROWB[:, :], in_=RSF[:, :],
                 w=[(sE, cs_t), (sM, sM.n)], i16=sL)
            rsrow_t = sL.n
            b.raw("tensor", (lambda e: e.transpose(PZ[0][0:64, 0:64], CSF[:],
                                                   C.id_f[0:64, 0:64])),
                  w=[(sE, cs_t), (sA, sA.n), (sE, sE.n)])
            b.raw("tensor", (lambda e: e.transpose(PZ[0][0:64, 64:128], RSF[:],
                                                   C.id_f[0:64, 0:64])))
            b.raw("tensor", (lambda e: e.transpose(PZ[0][0:1, 128:192],
                                                   CSF[:, 63:64],
                                                   C.id_f[0:64, 0:64])),
                  i=sM)
            tr_t = sM.n
            b.op("scalar", "activation", EXPT[:], PZ[0][0:64, 0:64], AF.Exp,
                 w=[(sM, tr_t)])
            b.op("scalar", "activation", DEC2[:], PZ[0][0:64, 64:128], AF.Exp,
                 scale=-1.0)
            b.op("scalar", "activation", RST[:], PZ[0][0:64, 64:128],
                 AF.Identity)
            b.op("scalar", "activation", TOTB[:], PZ[0][0:1, 128:192], AF.Exp,
                 i=sA)
            act_t = sA.n
            b.raw("tensor", (lambda e, _l=Bmat[0:64, gcol:gcol + 64],
                             _r=Cmat[0:64, gcol:gcol + 64]:
                             e.matmul(PZ[0][0:64, 192:256], _l, _r,
                                      start=True, stop=True)),
                  w=[(sA, act_t), (sE, sE.n)])
            b.raw("tensor", (lambda e: e.matmul(PZ[0][0:64, 256:320], ONES64[:],
                                                TOTB[:], start=True, stop=True)))
            b.raw("tensor", (lambda e, _i=Bmat[0:64, gcol:gcol + 64]:
                             e.transpose(PBF[0:64, 0:64], _i,
                                         C.id_bf[0:64, 0:64])),
                  i=sM)
            pe2_t = sM.n
            b.op("vector", "tensor_copy", MB[:], PZ[0][0:64, 192:256],
                 w=[(sM, pe2_t)])
            b.op("vector", "tensor_copy", DBCC[:], PZ[0][0:64, 256:320])
            b.op("vector", "tensor_copy", BTM[:], PBF[0:64, 0:64], i=sE)
            ev2_t = sE.n
            b.op("vector", "tensor_copy", YSB[:, 0:2048], HP[0:64, 0:2048],
                 w=[(sE, scan_add_t), (sM, sM.n), (sE, ev2_t)])
            b.op("vector", "tensor_copy", YSB[:, 2048:4096],
                 HP[0:64, 2048:4096], i=sE)
            hpb_t = sE.n
            yq_t = {}
            for q4 in range(4):
                h0 = q4 * 16
                qs = slice(q4 * 1024, (q4 + 1) * 1024)
                for i2 in range(2):
                    b.raw("tensor",
                          (lambda e, _o=PYA[0:64, i2 * 512:(i2 + 1) * 512],
                           _r=RSROWB[0:1, q4 * 1024 + i2 * 512:
                                     q4 * 1024 + (i2 + 1) * 512]:
                           e.matmul(_o, ONES64[:], _r, start=True, stop=True)),
                          w=([(sL, rsrow_t), (sE, hpb_t), (sA, sA.n),
                              (sE, yq_t.get(q4 - 1, scan_add_t))]
                             if i2 == 0 else []),
                          i=(sM if i2 == 1 else None))
                css_t = sM.n
                b.op("vector", "tensor_tensor",
                     T1Q[:].rearrange("p (h s) -> p h s", h=16),
                     TRIU2[:, None, :].broadcast_to((64, 16, 64)),
                     PYA[0:64, 0:1024].rearrange("p (h s) -> p h s", h=16),
                     ALU.add, w=[(sM, css_t), (sA, sA.n)])
                b.op("vector", "tensor_tensor",
                     T1Q[:].rearrange("p (h s) -> p h s", h=16),
                     T1Q[:].rearrange("p (h s) -> p h s", h=16),
                     RST[:, h0:h0 + 16][:, :, None].broadcast_to((64, 16, 64)),
                     ALU.subtract, i=sE)
                t1_t = sE.n
                b.op("scalar", "activation", GA[:, qs], T1Q[:], AF.Exp,
                     bias=C.scal[0:64, 6:7], w=[(sE, t1_t), (sM, sM.n)], i=sA)
                ga_e = sA.n
                b.op("vector", "tensor_tensor",
                     GA[:, qs].rearrange("p (h s) -> p h s", h=16),
                     GA[:, qs].rearrange("p (h s) -> p h s", h=16),
                     MB[:, None, :].broadcast_to((64, 16, 64)), ALU.mult,
                     w=[(sA, ga_e)])
                b.op("vector", "tensor_add", GA[:, qs], GA[:, qs],
                     DID[:, qs], i=sE)
                ga_t = sE.n
                for hh in range(16):
                    h = h0 + hh
                    wv = ([(sE, ga_t)] + ([vt_t] if q4 == 0 else [])
                          if hh == 0 else [])
                    b.raw("tensor",
                          (lambda e, _o=PYA[0:64, hh * 64:(hh + 1) * 64],
                           _l=GA[:, h * 64:(h + 1) * 64],
                           _r=VT[pv][:, h * 64:(h + 1) * 64]:
                           e.matmul(_o, _l, _r, start=True, stop=True)),
                          w=wv)
                    b.raw("tensor",
                          (lambda e, _o=PYB[0:64, hh * 64:(hh + 1) * 64],
                           _l=Cmat[0:64, gcol:gcol + 64],
                           _r=YSB[:, h * 64:(h + 1) * 64]:
                           e.matmul(_o, _l, _r, start=True, stop=True)),
                          i=(sM if hh == 15 else None))
                ymm_t = sM.n
                b.op("vector", "tensor_tensor",
                     T16[:].rearrange("p (h q) -> p h q", h=16),
                     PYB[0:64, 0:1024].rearrange("p (h q) -> p h q", h=16),
                     EXPT[:, h0:h0 + 16][:, :, None].broadcast_to((64, 16, 64)),
                     ALU.mult, w=[(sM, ymm_t)], i=sE)
                b.op("vector", "tensor_add", YSB[:, qs], PYA[0:64, 0:1024],
                     T16[:], w=[(sE, sE.n)], i=sE)
                yq_t[q4] = sE.n
            ysb_all = sE.n
            b.op("vector", "tensor_tensor",
                 GA[:].rearrange("p (h n) -> p h n", h=64),
                 BTM[:, None, :].broadcast_to((64, 64, 64)),
                 DEC2[:, :, None].broadcast_to((64, 64, 64)), ALU.mult,
                 w=[(sM, sM.n)], i=sE)
            ks_t = sE.n
            for hw in range(2):
                for hh in range(32):
                    h = hw * 32 + hh
                    pa = PYA if hh < 16 else PYB
                    b.raw("tensor",
                          (lambda e, _o=pa[0:64, (hh % 16) * 64:(hh % 16 + 1) * 64],
                           _l=GA[:, h * 64:(h + 1) * 64],
                           _r=VT[pv][:, h * 64:(h + 1) * 64]:
                           e.matmul(_o, _l, _r, start=True, stop=True)),
                          w=([(sE, ks_t if hw == 0 else sE.n)] if hh == 0 else []),
                          i=(sM if hh == 31 else None))
                st_t = sM.n
                b.op("vector", "tensor_tensor",
                     HP[0:64, hw * 2048:(hw + 1) * 2048].rearrange(
                         "p (h q) -> p h q", h=32),
                     HP[0:64, hw * 2048:(hw + 1) * 2048].rearrange(
                         "p (h q) -> p h q", h=32),
                     DBCC[:, hw * 32:(hw + 1) * 32][:, :, None].broadcast_to(
                         (64, 32, 64)),
                     ALU.mult, w=[(sM, st_t)], i=sE)
                b.op("vector", "tensor_add",
                     HP[0:64, hw * 2048:hw * 2048 + 1024],
                     HP[0:64, hw * 2048:hw * 2048 + 1024], PYA[0:64, 0:1024],
                     w=[(sE, sE.n)])
                b.op("vector", "tensor_add",
                     HP[0:64, hw * 2048 + 1024:hw * 2048 + 2048],
                     HP[0:64, hw * 2048 + 1024:hw * 2048 + 2048],
                     PYB[0:64, 0:1024], i=sE)
                scan_add_t = sE.n
            b.op("scalar", "activation", PYA[0:64, 0:1024], YSB[:, 0:1024],
                 AF.Square, accum_out=SSQ[:, 0:1],
                 w=[(sE, scan_add_t)])
            b.op("scalar", "activation", PYA[0:64, 0:1024], YSB[:, 1024:2048],
                 AF.Square, accum_out=SSQ[:, 1:2])
            b.op("scalar", "activation", PYB[0:64, 0:1024], YSB[:, 2048:3072],
                 AF.Square, accum_out=SSQ[:, 2:3])
            b.op("scalar", "activation", PYB[0:64, 0:1024], YSB[:, 3072:4096],
                 AF.Square, accum_out=SSQ[:, 3:4], i=sA)
            sq_t = sA.n
            b.op("vector", "tensor_reduce", SSQ[:, 4:5], SSQ[:, 0:4],
                 mybir.AxisListType.X, ALU.add, w=[(sA, sq_t)], i=sE)
            b.op("scalar", "activation", SSQ[:, 4:5], SSQ[:, 4:5], AF.Ln,
                 scale=1.0 / DIN, bias=C.scal[0:64, 1:2], w=[(sE, sE.n)])
            b.op("scalar", "activation", RSC[:, 0:1], SSQ[:, 4:5], AF.Exp,
                 scale=-0.5, i=sA)
            rsc_t = sA.n
            b.raw("tensor", (lambda e: e.transpose(PZ[1][0:1, 0:64],
                                                   RSC[:, 0:1],
                                                   C.id_f[0:64, 0:64])),
                  w=[(sA, rsc_t), (sE, sE.n)], i=sM)
            b.op("vector", "tensor_copy", RR1[0:1, 0:64], PZ[1][0:1, 0:64],
                 w=[(sM, sM.n)], i=sE)
            b.raw("tensor", (lambda e: e.matmul(PZ[1][:, 64:128], C.ones_bf[:],
                                                RR1[0:1, 0:64],
                                                start=True, stop=True)),
                  w=[(sE, sE.n)], i=sM)
            b.op("vector", "tensor_copy", RSTDBC[:, gc * 64:(gc + 1) * 64],
                 PZ[1][:, 64:128], w=[(sM, sM.n)], i=sE)
            YFMr = YFM.rearrange("p (j t) -> p j t", t=512)
            PYTr = PYT[:, 0:1024].rearrange("p (j t) -> p j t", t=64)
            for g in range(2):
                for jj in range(16):
                    j = g * 16 + jj
                    b.raw("tensor",
                          (lambda e, _o=PYT[:, jj * 64:(jj + 1) * 64],
                           _i=YSB[:, j * 128:(j + 1) * 128]:
                           e.transpose(_o, _i, C.id_bf[0:64, 0:64])),
                          w=([(sE, ysb_all), (sE, sE.n), (sA, sq_t)]
                             if jj == 0 else []),
                          i=(sM if jj == 15 else None))
                b.op("vector", "tensor_copy",
                     YFMr[:, g * 16:(g + 1) * 16, c * 64:c * 64 + 64],
                     PYTr, w=[(sM, sM.n)], i=sE)
        for j in range(32):
            b.op("sync", "dma_start", out=yscr[j, :, sb * 512:(sb + 1) * 512],
                 in_=YFM[:, j * 512:(j + 1) * 512],
                 w=([(sE, sE.n)] if j == 0 else []), i16=sS)
        yfm_st = sS.n
    if stop_after <= 3:
        return

    sX2 = b.sem(f"{tag}_sx2")
    emit_ln2(b, C, l2, sems, PZ[0][:, :], PZ[1][:, :], resin, HALO, TOK, SUBS_R,
             XT, TOK, 0, sX2)
    x2_tick = (sX2, sX2.n)
    if stop_after <= 3.5:
        return
    for j in range(32):
        par = j % 2
        wv = [(sM, sM.n)] if j >= 2 else []
        b.op("gpsimd", "dma_start",
             out=WB[par][:, 0:2048].rearrange("p (kk o) -> p kk o", o=128),
             in_=Wm["ipw"][0:2048, j * 128:(j + 1) * 128].rearrange(
                 "(kk p) o -> p kk o", p=128), w=wv, i16=sW)
        wt = sW.n
        for (so, sw) in SUBS_R:
            first = (so, sw) == SUBS_R[0]
            for kk in range(KCH):
                b.raw("tensor",
                      (lambda e, _o=PZ[par][:, 0:sw],
                       _l=WB[par][:, kk * 128:(kk + 1) * 128],
                       _r=XT[:, kk * TOK + so:kk * TOK + so + sw],
                       _s=(kk == 0), _t=(kk == KCH - 1):
                       e.matmul(_o, _l, _r, start=_s, stop=_t)),
                      w=((([(sW, wt)] if first else []) + [(sA, sA.n)] +
                          ([x2_tick] if (first and j < 2) else []))
                         if kk == 0 else []),
                      i=(sM if kk == KCH - 1 else None))
            b.op("scalar", "activation", SZT[:, so:so + sw], PZ[par][:, 0:sw],
                 AF.Silu, w=[(sM, sM.n)] + ([(sS, sS.n)] if first else []),
                 i=sA)
        b.op("sync", "dma_start", out=szscr[j, :, :], in_=SZT[:],
             w=[(sA, sA.n)], i16=sS)
    b.op("sync", "dma_start", out=hpout[:, :], in_=HP[:],
         w=[(sE, scan_add_t)], i16=sS)
    if stop_after <= 4:
        return

    for tt in (0, 1024):
        for j in range(32):
            par = j % 2
            b.op("sync", "dma_start", out=YJT[par][:],
                 in_=yscr[j, :, tt:tt + 1024],
                 w=[(sE, sE.n)] + ([(sM, sM.n), (sS, sS.n)]
                                   if (tt == 0 and j == 0) else []),
                 i16=sL)
            b.op("sync", "dma_start", out=SJT[par][:],
                 in_=szscr[j, :, tt:tt + 1024], i16=sL)
            ldt = sL.n
            b.op("vector", "scalar_tensor_tensor", RH[:, j * 1024:(j + 1) * 1024],
                 YJT[par][:], NS[:, j:j + 1], SJT[par][:], ALU.mult, ALU.mult,
                 w=[(sL, ldt)], i=sE)
        rh_t = sE.n
        for m in range(KCH):
            par = m % 2
            wv = [(sM, sM.n)] if (tt, m) != (0, 0) else [(sE, rh_t)]
            b.op("gpsimd", "dma_start",
                 out=WB[par][:].rearrange("p (kk o) -> p kk o", o=128),
                 in_=Wm["opw"][0:4096, m * 128:(m + 1) * 128].rearrange(
                     "(kk p) o -> p kk o", p=128), w=wv, i16=sW)
            wt = sW.n
            for (so, sw) in ((0, 512), (512, 512)):
                gso = tt + so
                for kk in range(32):
                    b.raw("tensor",
                          (lambda e, _o=PZ[par][:, 0:sw],
                           _l=WB[par][:, kk * 128:(kk + 1) * 128],
                           _r=RH[:, kk * 1024 + so:kk * 1024 + so + sw],
                           _s=(kk == 0), _t=(kk == 31):
                           e.matmul(_o, _l, _r, start=_s, stop=_t)),
                          w=([(sW, wt), (sE, rh_t), (sE, sE.n)] if kk == 0
                             else []),
                          i=(sM if kk == 31 else None))
                mmt = sM.n
                b.op("sync", "dma_start", out=RESCH[:],
                     in_=resin[m * 128:(m + 1) * 128, HALO + gso:HALO + gso + sw],
                     w=[(sE, sE.n)], i16=sL)
                ldt = sL.n
                b.op("vector", "tensor_mul", ROCH[:, 0:sw], PZ[par][:, 0:sw],
                     RSTDBC[:, gso:gso + sw], w=[(sM, mmt), (sS, sS.n)])
                b.op("vector", "tensor_add", ROCH[:, 0:sw], ROCH[:, 0:sw],
                     RESCH[:, 0:sw], w=[(sL, ldt), (sE, sE.n)], i=sE)
                b.op("sync", "dma_start",
                     out=resout[m * 128:(m + 1) * 128, gso:gso + sw],
                     in_=ROCH[:, 0:sw], w=[(sE, sE.n)], i16=sS)
    b.op("sync", "nop", w=[(sS, sS.n)])


def build_mamba_stage(stop_after=99, nblocks=4, dbg=False):
    nc = bass.Bass("TRN2", target_bir_lowering=False, debug=False)
    resin = nc.dram_tensor("resin", [D, TOKH], F32, kind="ExternalInput").ap()
    resout = nc.dram_tensor("resout", [D, TOK], F32, kind="ExternalOutput").ap()
    Wm = {
        "ipw": nc.dram_tensor("ipw", [2048, 8384], F32, kind="ExternalInput").ap(),
        "opw": nc.dram_tensor("opw", [4096, 2048], F32, kind="ExternalInput").ap(),
        "cwq": nc.dram_tensor("cwq", [128, 4, 32], F32, kind="ExternalInput").ap(),
        "cbv": nc.dram_tensor("cbv", [128, 32], F32, kind="ExternalInput").ap(),
        "nsv": nc.dram_tensor("nsv", [128, 32], F32, kind="ExternalInput").ap(),
        "alog": nc.dram_tensor("alog", [64, 1], F32, kind="ExternalInput").ap(),
        "dtb": nc.dram_tensor("dtb", [64, 1], F32, kind="ExternalInput").ap(),
        "drow": nc.dram_tensor("drow", [1, 64], F32, kind="ExternalInput").ap(),
        "hpin": nc.dram_tensor("hpin", [64, 4096], F32, kind="ExternalInput").ap(),
    }
    hpout = nc.dram_tensor("hpout", [64, 4096], F32, kind="ExternalOutput").ap()
    kind = "ExternalOutput" if dbg else "Internal"
    xraw = nc.dram_tensor("xraw", [32, 128, TOKH], BF16, kind=kind).ap()
    yscr = nc.dram_tensor("yscr", [32, 128, TOK], BF16, kind=kind).ap()
    szscr = nc.dram_tensor("szscr", [32, 128, TOK], BF16, kind=kind).ap()
    if dbg:
        dbgx = nc.dram_tensor("dbgx", [D, TOKH], BF16, kind="ExternalOutput").ap()
        dbgv = nc.dram_tensor("dbgv", [64, 3 * TOKH], F32, kind="ExternalOutput").ap()
        dbgr = nc.dram_tensor("dbgr", [128, TOK], BF16, kind="ExternalOutput").ap()
        dbgf = nc.dram_tensor("dbgf", [128, 28672], BF16, kind="ExternalOutput").ap()
    else:
        dbgx = dbgv = dbgr = dbgf = None
    b = B(nc)
    sC = b.sem("C")
    C = Consts(b, sC)
    emit_mamba(b, C, "m", resin, resout, Wm, hpout, (xraw, yscr, szscr),
               stop_after=stop_after, nblocks=nblocks,
               dbg=(dbgv, dbgr, dbgx, dbgf))
    for s in b.sems_all:
        b.op("sync", "nop", w=[(s, s.n)])
    b.emit()
    return nc


def mamba_weight_maps(inputs):
    cw = np.asarray(inputs["L0_attn_conv_weight"], np.float32)
    cwq = np.ascontiguousarray(cw.reshape(4, 32, 128).transpose(2, 0, 1))
    cbv = np.ascontiguousarray(
        np.asarray(inputs["L0_attn_conv_bias"], np.float32).reshape(32, 128).T)
    nsv = np.ascontiguousarray(
        np.asarray(inputs["L0_attn_norm_scale"], np.float32).reshape(32, 128).T)
    return {
        "ipw": np.asarray(inputs["L0_attn_in_proj_w"], np.float32),
        "opw": np.asarray(inputs["L0_attn_out_proj_w"], np.float32),
        "cwq": cwq, "cbv": cbv, "nsv": nsv,
        "alog": np.asarray(inputs["L0_attn_A_log"], np.float32).reshape(64, 1),
        "dtb": np.asarray(inputs["L0_attn_dt_bias"], np.float32).reshape(64, 1),
        "drow": np.asarray(inputs["L0_attn_D_param"], np.float32).reshape(1, 64),
    }


GOLDEN_CENTER = 1.0 / np.e
GOLDEN_LOWER = 0.5 - np.log(4.0 / 3.0)
GOLDEN_UPPER = 0.5


def emit_moe(b, C, tag, resin, resout, Wm):
    nc = b.nc
    accd = Wm["accd"]
    sL = b.sem(f"{tag}_L")
    sW = b.sem(f"{tag}_W")
    sE = b.sem(f"{tag}_E")
    sA = b.sem(f"{tag}_A")
    sM = b.sem(f"{tag}_M")
    sS = b.sem(f"{tag}_S")
    sems = (sL, sE, sA, sM, sS)

    XT = nc.alloc_sbuf_tensor(f"{tag}_xt", [128, 16 * 1024], BF16)
    MID = nc.alloc_sbuf_tensor(f"{tag}_mid", [128, 16 * 1024], BF16)
    SG = nc.alloc_sbuf_tensor(f"{tag}_sg", [128, 512], BF16)
    WGB = [nc.alloc_sbuf_tensor(f"{tag}_wgb{i}", [128, 16 * 256], BF16) for i in range(2)]
    WUB = [nc.alloc_sbuf_tensor(f"{tag}_wub{i}", [128, 16 * 256], BF16) for i in range(2)]
    WDB = WUB
    GAB = nc.alloc_sbuf_tensor(f"{tag}_gab", [128, 32 * 64], BF16)
    UAB = nc.alloc_sbuf_tensor(f"{tag}_uab", [128, 32 * 64], BF16)
    DAB = nc.alloc_sbuf_tensor(f"{tag}_dab", [128, 32 * 64], BF16)
    GBB = nc.alloc_sbuf_tensor(f"{tag}_gbb", [64, 2 * 2048], BF16)
    UBB = nc.alloc_sbuf_tensor(f"{tag}_ubb", [64, 2 * 2048], BF16)
    DBB = nc.alloc_sbuf_tensor(f"{tag}_dbb", [64, 2 * 2048], BF16)
    T64 = nc.alloc_sbuf_tensor(f"{tag}_t64", [64, 1024], BF16)
    T64b = nc.alloc_sbuf_tensor(f"{tag}_t64b", [64, 1024], BF16)
    WRB = nc.alloc_sbuf_tensor(f"{tag}_wrb", [128, 16 * 8], BF16)
    LG = nc.alloc_sbuf_tensor(f"{tag}_lg", [8, 1024], F32)
    LGT = nc.alloc_sbuf_tensor(f"{tag}_lgt", [128, 64], F32)
    RT = [nc.alloc_sbuf_tensor(f"{tag}_rt{i}", [128, 64], F32) for i in range(4)]
    RM = nc.alloc_sbuf_tensor(f"{tag}_rm", [128, 64], mybir.dt.uint8)
    RRED = nc.alloc_sbuf_tensor(f"{tag}_rred", [128, 32], F32)
    INVT = nc.alloc_sbuf_tensor(f"{tag}_invt", [128, 1], F32)
    WEXP = nc.alloc_sbuf_tensor(f"{tag}_wexp", [8, 1024], BF16)
    WROW1 = nc.alloc_sbuf_tensor(f"{tag}_wrow1", [1, 1024], BF16)
    WBC = nc.alloc_sbuf_tensor(f"{tag}_wbc", [128, 1024], BF16)
    RESCH = nc.alloc_sbuf_tensor(f"{tag}_resch", [128, 1024], F32)
    ACCB = nc.alloc_sbuf_tensor(f"{tag}_accb", [128, 1024], F32)
    ROCH = [nc.alloc_sbuf_tensor(f"{tag}_roch{i}", [128, 512], F32)
            for i in range(2)]
    sDA = b.sem(f"{tag}_DA")
    l2 = LN2State(nc, tag)

    PG = nc.alloc_psum_tensor(f"{tag}_pg", [128, 512], F32)
    PU = nc.alloc_psum_tensor(f"{tag}_pu", [128, 512], F32)
    PD = [nc.alloc_psum_tensor(f"{tag}_pd{i}", [128, 512], F32) for i in range(2)]
    PR = nc.alloc_psum_tensor(f"{tag}_pr", [64, 512], F32)
    PL = nc.alloc_psum_tensor(f"{tag}_pl", [128, 1024], F32)
    b.op("gpsimd", "dma_start", out=WRB[:].rearrange("p (k e) -> p k e", e=8),
         in_=Wm["wr"].rearrange("(k p) e -> p k e", p=128), i16=sW)
    for (dst, key) in ((GAB, "ga"), (UAB, "ua"), (DAB, "da")):
        b.op("gpsimd", "dma_start",
             out=dst[:].rearrange("p (ek r) -> p ek r", r=64),
             in_=Wm[key].rearrange("e (k p) r -> p (e k) r", p=128), i16=sW)
    for (dst, key) in ((GBB, "gb"), (UBB, "ub"), (DBB, "db")):
        b.op("gpsimd", "dma_start",
             out=dst[:].rearrange("r (e ko) -> r e ko", e=2),
             in_=Wm[key].rearrange("e r ko -> r e ko"), i16=sW)
    b.op("gpsimd", "dma_start", out=INVT[:],
         in_=Wm["temp"][0:1, 0:1].broadcast_to((128, 1)), i16=sW)
    swt = sW.n
    b.op("vector", "reciprocal", INVT[:], INVT[:], w=[(sW, swt)], i=sE)
    invt_t = sE.n

    subs2 = ((0, 512), (512, 512))
    wexd = Wm["wexd"]
    roch_tick = {}
    prev_store = 0
    for tt in (0, 1024):
        sXT = b.sem(f"{tag}_sxt{tt}")
        emit_ln2(b, C, l2, sems, PG[:, :], PU[:, :], resin, tt, 1024,
                 subs2, XT, 1024, 0, sXT)
        xt_tick = (sXT, sXT.n)
        for si, (so, sw) in enumerate(subs2):
            for kk in range(KCH):
                b.raw("tensor",
                      (lambda e, _o=PL[0:8, 0:sw],
                       _l=WRB[:, kk * 8:(kk + 1) * 8],
                       _r=XT[:, kk * 1024 + so:kk * 1024 + so + sw],
                       _s=(kk == 0), _t=(kk == KCH - 1):
                       e.matmul(_o, _l, _r, start=_s, stop=_t)),
                      w=([(sW, swt), xt_tick, (sE, sE.n), (sA, sA.n)]
                         if kk == 0 else []),
                      i=(sM if kk == KCH - 1 else None))
            b.op("vector", "tensor_copy", LG[0:8, so:so + sw], PL[0:8, 0:sw],
                 w=[(sM, sM.n)], i=sE)
        lg_t = sE.n
        for tc8 in range(8):
            b.raw("tensor",
                  (lambda e, _o=PL[0:128, tc8 * 8:(tc8 + 1) * 8],
                   _i=LG[0:8, tc8 * 128:(tc8 + 1) * 128]:
                   e.transpose(_o, _i, C.id_f[0:8, 0:8])),
                  w=([(sE, lg_t), (sE, sE.n)] if tc8 == 0 else []),
                  i=(sM if tc8 == 7 else None))
        b.op("vector", "tensor_copy", LGT[:, 0:64], PL[0:128, 0:64],
             w=[(sM, sM.n)], i=sE)
        lgt_t = sE.n
        LGT3 = lambda t: t[:, 0:64].rearrange("p (t e) -> p t e", e=8)
        RRED3 = RRED[:, 0:32].rearrange("p (t r) -> p t r", r=4)
        NT8 = 8
        b.op("scalar", "activation", RT[0][:, 0:64], LGT[:, 0:64], AF.Sigmoid,
             scale=INVT[:, 0:1], w=[(sE, lgt_t), (sE, invt_t)], i=sA)
        b.op("scalar", "activation", RT[1][:, 0:64], RT[0][:, 0:64], AF.Abs,
             bias=C.scal[:, 7:8], i=sA)
        sg_t = sA.n
        b.op("vector", "tensor_scalar", RT[2][:, 0:64], RT[0][:, 0:64],
             float(GOLDEN_LOWER), None, ALU.is_ge, w=[(sA, sg_t)])
        b.op("vector", "tensor_scalar", RT[0][:, 0:64], RT[0][:, 0:64],
             float(GOLDEN_UPPER), None, ALU.is_le, i=sE)
        b.op("vector", "tensor_mul", RT[2][:, 0:64], RT[2][:, 0:64],
             RT[0][:, 0:64], w=[(sE, sE.n)], i=sE)
        zone_t = sE.n
        b.op("scalar", "activation", RT[3][:, 0:64], RT[1][:, 0:64], AF.Exp,
             scale=-10.0, w=[(sE, zone_t)])
        b.op("scalar", "activation", RT[1][:, 0:64], RT[1][:, 0:64], AF.Exp,
             scale=-1.0 / 0.3, i=sA)
        exp_t = sA.n
        b.op("vector", "tensor_mul", RT[3][:, 0:64], RT[3][:, 0:64],
             RT[2][:, 0:64], w=[(sA, exp_t)], i=sE)
        b.op("vector", "tensor_reduce", RRED3[:, :, 0:1], LGT3(RT[3]),
             mybir.AxisListType.X, ALU.add, w=[(sE, sE.n)], i=sE)
        b.op("vector", "reduce_max", RRED3[:, :, 1:2], LGT3(RT[1]),
             mybir.AxisListType.X, w=[(sE, sE.n)], i=sE)
        m1_t = sE.n
        b.op("vector", "tensor_tensor", LGT3(RT[2]), LGT3(RT[1]),
             RRED3[:, :, 1:2].broadcast_to((128, NT8, 8)), ALU.is_equal,
             w=[(sE, m1_t)], i=sE)
        b.op("vector", "tensor_scalar", RT[0][:, 0:64], RT[2][:, 0:64],
             -1.0, 1.0, ALU.mult, ALU.add, w=[(sE, sE.n)])
        b.op("vector", "tensor_mul", RT[0][:, 0:64], RT[0][:, 0:64],
             RT[1][:, 0:64], i=sE)
        b.op("vector", "reduce_max", RRED3[:, :, 2:3], LGT3(RT[0]),
             mybir.AxisListType.X, w=[(sE, sE.n)], i=sE)
        m2_t = sE.n
        b.op("vector", "tensor_tensor", LGT3(RT[0]), LGT3(RT[1]),
             RRED3[:, :, 2:3].broadcast_to((128, NT8, 8)), ALU.is_equal,
             w=[(sE, m2_t)], i=sE)
        b.op("vector", "tensor_max", RT[0][:, 0:64], RT[0][:, 0:64],
             RT[2][:, 0:64], w=[(sE, sE.n)])
        b.op("vector", "tensor_mul", RT[1][:, 0:64], RT[1][:, 0:64],
             RT[0][:, 0:64], i=sE)
        fbw_t = sE.n
        b.op("vector", "tensor_reduce", RRED3[:, :, 3:4], LGT3(RT[1]),
             mybir.AxisListType.X, ALU.add, w=[(sE, fbw_t)], i=sE)
        b.op("vector", "tensor_scalar_max", RRED3[:, :, 3:4],
             RRED3[:, :, 3:4], 1e-8, w=[(sE, sE.n)], i=sE)
        b.op("vector", "reciprocal", RRED3[:, :, 3:4], RRED3[:, :, 3:4],
             w=[(sE, sE.n)], i=sE)
        b.op("vector", "tensor_tensor", LGT3(RT[1]), LGT3(RT[1]),
             RRED3[:, :, 3:4].broadcast_to((128, NT8, 8)), ALU.mult,
             w=[(sE, sE.n)], i=sE)
        b.op("vector", "tensor_scalar",
             RM[:, 0:64].rearrange("p (t e) -> p t e", e=8),
             RRED3[:, :, 0:1].broadcast_to((128, NT8, 8)), 1e-8, None,
             ALU.is_lt, w=[(sE, sE.n)], i=sE)
        b.op("vector", "copy_predicated", RT[3][:, 0:64], RM[:, 0:64],
             RT[1][:, 0:64], w=[(sE, sE.n)], i=sE)
        wf_t = sE.n
        b.op("vector", "tensor_reduce", RRED3[:, :, 0:1], LGT3(RT[3]),
             mybir.AxisListType.X, ALU.add, w=[(sE, wf_t)], i=sE)
        b.op("vector", "tensor_scalar_max", RRED3[:, :, 0:1],
             RRED3[:, :, 0:1], 1e-8, w=[(sE, sE.n)], i=sE)
        b.op("vector", "reciprocal", RRED3[:, :, 0:1], RRED3[:, :, 0:1],
             w=[(sE, sE.n)], i=sE)
        b.op("vector", "tensor_tensor", LGT3(RT[3]), LGT3(RT[3]),
             RRED3[:, :, 0:1].broadcast_to((128, NT8, 8)), ALU.mult,
             w=[(sE, sE.n)], i=sE)
        b.op("vector", "tensor_scalar", RT[3][:, 0:64], RT[3][:, 0:64], 0.5,
             None, ALU.mult, w=[(sE, sE.n)], i=sE)
        wtok_t = sE.n
        for tc8 in range(8):
            b.raw("tensor",
                  (lambda e, _o=PL[0:8, tc8 * 128:(tc8 + 1) * 128],
                   _i=RT[3][:, tc8 * 8:(tc8 + 1) * 8]:
                   e.transpose(_o, _i, C.id_f[:])),
                  w=([(sE, wtok_t), (sE, sE.n)] if tc8 == 0 else []),
                  i=(sM if tc8 == 7 else None))
        b.op("vector", "tensor_copy", WEXP[0:8, 0:1024], PL[0:8, 0:1024],
             w=[(sM, sM.n)], i=sE)
        wex_t = sE.n
        b.op("gpsimd", "dma_start", out=wexd[:, tt:tt + 1024],
             in_=WEXP[0:8, 0:1024], w=[(sE, wex_t)], i16=sS)
        wrow_t = sS.n
        for e_i in range(8):
            lowrank = e_i < 2
            b.op("sync", "dma_start", out=WROW1[:, :],
                 in_=wexd[e_i:e_i + 1, tt:tt + 1024],
                 w=[(sS, wrow_t), (sM, sM.n)], i16=sL)
            wr1_t = sL.n
            for i2 in range(2):
                b.raw("tensor",
                      (lambda e, _o=PL[:, i2 * 512:(i2 + 1) * 512],
                       _r=WROW1[0:1, i2 * 512:(i2 + 1) * 512]:
                       e.matmul(_o, C.ones_bf[:], _r, start=True, stop=True)),
                      w=([(sL, wr1_t), (sE, sE.n)] if i2 == 0 else []),
                      i=(sM if i2 == 1 else None))
            b.op("vector", "tensor_copy", WBC[:], PL[:, 0:1024],
                 w=[(sM, sM.n)], i=sE)
            wbc_t = sE.n
            if lowrank:
                for (aw, dst) in ((GAB, T64), (UAB, T64b)):
                    for (so, sw) in subs2:
                        for kk in range(KCH):
                            b.raw("tensor",
                                  (lambda e, _o=PR[0:64, 0:sw],
                                   _l=aw[:, (e_i * 16 + kk) * 64:
                                         (e_i * 16 + kk + 1) * 64],
                                   _r=XT[:, kk * 1024 + so:
                                         kk * 1024 + so + sw],
                                   _s=(kk == 0), _t=(kk == KCH - 1):
                                   e.matmul(_o, _l, _r, start=_s, stop=_t)),
                                  w=([(sW, swt), (sE, sE.n)] if kk == 0 else []),
                                  i=(sM if kk == KCH - 1 else None))
                        b.op("vector", "tensor_copy", dst[0:64, so:so + sw],
                             PR[0:64, 0:sw], w=[(sM, sM.n)], i=sE)
                lr_t = sE.n
            for fb2 in range(8):
                par = fb2 % 2
                wv = [(sM, sM.n)] if not (tt == 0 and e_i == 0 and fb2 < 2) else []
                b.op("gpsimd", "dma_start",
                     out=WGB[par][:].rearrange("p (k o) -> p k o", o=256),
                     in_=Wm["wg"][e_i, :, fb2 * 256:(fb2 + 1) * 256].rearrange(
                         "(k p) o -> p k o", p=128), w=wv, i16=sW)
                b.op("gpsimd", "dma_start",
                     out=WUB[par][:].rearrange("p (k o) -> p k o", o=256),
                     in_=Wm["wu"][e_i, :, fb2 * 256:(fb2 + 1) * 256].rearrange(
                         "(k p) o -> p k o", p=128), i16=sW)
                wt = sW.n
                for fh in range(2):
                    fc = fb2 * 2 + fh
                    for (so, sw) in subs2:
                        for kk in range(KCH):
                            b.raw("tensor",
                                  (lambda e, _o=PG[:, 0:sw],
                                   _l=WGB[par][:, kk * 256 + fh * 128:
                                               kk * 256 + (fh + 1) * 128],
                                   _r=XT[:, kk * 1024 + so:
                                         kk * 1024 + so + sw],
                                   _s=(kk == 0),
                                   _t=(kk == KCH - 1 and not lowrank):
                                   e.matmul(_o, _l, _r, start=_s, stop=_t)),
                                  w=([(sW, wt), (sA, sA.n), (sE, sE.n)]
                                     if kk == 0 else []))
                        if lowrank:
                            b.raw("tensor",
                                  (lambda e, _o=PG[:, 0:sw],
                                   _l=GBB[0:64, e_i * 2048 + fc * 128:
                                          e_i * 2048 + (fc + 1) * 128],
                                   _r=T64[0:64, so:so + sw]:
                                   e.matmul(_o, _l, _r, start=False, stop=True)),
                                  w=[(sE, lr_t)])
                        for kk in range(KCH):
                            b.raw("tensor",
                                  (lambda e, _o=PU[:, 0:sw],
                                   _l=WUB[par][:, kk * 256 + fh * 128:
                                               kk * 256 + (fh + 1) * 128],
                                   _r=XT[:, kk * 1024 + so:
                                         kk * 1024 + so + sw],
                                   _s=(kk == 0),
                                   _t=(kk == KCH - 1 and not lowrank):
                                   e.matmul(_o, _l, _r, start=_s, stop=_t)),
                                  i=(sM if (kk == KCH - 1 and not lowrank)
                                     else None))
                        if lowrank:
                            b.raw("tensor",
                                  (lambda e, _o=PU[:, 0:sw],
                                   _l=UBB[0:64, e_i * 2048 + fc * 128:
                                          e_i * 2048 + (fc + 1) * 128],
                                   _r=T64b[0:64, so:so + sw]:
                                   e.matmul(_o, _l, _r, start=False, stop=True)),
                                  i=sM)
                        mmt = sM.n
                        b.op("scalar", "activation", SG[:, 0:sw], PG[:, 0:sw],
                             AF.Silu, w=[(sM, mmt)], i=sA)
                        b.op("vector", "tensor_mul",
                             MID[:, fc * 1024 + so:fc * 1024 + so + sw],
                             SG[:, 0:sw], PU[:, 0:sw], w=[(sA, sA.n)], i=sE)
                        b.op("vector", "tensor_mul",
                             MID[:, fc * 1024 + so:fc * 1024 + so + sw],
                             MID[:, fc * 1024 + so:fc * 1024 + so + sw],
                             WBC[:, so:so + sw],
                             w=[(sE, wbc_t), (sE, sE.n)], i=sE)
            mid_t = sE.n
            if lowrank:
                for (so, sw) in subs2:
                    for kk in range(KCH):
                        b.raw("tensor",
                              (lambda e, _o=PR[0:64, 0:sw],
                               _l=DAB[:, (e_i * 16 + kk) * 64:
                                      (e_i * 16 + kk + 1) * 64],
                               _r=MID[:, kk * 1024 + so:kk * 1024 + so + sw],
                               _s=(kk == 0), _t=(kk == KCH - 1):
                               e.matmul(_o, _l, _r, start=_s, stop=_t)),
                              w=([(sE, mid_t), (sE, sE.n)] if kk == 0 else []),
                              i=(sM if kk == KCH - 1 else None))
                    b.op("vector", "tensor_copy", T64[0:64, so:so + sw],
                         PR[0:64, 0:sw], w=[(sM, sM.n)], i=sE)
                lrd_t = sE.n
            for mb2 in range(8):
                par = mb2 % 2
                wv = [(sM, sM.n)] if not (tt == 0 and e_i == 0 and mb2 < 2) else []
                b.op("gpsimd", "dma_start",
                     out=WDB[par][:].rearrange("p (k o) -> p k o", o=256),
                     in_=Wm["wd"][e_i, :, mb2 * 256:(mb2 + 1) * 256].rearrange(
                         "(k p) o -> p k o", p=128), w=wv, i16=sW)
                wt = sW.n
                for mh in range(2):
                    m = mb2 * 2 + mh
                    for (so, sw) in subs2:
                        pd = PD[(m * 2 + (so // 512)) % 2]
                        for kk in range(KCH):
                            b.raw("tensor",
                                  (lambda e, _o=pd[:, 0:sw],
                                   _l=WDB[par][:, kk * 256 + mh * 128:
                                               kk * 256 + (mh + 1) * 128],
                                   _r=MID[:, kk * 1024 + so:kk * 1024 + so + sw],
                                   _s=(kk == 0),
                                   _t=(kk == KCH - 1 and not lowrank):
                                   e.matmul(_o, _l, _r, start=_s, stop=_t)),
                                  w=([(sW, wt), (sE, mid_t), (sE, sE.n)]
                                     if kk == 0 else []),
                                  i=(sM if (kk == KCH - 1 and not lowrank)
                                     else None))
                        if lowrank:
                            b.raw("tensor",
                                  (lambda e, _o=pd[:, 0:sw],
                                   _l=DBB[0:64, e_i * 2048 + m * 128:
                                          e_i * 2048 + (m + 1) * 128],
                                   _r=T64[0:64, so:so + sw]:
                                   e.matmul(_o, _l, _r, start=False, stop=True)),
                                  w=[(sE, lrd_t)], i=sM)
                        mmt = sM.n
                        rpar = (m * 2 + so // 512) % 2
                        b.op("vector", "tensor_copy", ROCH[rpar][:, 0:sw],
                             pd[:, 0:sw],
                             w=[(sM, mmt), (sDA, roch_tick.get(rpar, 0))], i=sE)
                        b.op("gpsimd", "dma_start",
                             out=accd[m * 128:(m + 1) * 128, so:so + sw],
                             in_=ROCH[rpar][:, 0:sw],
                             accum_op=(ALU.bypass if e_i == 0 else ALU.add),
                             w=[(sE, sE.n)], i16=sDA)
                        roch_tick[rpar] = sDA.n
        for m in range(KCH):
            b.op("sync", "dma_start", out=RESCH[:],
                 in_=resin[m * 128:(m + 1) * 128, tt:tt + 1024],
                 w=([(sDA, sDA.n)] if m == 0 else []) + [(sE, sE.n)], i16=sL)
            b.op("sync", "dma_start", out=ACCB[:],
                 in_=accd[m * 128:(m + 1) * 128, 0:1024], i16=sL)
            ldt = sL.n
            b.op("vector", "tensor_add", ACCB[:], ACCB[:],
                 RESCH[:], w=[(sL, ldt), (sS, sS.n)], i=sE)
            b.op("sync", "dma_start", out=resout[m * 128:(m + 1) * 128,
                                                 tt:tt + 1024],
                 in_=ACCB[:], w=[(sE, sE.n)], i16=sS)
        prev_store = sS.n
    b.op("sync", "nop", w=[(sS, sS.n)])



def build_moe_stage():
    nc = bass.Bass("TRN2", target_bir_lowering=False, debug=False)
    resin = nc.dram_tensor("resin", [D, TOK], F32, kind="ExternalInput").ap()
    resout = nc.dram_tensor("resout", [D, TOK], F32, kind="ExternalOutput").ap()
    Wm = {
        "wg": nc.dram_tensor("wg", [8, 2048, 2048], F32, kind="ExternalInput").ap(),
        "wu": nc.dram_tensor("wu", [8, 2048, 2048], F32, kind="ExternalInput").ap(),
        "wd": nc.dram_tensor("wd", [8, 2048, 2048], F32, kind="ExternalInput").ap(),
        "ga": nc.dram_tensor("ga", [2, 2048, 64], F32, kind="ExternalInput").ap(),
        "gb": nc.dram_tensor("gb", [2, 64, 2048], F32, kind="ExternalInput").ap(),
        "ua": nc.dram_tensor("ua", [2, 2048, 64], F32, kind="ExternalInput").ap(),
        "ub": nc.dram_tensor("ub", [2, 64, 2048], F32, kind="ExternalInput").ap(),
        "da": nc.dram_tensor("da", [2, 2048, 64], F32, kind="ExternalInput").ap(),
        "db": nc.dram_tensor("db", [2, 64, 2048], F32, kind="ExternalInput").ap(),
        "wr": nc.dram_tensor("wr", [2048, 8], F32, kind="ExternalInput").ap(),
        "temp": nc.dram_tensor("temp", [1, 1], F32, kind="ExternalInput").ap(),
        "wexd": nc.dram_tensor("wexd", [8, 2048], BF16, kind="Internal").ap(),
        "accd": nc.dram_tensor("accd", [2048, 1024], F32, kind="Internal").ap(),
    }
    b = B(nc)
    sC = b.sem("C")
    C = Consts(b, sC)
    emit_moe(b, C, "o", resin, resout, Wm)
    b.emit()
    return nc


def moe_weight_maps(inputs):
    return {
        "wg": np.asarray(inputs["L1_ffn_Wg"], np.float32),
        "wu": np.asarray(inputs["L1_ffn_Wu"], np.float32),
        "wd": np.asarray(inputs["L1_ffn_Wd"], np.float32),
        "ga": np.asarray(inputs["L1_ffn_Ga"], np.float32),
        "gb": np.asarray(inputs["L1_ffn_Gb"], np.float32),
        "ua": np.asarray(inputs["L1_ffn_Ua"], np.float32),
        "ub": np.asarray(inputs["L1_ffn_Ub"], np.float32),
        "da": np.asarray(inputs["L1_ffn_Da"], np.float32),
        "db": np.asarray(inputs["L1_ffn_Db"], np.float32),
        "wr": np.asarray(inputs["L1_ffn_Wr"], np.float32),
        "temp": np.asarray(inputs["L1_ffn_temperature"], np.float32).reshape(1, 1),
    }


def emit_qkv(b, C, tag, resin, qf, kf, vt, Wm):
    nc = b.nc
    sL = b.sem(f"{tag}_L")
    sW = b.sem(f"{tag}_W")
    sE = b.sem(f"{tag}_E")
    sA = b.sem(f"{tag}_A")
    sM = b.sem(f"{tag}_M")
    sS = b.sem(f"{tag}_S")
    sems = (sL, sE, sA, sM, sS)

    XT = nc.alloc_sbuf_tensor(f"{tag}_xt", [128, 16 * 2048], BF16)
    WT = nc.alloc_sbuf_tensor(f"{tag}_wt", [128, 16 * 2048], BF16)
    WS = [nc.alloc_sbuf_tensor(f"{tag}_ws{i}", [128, 2048], F32) for i in range(2)]
    OB = nc.alloc_sbuf_tensor(f"{tag}_ob", [128, 2048], F32)
    OBV = nc.alloc_sbuf_tensor(f"{tag}_obv", [128, 2048], BF16)
    l2 = LN2State(nc, tag)
    PZ = [nc.alloc_psum_tensor(f"{tag}_pz{i}", [128, 512], F32) for i in range(2)]
    PT = nc.alloc_psum_tensor(f"{tag}_pt", [128, 512], F32)

    sXT = b.sem(f"{tag}_sxt")
    emit_ln2(b, C, l2, sems, PZ[0][:, :], PZ[1][:, :], resin, HALO, TOK, SUBS_R,
             XT, TOK, 0, sXT)
    xt_tick = (sXT, sXT.n)

    for wi, (wkey, outap, tmaj) in enumerate(
            (("wq", qf, False), ("wk", kf, False), ("wv", vt, True))):
        for m in range(KCH):
            par = m % 2
            b.op("sync", "dma_start", out=WS[par][:],
                 in_=Wm[wkey][m * 128:(m + 1) * 128, :],
                 w=[(sM, sM.n)], i16=sL)
            wt_ld = sL.n
            for g in range(4):
                for jj in range(4):
                    k = g * 4 + jj
                    b.raw("tensor",
                          (lambda e, _o=PT[:, jj * 128:(jj + 1) * 128],
                           _i=WS[par][:, k * 128:(k + 1) * 128]:
                           e.transpose(_o, _i, C.id_f[:])),
                          w=([(sL, wt_ld), (sE, sE.n)] if jj == 0 else []),
                          i=(sM if jj == 3 else None))
                b.op("vector", "tensor_copy",
                     WT[:, (g * 4) * 2048 // 16 + m * 128:0] if False else
                     WT[:].rearrange("p (k o) -> p k o", o=128)[
                         :, g * 4:g * 4 + 4, m * 128:(m + 1) * 128][:, :, 0:128]
                     if False else
                     WT[:].rearrange("p (kk o) -> p kk o", o=2048)[
                         :, g * 4:g * 4 + 4, m * 128:(m + 1) * 128],
                     PT[:, 0:512].rearrange("p (j o) -> p j o", o=128),
                     w=[(sM, sM.n)], i=sE)
        wt_t = sE.n
        if not tmaj:
            for m in range(KCH):
                for (so, sw) in SUBS_R:
                    for kk in range(KCH):
                        b.raw("tensor",
                              (lambda e, _o=PZ[m % 2][:, 0:sw],
                               _l=WT[:, kk * 2048 + m * 128:
                                      kk * 2048 + (m + 1) * 128],
                               _r=XT[:, kk * TOK + so:kk * TOK + so + sw],
                               _s=(kk == 0), _t=(kk == KCH - 1):
                               e.matmul(_o, _l, _r, start=_s, stop=_t)),
                              w=([(sE, wt_t), xt_tick, (sE, sE.n)]
                                 if kk == 0 else []),
                              i=(sM if kk == KCH - 1 else None))
                    b.op("vector", "tensor_copy", OB[:, so:so + sw],
                         PZ[m % 2][:, 0:sw],
                         w=[(sM, sM.n)] + ([(sS, sS.n)] if so == 0 else []),
                         i=sE)
                b.op("sync", "dma_start",
                     out=outap[m * 128:(m + 1) * 128, :], in_=OB[:],
                     w=[(sE, sE.n)], i16=sS)
        else:
            for m in range(KCH):
                for (so, sw) in SUBS_R:
                    for kk in range(KCH):
                        b.raw("tensor",
                              (lambda e, _o=PZ[m % 2][:, 0:sw],
                               _l=XT[:, kk * TOK + m * 128:
                                     kk * TOK + (m + 1) * 128],
                               _r=WT[:, kk * 2048 + so:kk * 2048 + so + sw],
                               _s=(kk == 0), _t=(kk == KCH - 1):
                               e.matmul(_o, _l, _r, start=_s, stop=_t)),
                              w=([(sE, wt_t), xt_tick, (sE, sE.n)]
                                 if kk == 0 else []),
                              i=(sM if kk == KCH - 1 else None))
                    b.op("vector", "tensor_copy", OBV[:, so:so + sw],
                         PZ[m % 2][:, 0:sw],
                         w=[(sM, sM.n)] + ([(sS, sS.n)] if so == 0 else []),
                         i=sE)
                b.op("sync", "dma_start",
                     out=outap[m * 128:(m + 1) * 128, :], in_=OBV[:],
                     w=[(sE, sE.n)], i16=sS)
    b.op("sync", "nop", w=[(sS, sS.n)])


def emit_attn(b, C, tag, resin, resout, qf, kfull, vfull, Wm):
    nc = b.nc
    sL = b.sem(f"{tag}_L")
    sW = b.sem(f"{tag}_W")
    sE = b.sem(f"{tag}_E")
    sA = b.sem(f"{tag}_A")
    sM = b.sem(f"{tag}_M")
    sS = b.sem(f"{tag}_S")

    OSB = nc.alloc_sbuf_tensor(f"{tag}_osb", [128, 16 * 2048], BF16)
    EB = nc.alloc_sbuf_tensor(f"{tag}_eb", [128, 32 * 512], BF16)
    KH = [nc.alloc_sbuf_tensor(f"{tag}_kh{i}", [128, 4096], F32) for i in range(2)]
    VH = [nc.alloc_sbuf_tensor(f"{tag}_vh{i}", [128, 4096], BF16) for i in range(2)]
    QH = [nc.alloc_sbuf_tensor(f"{tag}_qh{i}", [128, 2048], F32) for i in range(2)]
    DEN = nc.alloc_sbuf_tensor(f"{tag}_den", [1, 2048], F32)
    DENB = nc.alloc_sbuf_tensor(f"{tag}_denb", [1, 2048], BF16)
    DBC = nc.alloc_sbuf_tensor(f"{tag}_dbc", [128, 512], BF16)
    WOS = [nc.alloc_sbuf_tensor(f"{tag}_wos{i}", [128, 2048], F32) for i in range(2)]
    WOT = [nc.alloc_sbuf_tensor(f"{tag}_wot{i}", [128, 2048], BF16) for i in range(2)]
    RESCH = nc.alloc_sbuf_tensor(f"{tag}_resch", [128, 512], F32)
    ROCH = nc.alloc_sbuf_tensor(f"{tag}_roch", [128, 512], F32)
    PST = [nc.alloc_psum_tensor(f"{tag}_pst{i}", [128, 512], F32) for i in range(2)]
    POV = [nc.alloc_psum_tensor(f"{tag}_pov{i}", [128, 512], F32) for i in range(2)]
    PDN = nc.alloc_psum_tensor(f"{tag}_pdn", [128, 512], F32)
    PTT = nc.alloc_psum_tensor(f"{tag}_ptt", [128, 512], F32)

    kh_tick = {}
    vh_tick = {}
    qh_tick = {}
    for h in range(16):
        par = h % 2
        b.op("sync", "dma_start", out=KH[par][:, 0:2048],
             in_=kfull[h * 128:(h + 1) * 128, :],
             w=[(sM, kh_tick.get(h - 2, 0))], i16=sL)
        b.op("sync", "dma_start", out=KH[par][:, 2048:4096],
             in_=kfull[2048 + h * 128:2048 + (h + 1) * 128, :], i16=sL)
        b.op("sync", "dma_start", out=QH[par][:],
             in_=qf[h * 128:(h + 1) * 128, :], i16=sL)
        b.op("gpsimd", "dma_start",
             out=VH[par][:].rearrange("p (c o) -> p c o", o=128),
             in_=vfull[:, h * 128:(h + 1) * 128].rearrange(
                 "(c p) o -> p c o", p=128),
             w=[(sM, vh_tick.get(h - 2, 0))], i16=sL)
        ld_t = sL.n
        for qt in range(4):
            for kt in range(32):
                b.raw("tensor",
                      (lambda e, _o=PST[(qt * 32 + kt) % 2][:, 0:512],
                       _l=KH[par][:, kt * 128:(kt + 1) * 128],
                       _r=QH[par][:, qt * 512:(qt + 1) * 512]:
                       e.matmul(_o, _l, _r, start=True, stop=True)),
                      w=([(sL, ld_t), (sA, sA.n), (sE, sE.n)]
                         if (qt == 0 and kt == 0) else
                         ([(sA, sA.n)] if kt % 8 == 0 else [])),
                      i=(sM if kt % 8 == 7 else None))
                if kt % 8 == 7:
                    for kt2 in range(kt - 7, kt + 1):
                        b.op("scalar", "activation",
                             EB[:, kt2 * 512:(kt2 + 1) * 512],
                             PST[(qt * 32 + kt2) % 2][:, 0:512], AF.Exp,
                             scale=float(1.0 / np.sqrt(128.0)),
                             bias=C.scal[:, 2:3],
                             w=([(sM, sM.n)] if kt2 == kt - 7 else []),
                             i=(sA if kt2 == kt else None))
            ex_t = sA.n
            for kt in range(32):
                b.raw("tensor",
                      (lambda e, _o=POV[qt % 2][:, 0:512],
                       _l=VH[par][:, kt * 128:(kt + 1) * 128],
                       _r=EB[:, kt * 512:(kt + 1) * 512],
                       _s=(kt == 0), _t=(kt == 31):
                       e.matmul(_o, _l, _r, start=_s, stop=_t)),
                      w=([(sA, ex_t), (sE, sE.n)] if kt == 0 else []))
            for kt in range(32):
                b.raw("tensor",
                      (lambda e, _o=PDN[0:1, 0:512],
                       _r=EB[:, kt * 512:(kt + 1) * 512],
                       _s=(kt == 0), _t=(kt == 31):
                       e.matmul(_o, C.ones_bf128[:], _r, start=_s, stop=_t)),
                      i=(sM if kt == 31 else None))
            ov_t = sM.n
            b.op("vector", "tensor_copy", DEN[0:1, qt * 512:(qt + 1) * 512],
                 PDN[0:1, 0:512], w=[(sM, ov_t)], i=sE)
            b.op("vector", "reciprocal", DEN[0:1, qt * 512:(qt + 1) * 512],
                 DEN[0:1, qt * 512:(qt + 1) * 512], w=[(sE, sE.n)], i=sE)
            b.op("vector", "tensor_copy", DENB[0:1, qt * 512:(qt + 1) * 512],
                 DEN[0:1, qt * 512:(qt + 1) * 512], w=[(sE, sE.n)], i=sE)
            dn_t = sE.n
            b.raw("tensor",
                  (lambda e, _o=PDN[0:128, 0:512],
                   _r=DENB[0:1, qt * 512:(qt + 1) * 512]:
                   e.matmul(_o, C.ones_bf[:], _r, start=True, stop=True)),
                  w=[(sE, dn_t)], i=sM)
            b.op("vector", "tensor_copy", DBC[:], PDN[0:128, 0:512],
                 w=[(sM, sM.n)], i=sE)
            b.op("vector", "tensor_mul",
                 OSB[:, h * 2048 + qt * 512:h * 2048 + (qt + 1) * 512],
                 POV[qt % 2][:, 0:512], DBC[:], w=[(sE, sE.n)], i=sE)
        kh_tick[h] = sM.n
        vh_tick[h] = sM.n
        qh_tick[h] = sM.n
    osb_t = sE.n
    for m in range(KCH):
        par = m % 2
        b.op("sync", "dma_start", out=WOS[par][:],
             in_=Wm["wo"][m * 128:(m + 1) * 128, :],
             w=[(sM, sM.n)], i16=sL)
        wo_ld = sL.n
        for g in range(4):
            for jj in range(4):
                k = g * 4 + jj
                b.raw("tensor",
                      (lambda e, _o=PTT[:, jj * 128:(jj + 1) * 128],
                       _i=WOS[par][:, k * 128:(k + 1) * 128]:
                       e.transpose(_o, _i, C.id_f[:])),
                      w=([(sL, wo_ld), (sE, sE.n)] if jj == 0 else []),
                      i=(sM if jj == 3 else None))
            b.op("vector", "tensor_copy",
                 WOT[par][:].rearrange("p (kk o) -> p kk o", o=128)[
                     :, g * 4:g * 4 + 4, :],
                 PTT[:, 0:512].rearrange("p (j o) -> p j o", o=128),
                 w=[(sM, sM.n)], i=sE)
        wo_t = sE.n
        for (so, sw) in SUBS_R:
            for hk in range(16):
                b.raw("tensor",
                      (lambda e, _o=PST[m % 2][:, 0:sw],
                       _l=WOT[par][:, hk * 128:(hk + 1) * 128],
                       _r=OSB[:, hk * 2048 + so:hk * 2048 + so + sw],
                       _s=(hk == 0), _t=(hk == 15):
                       e.matmul(_o, _l, _r, start=_s, stop=_t)),
                      w=([(sE, wo_t), (sE, osb_t), (sE, sE.n)]
                         if hk == 0 else []),
                      i=(sM if hk == 15 else None))
            mmt = sM.n
            b.op("sync", "dma_start", out=RESCH[:],
                 in_=resin[m * 128:(m + 1) * 128, HALO + so:HALO + so + sw],
                 w=[(sE, sE.n)], i16=sL)
            ldt = sL.n
            b.op("vector", "tensor_add", ROCH[:, 0:sw], PST[m % 2][:, 0:sw],
                 RESCH[:, 0:sw], w=[(sM, mmt), (sL, ldt), (sS, sS.n)], i=sE)
            b.op("sync", "dma_start",
                 out=resout[m * 128:(m + 1) * 128, so:so + sw],
                 in_=ROCH[:, 0:sw], w=[(sE, sE.n)], i16=sS)
    b.op("sync", "nop", w=[(sS, sS.n)])


def build_qkv_stage():
    nc = bass.Bass("TRN2", target_bir_lowering=False, debug=False)
    resin = nc.dram_tensor("resin", [D, TOKH], F32, kind="ExternalInput").ap()
    qf = nc.dram_tensor("qf", [D, TOK], F32, kind="ExternalOutput").ap()
    kf = nc.dram_tensor("kf", [D, TOK], F32, kind="ExternalOutput").ap()
    vt = nc.dram_tensor("vt", [TOK, D], BF16, kind="ExternalOutput").ap()
    Wm = {
        "wq": nc.dram_tensor("wq", [D, D], F32, kind="ExternalInput").ap(),
        "wk": nc.dram_tensor("wk", [D, D], F32, kind="ExternalInput").ap(),
        "wv": nc.dram_tensor("wv", [D, D], F32, kind="ExternalInput").ap(),
    }
    b = B(nc)
    sC = b.sem("C")
    C = Consts(b, sC)
    emit_qkv(b, C, "q", resin, qf, kf, vt, Wm)
    for s in b.sems_all:
        b.op("sync", "nop", w=[(s, s.n)])
    b.emit()
    return nc


def build_attn_stage():
    nc = bass.Bass("TRN2", target_bir_lowering=False, debug=False)
    resin = nc.dram_tensor("resin", [D, TOKH], F32, kind="ExternalInput").ap()
    resout = nc.dram_tensor("resout", [D, TOK], F32, kind="ExternalOutput").ap()
    qf = nc.dram_tensor("qf", [D, TOK], F32, kind="ExternalInput").ap()
    kfull = nc.dram_tensor("kfull", [2 * D, TOK], F32, kind="ExternalInput").ap()
    vfull = nc.dram_tensor("vfull", [2 * TOK, D], BF16, kind="ExternalInput").ap()
    Wm = {"wo": nc.dram_tensor("wo", [D, D], F32, kind="ExternalInput").ap()}
    b = B(nc)
    sC = b.sem("C")
    C = Consts(b, sC)
    emit_attn(b, C, "a", resin, resout, qf, kfull, vfull, Wm)
    for s in b.sems_all:
        b.op("sync", "nop", w=[(s, s.n)])
    b.emit()
    return nc


def _gather_res(per_core, ncol=TOK, off=0):
    out = np.zeros((4, 4096, D), np.float32)
    for c in range(NCORE):
        bb, h = c // 2, c % 2
        out[bb, h * TOK:(h + 1) * TOK, :] = per_core[c][:, off:off + ncol].T
    return out


def kernel(**inputs):
    consts = mk_consts()
    cores = list(range(NCORE))

    wgt, wut, wdt = run_ghostw(inputs, consts)

    res1s = _shard_x0(np.asarray(inputs["L0_attn_u"], np.float32))
    nc1 = build_ffn0_stage(TOKH)
    r1 = run_bass_kernel_spmd(
        nc1, [dict(consts, resin=res1s[c], wgt=wgt, wut=wut, wdt=wdt)
              for c in cores], core_ids=cores)
    res1 = [np.asarray(r1.results[c]["resout"], np.float32) for c in cores]

    Wmm = mamba_weight_maps(inputs)
    ncm = build_mamba_stage()
    zeros_hp = np.zeros((64, 4096), np.float32)
    r2a = run_bass_kernel_spmd(
        ncm, [dict(consts, resin=res1[c], hpin=zeros_hp, **Wmm)
              for c in cores], core_ids=cores)
    maps2 = []
    for c in cores:
        hp = zeros_hp if c % 2 == 0 else np.asarray(
            r2a.results[c - 1]["hpout"], np.float32)
        maps2.append(dict(consts, resin=res1[c], hpin=hp, **Wmm))
    r2 = run_bass_kernel_spmd(ncm, maps2, core_ids=cores)
    res2 = [np.asarray(r2.results[c]["resout"], np.float32) for c in cores]
    res2s = _shard_x0(_gather_res(res2))

    r3 = run_bass_kernel_spmd(
        nc1, [dict(consts, resin=res2s[c], wgt=wgt, wut=wut, wdt=wdt)
              for c in cores], core_ids=cores)
    res3 = [np.asarray(r3.results[c]["resout"], np.float32) for c in cores]
    res3s = [np.ascontiguousarray(r[:, HALO:]) for r in res3]

    Wmo = moe_weight_maps(inputs)
    nco = build_moe_stage()
    r4 = run_bass_kernel_spmd(
        nco, [dict(consts, resin=res3s[c], **Wmo) for c in cores],
        core_ids=cores)
    res4 = [np.asarray(r4.results[c]["resout"], np.float32) for c in cores]
    res4s = _shard_x0(_gather_res(res4))

    ncq = build_qkv_stage()
    rq = run_bass_kernel_spmd(
        ncq, [dict(consts, resin=res4s[c],
                   wq=np.asarray(inputs["L1_attn_Wq"], np.float32),
                   wk=np.asarray(inputs["L1_attn_Wk"], np.float32),
                   wv=np.asarray(inputs["L1_attn_Wv"], np.float32))
              for c in cores], core_ids=cores)
    nca = build_attn_stage()
    mapsa = []
    for c in cores:
        p = (c // 2) * 2
        kfull = np.concatenate([rq.results[p]["kf"], rq.results[p + 1]["kf"]], 0)
        vfull = np.concatenate([rq.results[p]["vt"], rq.results[p + 1]["vt"]], 0)
        mapsa.append(dict(consts, resin=res4s[c], qf=rq.results[c]["qf"],
                          kfull=kfull, vfull=vfull,
                          wo=np.asarray(inputs["L1_attn_Wo"], np.float32)))
    r5 = run_bass_kernel_spmd(nca, mapsa, core_ids=cores)
    res5 = [np.asarray(r5.results[c]["resout"], np.float32) for c in cores]

    r6 = run_bass_kernel_spmd(
        nco, [dict(consts, resin=res5[c], **Wmo) for c in cores],
        core_ids=cores)
    res6 = [np.asarray(r6.results[c]["resout"], np.float32) for c in cores]

    return _gather_res(res6)


# version 04 (final)
# speedup vs baseline: 1.0869x; 1.0869x over previous
import numpy as np
import ml_dtypes
import concourse.bass as bass
import concourse.mybir as mybir
from concourse.bass_utils import run_bass_kernel_spmd

F32 = mybir.dt.float32
BF16 = mybir.dt.bfloat16
I32 = mybir.dt.int32
AF = mybir.ActivationFunctionType
ALU = mybir.AluOpType
BF16NP = ml_dtypes.bfloat16

NCORE = 8
D = 2048
TOK = 2048
HALO = 3
TOKH = TOK + HALO
KCH = D // 128
INTER = 5504
NOT = INTER // 128
DIN = 4096
EPS_LN = 1e-5


def tok_tiles(T):
    if T == TOKH:
        return [(0, 515, [(0, 256), (256, 259)]), (515, 512, [(0, 512)]),
                (1027, 512, [(0, 512)]), (1539, 512, [(0, 512)])]
    assert T % 512 == 0
    return [(o, 512, [(0, 512)]) for o in range(0, T, 512)]


ENGS = ("tensor", "vector", "scalar", "gpsimd", "sync")


class Sem:
    __slots__ = ("h", "n", "name")


class B:

    def __init__(self, nc):
        self.nc = nc
        self.q = {e: [] for e in ENGS}
        self.sems_all = []

    def sem(self, name):
        s = Sem()
        s.h = self.nc.alloc_semaphore(name)
        s.n = 0
        s.name = name
        self.sems_all.append(s)
        return s

    def _incs(self, i, i16):
        incs = []
        if i is not None:
            i.n += 1
            incs.append((i, 1))
        if i16 is not None:
            i16.n += 16
            incs.append((i16, 16))
        return incs

    def _waits(self, w):
        wl = []
        for (s, v) in w:
            v = int(v)
            assert v <= s.n, f"wait {s.name}>={v} but schedule max {s.n}"
            wl.append((s, v))
        return tuple(wl)

    def op(self, eng, meth, *args, w=(), i=None, i16=None, **kw):
        self.q[eng].append((meth, args, kw, self._waits(w), self._incs(i, i16)))

    def raw(self, eng, fn, w=(), i=None, i16=None):
        self.q[eng].append((fn, None, None, self._waits(w), self._incs(i, i16)))

    @staticmethod
    def _runner(ops):
        def run(e):
            for (meth, args, kw, waits, incs) in ops:
                for (s, v) in waits:
                    e.wait_ge(s.h, v)
                if args is None:
                    ins = meth(e)
                else:
                    ins = getattr(e, meth)(*args, **kw)
                for (s, n) in incs:
                    ins.then_inc(s.h, n)
        return run

    def emit(self):
        with self.nc.Block() as blk:
            for eng in ENGS:
                if self.q[eng]:
                    getattr(blk, eng)(self._runner(self.q[eng]))


def mk_consts():
    return {
        "c_ones_f": np.ones((128, 1), np.float32),
        "c_ones_bf": np.ones((1, 128), BF16NP),
        "c_id_bf": np.eye(128, dtype=BF16NP),
        "c_id_f": np.eye(128, dtype=np.float32),
        "c_triu_bf": np.triu(np.ones((64, 64))).astype(BF16NP),
        "c_onescol_bf": np.ones((128, 1), BF16NP),
        "c_scal": np.tile(np.array([[1e-5, 1e-6, -20.0, 1.0, 0.5, 0.0, -200.0, -0.36787944117144233]],
                                   np.float32), (128, 1)),
    }


class Consts:

    def __init__(self, b, sL):
        nc = b.nc
        self.sem = sL
        self.ones_f = nc.alloc_sbuf_tensor("k_ones_f", [128, 1], F32)
        self.ones_bf = nc.alloc_sbuf_tensor("k_ones_bf", [1, 128], BF16)
        self.id_bf = nc.alloc_sbuf_tensor("k_id_bf", [128, 128], BF16)
        self.id_f = nc.alloc_sbuf_tensor("k_id_f", [128, 128], F32)
        self.triu_bf = nc.alloc_sbuf_tensor("k_triu", [64, 64], BF16)
        self.scal = nc.alloc_sbuf_tensor("k_scal", [128, 8], F32)
        self.ones_bf128 = nc.alloc_sbuf_tensor("k_onescol", [128, 1], BF16)
        d = {}
        d["ones_f"] = nc.dram_tensor("c_ones_f", [128, 1], F32, kind="ExternalInput").ap()
        d["ones_bf"] = nc.dram_tensor("c_ones_bf", [1, 128], BF16, kind="ExternalInput").ap()
        d["id_bf"] = nc.dram_tensor("c_id_bf", [128, 128], BF16, kind="ExternalInput").ap()
        d["id_f"] = nc.dram_tensor("c_id_f", [128, 128], F32, kind="ExternalInput").ap()
        d["triu_bf"] = nc.dram_tensor("c_triu_bf", [64, 64], BF16, kind="ExternalInput").ap()
        d["scal"] = nc.dram_tensor("c_scal", [128, 8], F32, kind="ExternalInput").ap()
        d["onescol"] = nc.dram_tensor("c_onescol_bf", [128, 1], BF16, kind="ExternalInput").ap()
        b.op("sync", "dma_start", out=self.ones_f[:], in_=d["ones_f"], i16=sL)
        b.op("sync", "dma_start", out=self.ones_bf[:], in_=d["ones_bf"], i16=sL)
        b.op("sync", "dma_start", out=self.id_bf[:], in_=d["id_bf"], i16=sL)
        b.op("sync", "dma_start", out=self.id_f[:], in_=d["id_f"], i16=sL)
        b.op("sync", "dma_start", out=self.triu_bf[:], in_=d["triu_bf"], i16=sL)
        b.op("sync", "dma_start", out=self.scal[:], in_=d["scal"], i16=sL)
        b.op("sync", "dma_start", out=self.ones_bf128[:], in_=d["onescol"], i16=sL)
        self.tick = sL.n

    def w(self):
        return (self.sem, self.tick)


def emit_ghostw(b, C, sems, gidx, uidx, didx, lut, wgt_out, wut_out, wdt_out):
    nc = b.nc
    sL, sE, sM, sS = sems
    lutb = nc.alloc_sbuf_tensor("g_lutb", [128, 16], F32)
    b.op("gpsimd", "dma_start",
         out=lutb[:], in_=lut[None, :].broadcast_to((128, 16)), i16=sL)

    for (mat, idx, out_ap, n_ot, hwid, kT) in (
        ("g", gidx, wgt_out, 6, 2048, 16),
        ("u", uidx, wut_out, 6, 2048, 16),
        ("d", didx, wdt_out, 2, 5504, 43),
    ):
        idxf = nc.alloc_sbuf_tensor(f"g_idxf_{mat}", [128, hwid], F32)
        acc = nc.alloc_sbuf_tensor(f"g_acc_{mat}", [128, hwid], BF16)
        tmp = nc.alloc_sbuf_tensor(f"g_tmp_{mat}", [128, hwid], BF16)
        wtb = nc.alloc_sbuf_tensor(f"g_wtb_{mat}", [128, 512], BF16)
        pt = nc.alloc_psum_tensor(f"g_pt_{mat}", [128, 512], BF16)
        for t in range(n_ot):
            b.op("gpsimd", "dma_start", out=idxf[:],
                 in_=idx[t * 128:(t + 1) * 128, :],
                 w=[(sE, sE.n)], i16=sL)
            ldt = sL.n
            b.op("vector", "memset", acc[:], 0.0, w=[(sM, sM.n)])
            for v in range(16):
                wv = [(sL, ldt)] if v == 0 else []
                b.op("vector", "tensor_scalar", tmp[:], idxf[:], float(v),
                     lutb[:, v:v + 1], ALU.is_equal, ALU.mult, w=wv)
                b.op("vector", "tensor_add", acc[:], acc[:], tmp[:],
                     i=(sE if v == 15 else None))
            gat = sE.n
            for k4 in range((kT + 3) // 4):
                ks = [k4 * 4 + j for j in range(4) if k4 * 4 + j < kT]
                for jj, k in enumerate(ks):
                    b.raw("tensor",
                          (lambda e, _a=acc[:, k * 128:(k + 1) * 128],
                           _o=pt[:, jj * 128:(jj + 1) * 128], _i=C.id_bf:
                           e.transpose(_o, _a, _i[:])),
                          w=([(sE, gat), (sE, sE.n)] if jj == 0 else []),
                          i=(sM if jj == len(ks) - 1 else None))
                mmt = sM.n
                b.op("vector", "tensor_copy", wtb[:, 0:len(ks) * 128],
                     pt[:, 0:len(ks) * 128], w=[(sM, mmt), (sS, sS.n)], i=sE)
                evt = sE.n
                for jj, k in enumerate(ks):
                    b.op("sync", "dma_start",
                         out=out_ap[t, :, k, :],
                         in_=wtb[:, jj * 128:(jj + 1) * 128],
                         w=([(sE, evt)] if jj == 0 else []), i16=sS)
    b.op("sync", "nop", w=[(sS, sS.n)])


class LNState:
    def __init__(self, nc, tag, maxw=515):
        self.maxw = maxw
        self.x2 = nc.alloc_sbuf_tensor(f"ln_x2_{tag}", [128, KCH * maxw], F32)
        self.r0 = nc.alloc_sbuf_tensor(f"ln_r0_{tag}", [1, maxw], F32)
        self.r1 = nc.alloc_sbuf_tensor(f"ln_r1_{tag}", [1, maxw], F32)
        self.r2 = nc.alloc_sbuf_tensor(f"ln_r2_{tag}", [1, maxw], F32)
        self.mu_bf = nc.alloc_sbuf_tensor(f"ln_mu_{tag}", [1, maxw], BF16)
        self.rs_bf = nc.alloc_sbuf_tensor(f"ln_rs_{tag}", [1, maxw], BF16)
        self.mubc = nc.alloc_sbuf_tensor(f"ln_mubc_{tag}", [128, maxw], BF16)
        self.rsbc = nc.alloc_sbuf_tensor(f"ln_rsbc_{tag}", [128, maxw], BF16)


def emit_ln(b, C, st, sems, psA, psB, res_tile, xt, Tt, subs, res_tick, out_sem,
            xt_stride=None, xt_off=0):
    sL, sE, sA, sM, sS = sems
    if xt_stride is None:
        xt_stride = Tt
    for k in range(KCH):
        b.op("scalar", "activation", st.x2[:, k * Tt:(k + 1) * Tt],
             res_tile[:, k * Tt:(k + 1) * Tt], AF.Square,
             w=([res_tick, (sM, sM.n)] if k == 0 else []),
             i=(sA if k == KCH - 1 else None))
    x2t = sA.n
    for si, (so, sw) in enumerate(subs):
        for k in range(KCH):
            b.raw("tensor",
                  (lambda e, _o=psA[0:1, 0:sw],
                   _r=res_tile[:, k * Tt + so:k * Tt + so + sw], _k=k:
                   e.matmul(_o, C.ones_f[:], _r, start=(_k == 0),
                            stop=(_k == KCH - 1))),
                  w=([res_tick, (sE, sE.n)] if k == 0 else []))
        for k in range(KCH):
            b.raw("tensor",
                  (lambda e, _o=psB[0:1, 0:sw],
                   _r=st.x2[:, k * Tt + so:k * Tt + so + sw], _k=k:
                   e.matmul(_o, C.ones_f[:], _r, start=(_k == 0),
                            stop=(_k == KCH - 1))),
                  w=([(sA, x2t)] if k == 0 else []),
                  i=(sM if k == KCH - 1 else None))
        smt = sM.n
        b.op("vector", "tensor_scalar", st.r0[0:1, so:so + sw], psA[0:1, 0:sw],
             1.0 / D, None, ALU.mult, w=[(sM, smt)])
        b.op("vector", "tensor_scalar", st.r1[0:1, so:so + sw], psB[0:1, 0:sw],
             1.0 / D, None, ALU.mult)
        b.op("vector", "tensor_mul", st.r2[0:1, so:so + sw],
             st.r0[0:1, so:so + sw], st.r0[0:1, so:so + sw])
        b.op("vector", "tensor_sub", st.r1[0:1, so:so + sw],
             st.r1[0:1, so:so + sw], st.r2[0:1, so:so + sw])
        b.op("vector", "tensor_copy", st.mu_bf[0:1, so:so + sw],
             st.r0[0:1, so:so + sw], i=sE)
        vet = sE.n
        b.op("scalar", "activation", st.r2[0:1, so:so + sw],
             st.r1[0:1, so:so + sw], AF.Ln, bias=C.scal[0:1, 0:1], w=[(sE, vet)])
        b.op("scalar", "activation", st.r2[0:1, so:so + sw],
             st.r2[0:1, so:so + sw], AF.Exp, scale=-0.5, i=sA)
        rst = sA.n
        b.op("vector", "tensor_copy", st.rs_bf[0:1, so:so + sw],
             st.r2[0:1, so:so + sw], w=[(sA, rst)], i=sE)
        cvt = sE.n
        b.raw("tensor",
              (lambda e, _o=psA[:, 0:sw], _r=st.mu_bf[0:1, so:so + sw]:
               e.matmul(_o, C.ones_bf[:], _r, start=True, stop=True)),
              w=[(sE, cvt)])
        b.raw("tensor",
              (lambda e, _o=psB[:, 0:sw], _r=st.rs_bf[0:1, so:so + sw]:
               e.matmul(_o, C.ones_bf[:], _r, start=True, stop=True)), i=sM)
        bct = sM.n
        b.op("vector", "tensor_copy", st.mubc[:, so:so + sw], psA[:, 0:sw],
             w=[(sM, bct)])
        b.op("vector", "tensor_copy", st.rsbc[:, so:so + sw], psB[:, 0:sw], i=sE)
        bce = sE.n
        for k in range(KCH):
            b.op("vector", "tensor_sub", st.x2[:, k * Tt + so:k * Tt + so + sw],
                 res_tile[:, k * Tt + so:k * Tt + so + sw],
                 st.mubc[:, so:so + sw],
                 w=([(sE, bce)] if k == 0 else []))
            b.op("vector", "tensor_mul",
                 xt[:, k * xt_stride + xt_off + so:k * xt_stride + xt_off + so + sw],
                 st.x2[:, k * Tt + so:k * Tt + so + sw],
                 st.rsbc[:, so:so + sw],
                 i=(out_sem if (k == KCH - 1 and si == len(subs) - 1) else None))


def emit_ffn0(b, C, tag, T, resin, resout, wgt, wut, wdt):
    nc = b.nc
    sL = b.sem(f"{tag}_L")
    sW = b.sem(f"{tag}_W")
    sE = b.sem(f"{tag}_E")
    sA = b.sem(f"{tag}_A")
    sM = b.sem(f"{tag}_M")
    sS = b.sem(f"{tag}_S")
    sems = (sL, sE, sA, sM, sS)

    maxw = 515 if T == TOKH else 512
    RES = nc.alloc_sbuf_tensor(f"{tag}_res", [128, KCH * maxw], F32)
    XT = nc.alloc_sbuf_tensor(f"{tag}_xt", [128, KCH * maxw], BF16)
    H = nc.alloc_sbuf_tensor(f"{tag}_h", [128, NOT * maxw], BF16)
    SG = nc.alloc_sbuf_tensor(f"{tag}_sg", [128, maxw], BF16)
    WG = [nc.alloc_sbuf_tensor(f"{tag}_wg{i}", [128, 16 * 128], BF16) for i in range(2)]
    WU = [nc.alloc_sbuf_tensor(f"{tag}_wu{i}", [128, 16 * 128], BF16) for i in range(2)]
    WD = [nc.alloc_sbuf_tensor(f"{tag}_wd{i}", [128, 43 * 128], BF16) for i in range(2)]
    ln = LNState(nc, tag, maxw)
    RO = ln.x2
    PG = [nc.alloc_psum_tensor(f"{tag}_pg{i}", [128, 512], F32) for i in range(2)]
    PU = [nc.alloc_psum_tensor(f"{tag}_pu{i}", [128, 512], F32) for i in range(2)]
    PD = [nc.alloc_psum_tensor(f"{tag}_pd{i}", [128, 512], F32) for i in range(2)]

    prev_store = 0
    for (toff, Tt, subs) in tok_tiles(T):
        for k in range(KCH):
            b.op("sync", "dma_start", out=RES[:, k * Tt:(k + 1) * Tt],
                 in_=resin[k * 128:(k + 1) * 128, toff:toff + Tt],
                 w=([(sS, prev_store), C.w()] if k == 0 else []), i16=sL)
        res_tick = (sL, sL.n)
        sXT = b.sem(f"{tag}_x{toff}")
        emit_ln(b, C, ln, sems, PG[0][:, :], PU[0][:, :], RES, XT, Tt, subs,
                res_tick, sXT)
        xt_tick = (sXT, sXT.n)

        for t in range(NOT):
            par = t % 2
            wv = [(sM, sM.n)] if t >= 2 else []
            b.op("gpsimd", "dma_start", out=WG[par][:],
                 in_=wgt[t, :, :, :].rearrange("p k o -> p (k o)"),
                 w=wv, i16=sW)
            b.op("gpsimd", "dma_start", out=WU[par][:],
                 in_=wut[t, :, :, :].rearrange("p k o -> p (k o)"), i16=sW)
            wt = sW.n
            for (so, sw) in subs:
                for k in range(KCH):
                    b.raw("tensor",
                          (lambda e, _o=PG[par][:, 0:sw],
                           _l=WG[par][:, k * 128:(k + 1) * 128],
                           _r=XT[:, k * Tt + so:k * Tt + so + sw],
                           _k=k: e.matmul(_o, _l, _r, start=(_k == 0),
                                          stop=(_k == KCH - 1))),
                          w=([(sW, wt), xt_tick, (sA, sA.n), (sE, sE.n)]
                             if k == 0 else []))
                for k in range(KCH):
                    b.raw("tensor",
                          (lambda e, _o=PU[par][:, 0:sw],
                           _l=WU[par][:, k * 128:(k + 1) * 128],
                           _r=XT[:, k * Tt + so:k * Tt + so + sw],
                           _k=k: e.matmul(_o, _l, _r, start=(_k == 0),
                                          stop=(_k == KCH - 1))),
                          i=(sM if k == KCH - 1 else None))
                mmt = sM.n
                b.op("scalar", "activation", SG[:, so:so + sw], PG[par][:, 0:sw],
                     AF.Silu, w=[(sM, mmt)], i=sA)
                b.op("vector", "tensor_mul", H[:, t * Tt + so:t * Tt + so + sw],
                     SG[:, so:so + sw], PU[par][:, 0:sw],
                     w=[(sA, sA.n)], i=sE)
        h_tick = (sE, sE.n)

        for m in range(KCH):
            par = m % 2
            wv = [(sM, sM.n)] if m >= 2 else []
            b.op("gpsimd", "dma_start", out=WD[par][:],
                 in_=wdt[m, :, :, :].rearrange("p k o -> p (k o)"), w=wv, i16=sW)
            wt = sW.n
            for (so, sw) in subs:
                for k in range(NOT):
                    b.raw("tensor",
                          (lambda e, _o=PD[par][:, 0:sw],
                           _l=WD[par][:, k * 128:(k + 1) * 128],
                           _r=H[:, k * Tt + so:k * Tt + so + sw],
                           _k=k: e.matmul(_o, _l, _r, start=(_k == 0),
                                          stop=(_k == NOT - 1))),
                          w=([(sW, wt), h_tick, (sE, sE.n)] if k == 0 else []),
                          i=(sM if k == NOT - 1 else None))
                mmt = sM.n
                b.op("vector", "scalar_tensor_tensor",
                     RO[:, m * Tt + so:m * Tt + so + sw], PD[par][:, 0:sw], 0.5,
                     RES[:, m * Tt + so:m * Tt + so + sw], ALU.mult, ALU.add,
                     w=[(sM, mmt)], i=sE)
            b.op("sync", "dma_start",
                 out=resout[m * 128:(m + 1) * 128, toff:toff + Tt],
                 in_=RO[:, m * Tt:(m + 1) * Tt], w=[(sE, sE.n)], i16=sS)
        prev_store = sS.n
    b.op("sync", "nop", w=[(sS, sS.n)])


def build_s0():
    nc = bass.Bass("TRN2", target_bir_lowering=False, debug=False)
    gidx = nc.dram_tensor("gidx", [768, 2048], I32, kind="ExternalInput").ap()
    uidx = nc.dram_tensor("uidx", [768, 2048], I32, kind="ExternalInput").ap()
    didx = nc.dram_tensor("didx", [256, 5504], I32, kind="ExternalInput").ap()
    lut = nc.dram_tensor("lut", [16], F32, kind="ExternalInput").ap()
    wgt = nc.dram_tensor("wgt_s", [6, 128, 16, 128], BF16, kind="ExternalOutput").ap()
    wut = nc.dram_tensor("wut_s", [6, 128, 16, 128], BF16, kind="ExternalOutput").ap()
    wdt = nc.dram_tensor("wdt_s", [2, 128, 43, 128], BF16, kind="ExternalOutput").ap()
    b = B(nc)
    sL, sE, sM, sS = b.sem("L"), b.sem("E"), b.sem("M"), b.sem("S")
    C = Consts(b, sL)
    emit_ghostw(b, C, (sL, sE, sM, sS), gidx, uidx, didx, lut, wgt, wut, wdt)
    b.emit()
    return nc


def build_ffn0_stage(T):
    nc = bass.Bass("TRN2", target_bir_lowering=False, debug=False)
    resin = nc.dram_tensor("resin", [D, T], F32, kind="ExternalInput").ap()
    resout = nc.dram_tensor("resout", [D, T], F32, kind="ExternalOutput").ap()
    wgt = nc.dram_tensor("wgt", [48, 128, 16, 128], BF16, kind="ExternalInput").ap()
    wut = nc.dram_tensor("wut", [48, 128, 16, 128], BF16, kind="ExternalInput").ap()
    wdt = nc.dram_tensor("wdt", [16, 128, 43, 128], BF16, kind="ExternalInput").ap()
    b = B(nc)
    sC = b.sem("C")
    C = Consts(b, sC)
    emit_ffn0(b, C, "f", T, resin, resout, wgt, wut, wdt)
    b.emit()
    return nc


def _shard_x0(x0):
    shards = []
    for c in range(NCORE):
        bb, h = c // 2, c % 2
        if h == 0:
            halo = np.zeros((HALO, D), np.float32)
        else:
            halo = np.asarray(x0[bb, TOK - HALO:TOK, :], np.float32)
        s = np.concatenate([halo, np.asarray(x0[bb, h * TOK:(h + 1) * TOK, :],
                                             np.float32)], axis=0)
        shards.append(np.ascontiguousarray(s.T))
    return shards


def _pad_idx(idx, rows):
    out = np.zeros((rows, idx.shape[1]), np.int32)
    out[:idx.shape[0]] = idx
    return out


def run_ghostw(inputs, consts):
    gidx = _pad_idx(np.asarray(inputs["L0_ffn_gate_idx"], np.int32), 6144)
    uidx = _pad_idx(np.asarray(inputs["L0_ffn_up_idx"], np.int32), 6144)
    didx = np.asarray(inputs["L0_ffn_down_idx"], np.int32)
    lut = np.asarray(inputs["L0_ffn_lut"], np.float32)
    nc0 = build_s0()
    maps0 = [dict(consts,
                  gidx=gidx[c * 768:(c + 1) * 768],
                  uidx=uidx[c * 768:(c + 1) * 768],
                  didx=didx[c * 256:(c + 1) * 256],
                  lut=lut) for c in range(NCORE)]
    r0 = run_bass_kernel_spmd(nc0, maps0, core_ids=list(range(NCORE)))
    wgt = np.ascontiguousarray(np.concatenate(
        [r0.results[c]["wgt_s"] for c in range(NCORE)], axis=0))
    wut = np.ascontiguousarray(np.concatenate(
        [r0.results[c]["wut_s"] for c in range(NCORE)], axis=0))
    wdt = np.ascontiguousarray(np.concatenate(
        [r0.results[c]["wdt_s"] for c in range(NCORE)], axis=0))
    return wgt, wut, wdt


class LN2State:
    def __init__(self, nc, tag):
        self.rs = [nc.alloc_sbuf_tensor(f"l2_rs{i}_{tag}", [128, 512], F32)
                   for i in range(2)]
        self.x2 = [nc.alloc_sbuf_tensor(f"l2_x2{i}_{tag}", [128, 512], F32)
                   for i in range(2)]
        self.tm = nc.alloc_sbuf_tensor(f"l2_tm_{tag}", [128, 512], F32)
        self.r0 = nc.alloc_sbuf_tensor(f"l2_r0_{tag}", [1, 512], F32)
        self.r1 = nc.alloc_sbuf_tensor(f"l2_r1_{tag}", [1, 512], F32)
        self.r2 = nc.alloc_sbuf_tensor(f"l2_r2_{tag}", [1, 512], F32)
        self.mu_bf = nc.alloc_sbuf_tensor(f"l2_mu_{tag}", [1, 512], BF16)
        self.rs_bf = nc.alloc_sbuf_tensor(f"l2_rsb_{tag}", [1, 512], BF16)
        self.mubc = nc.alloc_sbuf_tensor(f"l2_mubc_{tag}", [128, 512], BF16)
        self.rsbc = nc.alloc_sbuf_tensor(f"l2_rsbc_{tag}", [128, 512], BF16)


def emit_ln2(b, C, st, sems, psA, psB, resin, col0, Tt, subs, xt, xt_stride,
             xt_off, out_sem):
    sL, sE, sA, sM, sS = sems
    for si, (so, sw) in enumerate(subs):
        c0 = col0 + so
        ticks = {}
        for k in range(KCH):
            par = k % 2
            b.op("sync", "dma_start", out=st.rs[par][:, 0:sw],
                 in_=resin[k * 128:(k + 1) * 128, c0:c0 + sw],
                 w=[(sE, sE.n), (sM, sM.n if k < 2 else ticks[k - 2]),
                    C.w()], i16=sL)
            ldt = sL.n
            b.op("scalar", "activation", st.x2[par][:, 0:sw], st.rs[par][:, 0:sw],
                 AF.Square,
                 w=[(sL, ldt), (sM, sM.n if k < 2 else ticks[k - 2])], i=sA)
            sqt = sA.n
            b.raw("tensor",
                  (lambda e, _o=psA[0:1, 0:sw], _r=st.rs[par][:, 0:sw],
                   _s=(k == 0), _t=(k == KCH - 1):
                   e.matmul(_o, C.ones_f[:], _r, start=_s, stop=_t)),
                  w=[(sL, ldt)] + ([(sE, sE.n)] if k == 0 else []))
            b.raw("tensor",
                  (lambda e, _o=psB[0:1, 0:sw], _r=st.x2[par][:, 0:sw],
                   _s=(k == 0), _t=(k == KCH - 1):
                   e.matmul(_o, C.ones_f[:], _r, start=_s, stop=_t)),
                  w=[(sA, sqt)], i=sM)
            ticks[k] = sM.n
        smt = sM.n
        b.op("vector", "tensor_scalar", st.r0[0:1, 0:sw], psA[0:1, 0:sw],
             1.0 / D, None, ALU.mult, w=[(sM, smt)])
        b.op("vector", "tensor_scalar", st.r1[0:1, 0:sw], psB[0:1, 0:sw],
             1.0 / D, None, ALU.mult)
        b.op("vector", "tensor_mul", st.r2[0:1, 0:sw],
             st.r0[0:1, 0:sw], st.r0[0:1, 0:sw])
        b.op("vector", "tensor_sub", st.r1[0:1, 0:sw],
             st.r1[0:1, 0:sw], st.r2[0:1, 0:sw])
        b.op("vector", "tensor_copy", st.mu_bf[0:1, 0:sw],
             st.r0[0:1, 0:sw], i=sE)
        vet = sE.n
        b.op("scalar", "activation", st.r2[0:1, 0:sw],
             st.r1[0:1, 0:sw], AF.Ln, bias=C.scal[0:1, 0:1], w=[(sE, vet)])
        b.op("scalar", "activation", st.r2[0:1, 0:sw],
             st.r2[0:1, 0:sw], AF.Exp, scale=-0.5, i=sA)
        rst = sA.n
        b.op("vector", "tensor_copy", st.rs_bf[0:1, 0:sw],
             st.r2[0:1, 0:sw], w=[(sA, rst)], i=sE)
        cvt = sE.n
        b.raw("tensor",
              (lambda e, _o=psA[:, 0:sw], _r=st.mu_bf[0:1, 0:sw]:
               e.matmul(_o, C.ones_bf[:], _r, start=True, stop=True)),
              w=[(sE, cvt)])
        b.raw("tensor",
              (lambda e, _o=psB[:, 0:sw], _r=st.rs_bf[0:1, 0:sw]:
               e.matmul(_o, C.ones_bf[:], _r, start=True, stop=True)), i=sM)
        bct = sM.n
        b.op("vector", "tensor_copy", st.mubc[:, 0:sw], psA[:, 0:sw],
             w=[(sM, bct)])
        b.op("vector", "tensor_copy", st.rsbc[:, 0:sw], psB[:, 0:sw], i=sE)
        bce = sE.n
        for k in range(KCH):
            par = k % 2
            b.op("sync", "dma_start", out=st.rs[par][:, 0:sw],
                 in_=resin[k * 128:(k + 1) * 128, c0:c0 + sw],
                 w=[(sE, sE.n)], i16=sL)
            ldt = sL.n
            b.op("vector", "tensor_sub", st.tm[:, 0:sw], st.rs[par][:, 0:sw],
                 st.mubc[:, 0:sw], w=[(sL, ldt), (sE, bce)])
            xo = k * xt_stride + xt_off + so
            b.op("vector", "tensor_mul", xt[:, xo:xo + sw],
                 st.tm[:, 0:sw], st.rsbc[:, 0:sw],
                 i=(out_sem if (k == KCH - 1 and si == len(subs) - 1) else sE))


SUBS_A = [(0, 512), (512, 512), (1024, 512), (1536, 256), (1792, 259)]
SUBS_R = [(0, 512), (512, 512), (1024, 512), (1536, 512)]


def emit_mamba(b, C, tag, resin, resout, Wm, hpout, scr, stop_after=99,
               nblocks=4, dbg=(None, None, None, None)):
    nc = b.nc
    sL = b.sem(f"{tag}_L")
    sW = b.sem(f"{tag}_W")
    sE = b.sem(f"{tag}_E")
    sG = b.sem(f"{tag}_G")
    sA = b.sem(f"{tag}_A")
    sM = b.sem(f"{tag}_M")
    sS = b.sem(f"{tag}_S")
    sems = (sL, sE, sA, sM, sS)
    xraw, yscr, szscr = scr

    XT = nc.alloc_sbuf_tensor(f"{tag}_xt", [128, 32880], BF16)
    XB = XT[:, 0:16480]
    VF = XT[:, 16480:32864]
    YFM = XT[:, 0:16384]
    RH = XT[:, 0:32768]
    Bmat = nc.alloc_sbuf_tensor(f"{tag}_bm", [64, TOKH], BF16)
    Cmat = nc.alloc_sbuf_tensor(f"{tag}_cm", [64, TOKH], BF16)
    LA = nc.alloc_sbuf_tensor(f"{tag}_la", [64, TOKH], F32)
    VT0 = nc.alloc_sbuf_tensor(f"{tag}_vt", [64, 4096], BF16)
    VT = [VT0, VT0]
    HP = nc.alloc_sbuf_tensor(f"{tag}_hp", [64, 4096], F32)
    GA = nc.alloc_sbuf_tensor(f"{tag}_ga", [64, 4096], BF16)
    YSB = nc.alloc_sbuf_tensor(f"{tag}_ysb", [64, 4096], BF16)
    DID = nc.alloc_sbuf_tensor(f"{tag}_did", [64, 4096], BF16)
    DB64 = nc.alloc_sbuf_tensor(f"{tag}_db64", [64, 64], BF16)
    CSF = nc.alloc_sbuf_tensor(f"{tag}_csf", [64, 64], F32)
    RSF = nc.alloc_sbuf_tensor(f"{tag}_rsf", [64, 64], F32)
    RST = nc.alloc_sbuf_tensor(f"{tag}_rst", [64, 64], F32)
    T16 = nc.alloc_sbuf_tensor(f"{tag}_t16", [64, 1024], F32)
    EXPT = nc.alloc_sbuf_tensor(f"{tag}_expt", [64, 64], F32)
    DEC2 = nc.alloc_sbuf_tensor(f"{tag}_dec2", [64, 64], BF16)
    BTM = nc.alloc_sbuf_tensor(f"{tag}_btm", [64, 64], BF16)
    MB = nc.alloc_sbuf_tensor(f"{tag}_mb", [64, 64], BF16)
    TRIU2 = nc.alloc_sbuf_tensor(f"{tag}_triu2", [64, 64], F32)
    TOTC = nc.alloc_sbuf_tensor(f"{tag}_totc", [64, 1], F32)
    DRW2 = nc.alloc_sbuf_tensor(f"{tag}_drw2", [1, 64], BF16)
    DBCC = nc.alloc_sbuf_tensor(f"{tag}_dbcc", [64, 64], BF16)
    CT = [nc.alloc_sbuf_tensor(f"{tag}_ct{i}", [128, 512], F32) for i in range(2)]
    E1 = CT[0][0:64, :]
    E3 = CT[1][0:64, :]
    E2 = T16[0:64, 0:512]
    DTT = T16[0:64, 512:1024]
    ONES64 = nc.alloc_sbuf_tensor(f"{tag}_on64", [1, 64], BF16)
    ONESB = nc.alloc_sbuf_tensor(f"{tag}_onb", [128, 1], BF16)
    CW = nc.alloc_sbuf_tensor(f"{tag}_cw", [128, 128], F32)
    CB = nc.alloc_sbuf_tensor(f"{tag}_cb", [128, 32], F32)
    NS = nc.alloc_sbuf_tensor(f"{tag}_ns", [128, 32], F32)
    NA = nc.alloc_sbuf_tensor(f"{tag}_na", [64, 1], F32)
    DTB = nc.alloc_sbuf_tensor(f"{tag}_dtb", [64, 1], F32)
    DRW = nc.alloc_sbuf_tensor(f"{tag}_drw", [1, 64], F32)
    RSTDBC = nc.alloc_sbuf_tensor(f"{tag}_rsbc", [128, TOK], BF16)
    RR1 = nc.alloc_sbuf_tensor(f"{tag}_rr1", [1, 64], BF16)
    WB = [nc.alloc_sbuf_tensor(f"{tag}_wb{i}", [128, 4096], BF16) for i in range(2)]
    XRT = nc.alloc_sbuf_tensor(f"{tag}_xrt", [128, TOKH + 1], BF16)
    SZT = XRT[:, 0:2048]
    T1Q = XRT[0:64, 0:2048].bitcast(F32)
    RSROWB = nc.alloc_sbuf_tensor(f"{tag}_rsrow", [1, 4096], BF16)
    SJT = [nc.alloc_sbuf_tensor(f"{tag}_sjt{i}", [128, 1024], BF16) for i in range(2)]
    YJT = [nc.alloc_sbuf_tensor(f"{tag}_yjt{i}", [128, 1024], BF16) for i in range(2)]
    RESCH = nc.alloc_sbuf_tensor(f"{tag}_resch", [128, 512], F32)
    ROCH = nc.alloc_sbuf_tensor(f"{tag}_roch", [128, 512], F32)
    TOTB = nc.alloc_sbuf_tensor(f"{tag}_totb", [1, 64], BF16)
    SSQ = nc.alloc_sbuf_tensor(f"{tag}_ssq", [64, 5], F32)
    RSC = nc.alloc_sbuf_tensor(f"{tag}_rsc", [64, 1], F32)
    l2 = LN2State(nc, tag)

    PZ = [nc.alloc_psum_tensor(f"{tag}_pz{i}", [128, 512], F32) for i in range(2)]
    PYA = nc.alloc_psum_tensor(f"{tag}_pya", [64, 1024], F32)
    PYB = nc.alloc_psum_tensor(f"{tag}_pyb", [64, 1024], F32)
    PBF = nc.alloc_psum_tensor(f"{tag}_pbf", [64, 1024], BF16)
    PYT = nc.alloc_psum_tensor(f"{tag}_pyt", [128, 1024], BF16)

    b.op("sync", "dma_start", out=CW[:],
         in_=Wm["cwq"].rearrange("p q j -> p (q j)"), i16=sL)
    b.op("sync", "dma_start", out=CB[:], in_=Wm["cbv"], i16=sL)
    b.op("sync", "dma_start", out=NS[:], in_=Wm["nsv"], i16=sL)
    b.op("sync", "dma_start", out=NA[:], in_=Wm["alog"], i16=sL)
    b.op("sync", "dma_start", out=DTB[:], in_=Wm["dtb"], i16=sL)
    b.op("sync", "dma_start", out=DRW[:], in_=Wm["drow"], i16=sL)
    b.op("sync", "dma_start", out=HP[:], in_=Wm["hpin"], i16=sL)
    pld = sL.n
    b.op("scalar", "activation", NA[:], NA[:], AF.Exp, w=[(sL, pld), C.w()])
    b.op("scalar", "activation", DRW2[:], DRW[:], AF.Identity, i=sA)
    b.op("vector", "tensor_scalar", NA[:], NA[:], -1.0, None, ALU.mult,
         w=[(sA, sA.n)])
    b.op("vector", "memset", ONES64[:], 1.0)
    b.op("vector", "memset", ONESB[:], 1.0, i=sE)
    o64t = sE.n
    b.raw("tensor", (lambda e: e.matmul(PZ[0][0:64, 0:64], ONES64[:], DRW2[:],
                                        start=True, stop=True)),
          w=[(sE, o64t), (sA, sA.n)], i=sM)
    b.op("vector", "tensor_copy", DB64[:], PZ[0][0:64, 0:64], w=[(sM, sM.n)], i=sE)
    b.op("vector", "tensor_scalar", TRIU2[:], C.triu_bf[:], 200.0, None,
         ALU.mult, w=[C.w()])
    b.op("vector", "tensor_tensor",
         DID[:].rearrange("p (h s) -> p h s", h=64),
         C.id_bf[0:64, 0:64][:, None, :].broadcast_to((64, 64, 64)),
         DB64[:, :, None].broadcast_to((64, 64, 64)),
         ALU.mult, w=[(sE, sE.n)], i=sE)
    did_t = sE.n

    sXT = b.sem(f"{tag}_sxt")
    emit_ln2(b, C, l2, sems, PZ[0][:, :], PZ[1][:, :], resin, 0, TOKH, SUBS_A,
             XT, TOKH, 0, sXT)
    xt_tick = (sXT, sXT.n)
    if dbg[2] is not None:
        for k in range(KCH):
            b.op("sync", "dma_start",
                 out=dbg[2][k * 128:(k + 1) * 128, :],
                 in_=XT[:, k * TOKH:(k + 1) * TOKH],
                 w=([xt_tick] if k == 0 else []), i16=sS)

    a_tiles = [("x", 4096 + j * 128, 128, j) for j in range(32)]
    a_tiles += [("B", 8192, 64, None), ("C", 8256, 64, None), ("dt", 8320, 64, None)]
    for idx_t, (kind, c0, ow, j) in enumerate(a_tiles):
        par = idx_t % 2
        wv = [(sM, sM.n)] if idx_t >= 2 else []
        b.op("gpsimd", "dma_start",
             out=WB[par][:, 0:16 * ow].rearrange("p (kk o) -> p kk o", o=ow),
             in_=Wm["ipw"][0:2048, c0:c0 + ow].rearrange("(kk p) o -> p kk o", p=128),
             w=wv, i16=sW)
        wt = sW.n
        for (so, sw) in SUBS_A:
            pp = PZ[par]
            first = (so, sw) == SUBS_A[0]
            for kk in range(KCH):
                b.raw("tensor",
                      (lambda e, _o=pp[0:ow, 0:sw],
                       _l=WB[par][:, kk * ow:(kk + 1) * ow],
                       _r=XT[:, kk * TOKH + so:kk * TOKH + so + sw],
                       _s=(kk == 0), _t=(kk == KCH - 1):
                       e.matmul(_o, _l, _r, start=_s, stop=_t)),
                      w=(([(sW, wt), xt_tick] if first else []) +
                         ([(sE, sE.n), (sA, sA.n)] if kk == 0 else [])),
                      i=(sM if kk == KCH - 1 else None))
            mmt = sM.n
            if kind == "x":
                b.op("vector", "tensor_copy", XRT[:, so:so + sw], pp[:, 0:sw],
                     w=[(sM, mmt)] + ([(sS, sS.n)] if first else []), i=sE)
            elif kind in ("B", "C"):
                dst = Bmat if kind == "B" else Cmat
                b.op("vector", "tensor_scalar_min", E1[:, 0:sw], pp[0:64, 0:sw],
                     0.0, w=[(sM, mmt)], i=sE)
                b.op("scalar", "activation", E2[:, 0:sw], E1[:, 0:sw],
                     AF.Exp, w=[(sE, sE.n)], i=sA)
                b.op("vector", "tensor_relu", E3[:, 0:sw], pp[0:64, 0:sw])
                b.op("vector", "tensor_add", dst[0:64, so:so + sw],
                     E2[:, 0:sw], E3[:, 0:sw], w=[(sA, sA.n)], i=sE)
            else:
                b.op("scalar", "activation", E2[:, 0:sw], pp[0:64, 0:sw],
                     AF.Abs, bias=DTB[:, 0:1], w=[(sM, mmt), (sE, sE.n)])
                b.op("scalar", "activation", E2[:, 0:sw], E2[:, 0:sw],
                     AF.Exp, scale=-1.0)
                b.op("scalar", "activation", E2[:, 0:sw], E2[:, 0:sw],
                     AF.Ln, bias=C.scal[0:64, 3:4])
                b.op("scalar", "activation", E3[:, 0:sw], pp[0:64, 0:sw],
                     AF.Relu, bias=DTB[:, 0:1], i=sA)
                b.op("vector", "tensor_add", DTT[:, 0:sw], E2[:, 0:sw],
                     E3[:, 0:sw], w=[(sA, sA.n)])
                b.op("vector", "tensor_scalar", LA[0:64, so:so + sw],
                     DTT[:, 0:sw], NA[:, 0:1], None, ALU.mult, i=sE)
        if kind == "x":
            b.op("sync", "dma_start", out=xraw[j, :, :], in_=XRT[:, 0:TOKH],
                 w=[(sE, sE.n)], i16=sS)
    m1_mm = sM.n
    m1_ve = sE.n
    xraw_st = sS.n
    if stop_after <= 1:
        return

    yfm_st = xraw_st
    scan_add_t = sE.n
    for sb in range(nblocks):
        for j in range(32):
            b.op("sync", "dma_start", out=XB[:, j * 515:j * 515 + 515],
                 in_=xraw[j, :, sb * 512:sb * 512 + 515],
                 w=([(sS, yfm_st), (sM, m1_mm), (sE, m1_ve), (sA, sA.n)]
                    if j == 0 else []), i16=sL)
        xb_t = sL.n
        conv_ticks = {}
        for j in range(32):
            eng = "vector"
            ssem = sE if eng == "vector" else sG
            par = j % 2
            b.op(eng, "tensor_scalar", CT[par][:, 0:512],
                 XB[:, j * 515 + 0:j * 515 + 512], CW[:, j:j + 1], None,
                 ALU.mult, w=[(sL, xb_t), (sA, sA.n), (sM, sM.n)])
            for q in (1, 2, 3):
                b.op(eng, "scalar_tensor_tensor", CT[par][:, 0:512],
                     XB[:, j * 515 + q:j * 515 + q + 512],
                     CW[:, q * 32 + j:q * 32 + j + 1], CT[par][:, 0:512],
                     ALU.mult, ALU.add, i=(ssem if q == 3 else None))
            conv_ticks[j] = (ssem, ssem.n)
            b.op("scalar", "activation", VF[:, j * 512:(j + 1) * 512],
                 CT[par][:, 0:512], AF.Silu, bias=CB[:, j:j + 1],
                 w=[conv_ticks[j]], i=sA)
        conv_t = sA.n
        if dbg[3] is not None and sb == 0:
            b.op("sync", "dma_start", out=dbg[3][:, 0:16384], in_=VF[:, :],
                 w=[(sA, conv_t)], i16=sS)
        if stop_after <= 2:
            return
        for c in range(8):
            gc = sb * 8 + c
            gcol = 3 + gc * 64
            pv = c % 2
            for g in range(4):
                for jj in range(8):
                    j = g * 8 + jj
                    b.raw("tensor",
                          (lambda e, _o=PBF[0:64, jj * 128:(jj + 1) * 128],
                           _i=VF[:, j * 512 + c * 64:j * 512 + c * 64 + 64]:
                           e.transpose(_o, _i, C.id_bf[:])),
                          w=([(sA, conv_t), (sE, sE.n)] if jj == 0 else []),
                          i=(sM if jj == 7 else None))
                b.op("vector", "tensor_copy",
                     VT[pv][:, g * 1024:(g + 1) * 1024], PBF[0:64, 0:1024],
                     w=[(sM, sM.n)], i=sE)
            vt_t = (sE, sE.n)
            b.op("vector", "tensor_tensor_scan", CSF[:],
                 ONESB[0:64, 0:1].broadcast_to((64, 64)),
                 LA[0:64, gcol:gcol + 64], 0.0, ALU.mult, ALU.add,
                 w=[(sA, sA.n), (sM, sM.n)], i=sE)
            b.op("vector", "tensor_copy", TOTC[:], CSF[:, 63:64],
                 w=[(sE, sE.n)], i=sE)
            b.op("vector", "tensor_tensor", RSF[:], CSF[:],
                 TOTC[:, 0:1].broadcast_to((64, 64)), ALU.subtract,
                 w=[(sE, sE.n)], i=sE)
            cs_t = sE.n
            b.op("gpsimd", "dma_start", out=RSROWB[:, :], in_=RSF[:, :],
                 w=[(sE, cs_t), (sM, sM.n)], i16=sL)
            rsrow_t = sL.n
            b.raw("tensor", (lambda e: e.transpose(PZ[0][0:64, 0:64], CSF[:],
                                                   C.id_f[0:64, 0:64])),
                  w=[(sE, cs_t), (sA, sA.n), (sE, sE.n)])
            b.raw("tensor", (lambda e: e.transpose(PZ[0][0:64, 64:128], RSF[:],
                                                   C.id_f[0:64, 0:64])))
            b.raw("tensor", (lambda e: e.transpose(PZ[0][0:1, 128:192],
                                                   CSF[:, 63:64],
                                                   C.id_f[0:64, 0:64])),
                  i=sM)
            tr_t = sM.n
            b.op("scalar", "activation", EXPT[:], PZ[0][0:64, 0:64], AF.Exp,
                 w=[(sM, tr_t)])
            b.op("scalar", "activation", DEC2[:], PZ[0][0:64, 64:128], AF.Exp,
                 scale=-1.0)
            b.op("scalar", "activation", RST[:], PZ[0][0:64, 64:128],
                 AF.Identity)
            b.op("scalar", "activation", TOTB[:], PZ[0][0:1, 128:192], AF.Exp,
                 i=sA)
            act_t = sA.n
            b.raw("tensor", (lambda e, _l=Bmat[0:64, gcol:gcol + 64],
                             _r=Cmat[0:64, gcol:gcol + 64]:
                             e.matmul(PZ[0][0:64, 192:256], _l, _r,
                                      start=True, stop=True)),
                  w=[(sA, act_t), (sE, sE.n)])
            b.raw("tensor", (lambda e: e.matmul(PZ[0][0:64, 256:320], ONES64[:],
                                                TOTB[:], start=True, stop=True)))
            b.raw("tensor", (lambda e, _i=Bmat[0:64, gcol:gcol + 64]:
                             e.transpose(PBF[0:64, 0:64], _i,
                                         C.id_bf[0:64, 0:64])),
                  i=sM)
            pe2_t = sM.n
            b.op("vector", "tensor_copy", MB[:], PZ[0][0:64, 192:256],
                 w=[(sM, pe2_t)])
            b.op("vector", "tensor_copy", DBCC[:], PZ[0][0:64, 256:320])
            b.op("vector", "tensor_copy", BTM[:], PBF[0:64, 0:64], i=sE)
            ev2_t = sE.n
            b.op("vector", "tensor_copy", YSB[:, 0:2048], HP[0:64, 0:2048],
                 w=[(sE, scan_add_t), (sM, sM.n), (sE, ev2_t)])
            b.op("vector", "tensor_copy", YSB[:, 2048:4096],
                 HP[0:64, 2048:4096], i=sE)
            hpb_t = sE.n
            yq_t = {}
            for q4 in range(4):
                h0 = q4 * 16
                qs = slice(q4 * 1024, (q4 + 1) * 1024)
                for i2 in range(2):
                    b.raw("tensor",
                          (lambda e, _o=PYA[0:64, i2 * 512:(i2 + 1) * 512],
                           _r=RSROWB[0:1, q4 * 1024 + i2 * 512:
                                     q4 * 1024 + (i2 + 1) * 512]:
                           e.matmul(_o, ONES64[:], _r, start=True, stop=True)),
                          w=([(sL, rsrow_t), (sE, hpb_t), (sA, sA.n),
                              (sE, yq_t.get(q4 - 1, scan_add_t))]
                             if i2 == 0 else []),
                          i=(sM if i2 == 1 else None))
                css_t = sM.n
                b.op("vector", "tensor_tensor",
                     T1Q[:].rearrange("p (h s) -> p h s", h=16),
                     TRIU2[:, None, :].broadcast_to((64, 16, 64)),
                     PYA[0:64, 0:1024].rearrange("p (h s) -> p h s", h=16),
                     ALU.add, w=[(sM, css_t), (sA, sA.n)])
                b.op("vector", "tensor_tensor",
                     T1Q[:].rearrange("p (h s) -> p h s", h=16),
                     T1Q[:].rearrange("p (h s) -> p h s", h=16),
                     RST[:, h0:h0 + 16][:, :, None].broadcast_to((64, 16, 64)),
                     ALU.subtract, i=sE)
                t1_t = sE.n
                b.op("scalar", "activation", GA[:, qs], T1Q[:], AF.Exp,
                     bias=C.scal[0:64, 6:7], w=[(sE, t1_t), (sM, sM.n)], i=sA)
                ga_e = sA.n
                b.op("vector", "tensor_tensor",
                     GA[:, qs].rearrange("p (h s) -> p h s", h=16),
                     GA[:, qs].rearrange("p (h s) -> p h s", h=16),
                     MB[:, None, :].broadcast_to((64, 16, 64)), ALU.mult,
                     w=[(sA, ga_e)])
                b.op("vector", "tensor_add", GA[:, qs], GA[:, qs],
                     DID[:, qs], i=sE)
                ga_t = sE.n
                for hh in range(16):
                    h = h0 + hh
                    wv = ([(sE, ga_t)] + ([vt_t] if q4 == 0 else [])
                          if hh == 0 else [])
                    b.raw("tensor",
                          (lambda e, _o=PYA[0:64, hh * 64:(hh + 1) * 64],
                           _l=GA[:, h * 64:(h + 1) * 64],
                           _r=VT[pv][:, h * 64:(h + 1) * 64]:
                           e.matmul(_o, _l, _r, start=True, stop=True)),
                          w=wv)
                    b.raw("tensor",
                          (lambda e, _o=PYB[0:64, hh * 64:(hh + 1) * 64],
                           _l=Cmat[0:64, gcol:gcol + 64],
                           _r=YSB[:, h * 64:(h + 1) * 64]:
                           e.matmul(_o, _l, _r, start=True, stop=True)),
                          i=(sM if hh == 15 else None))
                ymm_t = sM.n
                b.op("vector", "tensor_tensor",
                     T16[:].rearrange("p (h q) -> p h q", h=16),
                     PYB[0:64, 0:1024].rearrange("p (h q) -> p h q", h=16),
                     EXPT[:, h0:h0 + 16][:, :, None].broadcast_to((64, 16, 64)),
                     ALU.mult, w=[(sM, ymm_t)], i=sE)
                b.op("vector", "tensor_add", YSB[:, qs], PYA[0:64, 0:1024],
                     T16[:], w=[(sE, sE.n)], i=sE)
                yq_t[q4] = sE.n
            ysb_all = sE.n
            b.op("vector", "tensor_tensor",
                 GA[:].rearrange("p (h n) -> p h n", h=64),
                 BTM[:, None, :].broadcast_to((64, 64, 64)),
                 DEC2[:, :, None].broadcast_to((64, 64, 64)), ALU.mult,
                 w=[(sM, sM.n)], i=sE)
            ks_t = sE.n
            for hw in range(2):
                for hh in range(32):
                    h = hw * 32 + hh
                    pa = PYA if hh < 16 else PYB
                    b.raw("tensor",
                          (lambda e, _o=pa[0:64, (hh % 16) * 64:(hh % 16 + 1) * 64],
                           _l=GA[:, h * 64:(h + 1) * 64],
                           _r=VT[pv][:, h * 64:(h + 1) * 64]:
                           e.matmul(_o, _l, _r, start=True, stop=True)),
                          w=([(sE, ks_t if hw == 0 else sE.n)] if hh == 0 else []),
                          i=(sM if hh == 31 else None))
                st_t = sM.n
                b.op("vector", "tensor_tensor",
                     HP[0:64, hw * 2048:(hw + 1) * 2048].rearrange(
                         "p (h q) -> p h q", h=32),
                     HP[0:64, hw * 2048:(hw + 1) * 2048].rearrange(
                         "p (h q) -> p h q", h=32),
                     DBCC[:, hw * 32:(hw + 1) * 32][:, :, None].broadcast_to(
                         (64, 32, 64)),
                     ALU.mult, w=[(sM, st_t)], i=sE)
                b.op("vector", "tensor_add",
                     HP[0:64, hw * 2048:hw * 2048 + 1024],
                     HP[0:64, hw * 2048:hw * 2048 + 1024], PYA[0:64, 0:1024],
                     w=[(sE, sE.n)])
                b.op("vector", "tensor_add",
                     HP[0:64, hw * 2048 + 1024:hw * 2048 + 2048],
                     HP[0:64, hw * 2048 + 1024:hw * 2048 + 2048],
                     PYB[0:64, 0:1024], i=sE)
                scan_add_t = sE.n
            b.op("scalar", "activation", PYA[0:64, 0:1024], YSB[:, 0:1024],
                 AF.Square, accum_out=SSQ[:, 0:1],
                 w=[(sE, scan_add_t)])
            b.op("scalar", "activation", PYA[0:64, 0:1024], YSB[:, 1024:2048],
                 AF.Square, accum_out=SSQ[:, 1:2])
            b.op("scalar", "activation", PYB[0:64, 0:1024], YSB[:, 2048:3072],
                 AF.Square, accum_out=SSQ[:, 2:3])
            b.op("scalar", "activation", PYB[0:64, 0:1024], YSB[:, 3072:4096],
                 AF.Square, accum_out=SSQ[:, 3:4], i=sA)
            sq_t = sA.n
            b.op("vector", "tensor_reduce", SSQ[:, 4:5], SSQ[:, 0:4],
                 mybir.AxisListType.X, ALU.add, w=[(sA, sq_t)], i=sE)
            b.op("scalar", "activation", SSQ[:, 4:5], SSQ[:, 4:5], AF.Ln,
                 scale=1.0 / DIN, bias=C.scal[0:64, 1:2], w=[(sE, sE.n)])
            b.op("scalar", "activation", RSC[:, 0:1], SSQ[:, 4:5], AF.Exp,
                 scale=-0.5, i=sA)
            rsc_t = sA.n
            b.raw("tensor", (lambda e: e.transpose(PZ[1][0:1, 0:64],
                                                   RSC[:, 0:1],
                                                   C.id_f[0:64, 0:64])),
                  w=[(sA, rsc_t), (sE, sE.n)], i=sM)
            b.op("vector", "tensor_copy", RR1[0:1, 0:64], PZ[1][0:1, 0:64],
                 w=[(sM, sM.n)], i=sE)
            b.raw("tensor", (lambda e: e.matmul(PZ[1][:, 64:128], C.ones_bf[:],
                                                RR1[0:1, 0:64],
                                                start=True, stop=True)),
                  w=[(sE, sE.n)], i=sM)
            b.op("vector", "tensor_copy", RSTDBC[:, gc * 64:(gc + 1) * 64],
                 PZ[1][:, 64:128], w=[(sM, sM.n)], i=sE)
            YFMr = YFM.rearrange("p (j t) -> p j t", t=512)
            PYTr = PYT[:, 0:1024].rearrange("p (j t) -> p j t", t=64)
            for g in range(2):
                for jj in range(16):
                    j = g * 16 + jj
                    b.raw("tensor",
                          (lambda e, _o=PYT[:, jj * 64:(jj + 1) * 64],
                           _i=YSB[:, j * 128:(j + 1) * 128]:
                           e.transpose(_o, _i, C.id_bf[0:64, 0:64])),
                          w=([(sE, ysb_all), (sE, sE.n), (sA, sq_t)]
                             if jj == 0 else []),
                          i=(sM if jj == 15 else None))
                b.op("vector", "tensor_copy",
                     YFMr[:, g * 16:(g + 1) * 16, c * 64:c * 64 + 64],
                     PYTr, w=[(sM, sM.n)], i=sE)
        for j in range(32):
            b.op("sync", "dma_start", out=yscr[j, :, sb * 512:(sb + 1) * 512],
                 in_=YFM[:, j * 512:(j + 1) * 512],
                 w=([(sE, sE.n)] if j == 0 else []), i16=sS)
        yfm_st = sS.n
    if stop_after <= 3:
        return

    sX2 = b.sem(f"{tag}_sx2")
    emit_ln2(b, C, l2, sems, PZ[0][:, :], PZ[1][:, :], resin, HALO, TOK, SUBS_R,
             XT, TOK, 0, sX2)
    x2_tick = (sX2, sX2.n)
    if stop_after <= 3.5:
        return
    for j in range(32):
        par = j % 2
        wv = [(sM, sM.n)] if j >= 2 else []
        b.op("gpsimd", "dma_start",
             out=WB[par][:, 0:2048].rearrange("p (kk o) -> p kk o", o=128),
             in_=Wm["ipw"][0:2048, j * 128:(j + 1) * 128].rearrange(
                 "(kk p) o -> p kk o", p=128), w=wv, i16=sW)
        wt = sW.n
        for (so, sw) in SUBS_R:
            first = (so, sw) == SUBS_R[0]
            for kk in range(KCH):
                b.raw("tensor",
                      (lambda e, _o=PZ[par][:, 0:sw],
                       _l=WB[par][:, kk * 128:(kk + 1) * 128],
                       _r=XT[:, kk * TOK + so:kk * TOK + so + sw],
                       _s=(kk == 0), _t=(kk == KCH - 1):
                       e.matmul(_o, _l, _r, start=_s, stop=_t)),
                      w=((([(sW, wt)] if first else []) + [(sA, sA.n)] +
                          ([x2_tick] if (first and j < 2) else []))
                         if kk == 0 else []),
                      i=(sM if kk == KCH - 1 else None))
            b.op("scalar", "activation", SZT[:, so:so + sw], PZ[par][:, 0:sw],
                 AF.Silu, w=[(sM, sM.n)] + ([(sS, sS.n)] if first else []),
                 i=sA)
        b.op("sync", "dma_start", out=szscr[j, :, :], in_=SZT[:],
             w=[(sA, sA.n)], i16=sS)
    b.op("sync", "dma_start", out=hpout[:, :], in_=HP[:],
         w=[(sE, scan_add_t)], i16=sS)
    if stop_after <= 4:
        return

    for tt in (0, 1024):
        for j in range(32):
            par = j % 2
            b.op("sync", "dma_start", out=YJT[par][:],
                 in_=yscr[j, :, tt:tt + 1024],
                 w=[(sE, sE.n)] + ([(sM, sM.n), (sS, sS.n)]
                                   if (tt == 0 and j == 0) else []),
                 i16=sL)
            b.op("sync", "dma_start", out=SJT[par][:],
                 in_=szscr[j, :, tt:tt + 1024], i16=sL)
            ldt = sL.n
            b.op("vector", "scalar_tensor_tensor", RH[:, j * 1024:(j + 1) * 1024],
                 YJT[par][:], NS[:, j:j + 1], SJT[par][:], ALU.mult, ALU.mult,
                 w=[(sL, ldt)], i=sE)
        rh_t = sE.n
        for m in range(KCH):
            par = m % 2
            wv = [(sM, sM.n)] if (tt, m) != (0, 0) else [(sE, rh_t)]
            b.op("gpsimd", "dma_start",
                 out=WB[par][:].rearrange("p (kk o) -> p kk o", o=128),
                 in_=Wm["opw"][0:4096, m * 128:(m + 1) * 128].rearrange(
                     "(kk p) o -> p kk o", p=128), w=wv, i16=sW)
            wt = sW.n
            for (so, sw) in ((0, 512), (512, 512)):
                gso = tt + so
                for kk in range(32):
                    b.raw("tensor",
                          (lambda e, _o=PZ[par][:, 0:sw],
                           _l=WB[par][:, kk * 128:(kk + 1) * 128],
                           _r=RH[:, kk * 1024 + so:kk * 1024 + so + sw],
                           _s=(kk == 0), _t=(kk == 31):
                           e.matmul(_o, _l, _r, start=_s, stop=_t)),
                          w=([(sW, wt), (sE, rh_t), (sE, sE.n)] if kk == 0
                             else []),
                          i=(sM if kk == 31 else None))
                mmt = sM.n
                b.op("sync", "dma_start", out=RESCH[:],
                     in_=resin[m * 128:(m + 1) * 128, HALO + gso:HALO + gso + sw],
                     w=[(sE, sE.n)], i16=sL)
                ldt = sL.n
                b.op("vector", "tensor_mul", ROCH[:, 0:sw], PZ[par][:, 0:sw],
                     RSTDBC[:, gso:gso + sw], w=[(sM, mmt), (sS, sS.n)])
                b.op("vector", "tensor_add", ROCH[:, 0:sw], ROCH[:, 0:sw],
                     RESCH[:, 0:sw], w=[(sL, ldt), (sE, sE.n)], i=sE)
                b.op("sync", "dma_start",
                     out=resout[m * 128:(m + 1) * 128, gso:gso + sw],
                     in_=ROCH[:, 0:sw], w=[(sE, sE.n)], i16=sS)
    b.op("sync", "nop", w=[(sS, sS.n)])


def build_mamba_stage(stop_after=99, nblocks=4, dbg=False):
    nc = bass.Bass("TRN2", target_bir_lowering=False, debug=False)
    resin = nc.dram_tensor("resin", [D, TOKH], F32, kind="ExternalInput").ap()
    resout = nc.dram_tensor("resout", [D, TOK], F32, kind="ExternalOutput").ap()
    Wm = {
        "ipw": nc.dram_tensor("ipw", [2048, 8384], F32, kind="ExternalInput").ap(),
        "opw": nc.dram_tensor("opw", [4096, 2048], F32, kind="ExternalInput").ap(),
        "cwq": nc.dram_tensor("cwq", [128, 4, 32], F32, kind="ExternalInput").ap(),
        "cbv": nc.dram_tensor("cbv", [128, 32], F32, kind="ExternalInput").ap(),
        "nsv": nc.dram_tensor("nsv", [128, 32], F32, kind="ExternalInput").ap(),
        "alog": nc.dram_tensor("alog", [64, 1], F32, kind="ExternalInput").ap(),
        "dtb": nc.dram_tensor("dtb", [64, 1], F32, kind="ExternalInput").ap(),
        "drow": nc.dram_tensor("drow", [1, 64], F32, kind="ExternalInput").ap(),
        "hpin": nc.dram_tensor("hpin", [64, 4096], F32, kind="ExternalInput").ap(),
    }
    hpout = nc.dram_tensor("hpout", [64, 4096], F32, kind="ExternalOutput").ap()
    kind = "ExternalOutput" if dbg else "Internal"
    xraw = nc.dram_tensor("xraw", [32, 128, TOKH], BF16, kind=kind).ap()
    yscr = nc.dram_tensor("yscr", [32, 128, TOK], BF16, kind=kind).ap()
    szscr = nc.dram_tensor("szscr", [32, 128, TOK], BF16, kind=kind).ap()
    if dbg:
        dbgx = nc.dram_tensor("dbgx", [D, TOKH], BF16, kind="ExternalOutput").ap()
        dbgv = nc.dram_tensor("dbgv", [64, 3 * TOKH], F32, kind="ExternalOutput").ap()
        dbgr = nc.dram_tensor("dbgr", [128, TOK], BF16, kind="ExternalOutput").ap()
        dbgf = nc.dram_tensor("dbgf", [128, 28672], BF16, kind="ExternalOutput").ap()
    else:
        dbgx = dbgv = dbgr = dbgf = None
    b = B(nc)
    sC = b.sem("C")
    C = Consts(b, sC)
    emit_mamba(b, C, "m", resin, resout, Wm, hpout, (xraw, yscr, szscr),
               stop_after=stop_after, nblocks=nblocks,
               dbg=(dbgv, dbgr, dbgx, dbgf))
    for s in b.sems_all:
        b.op("sync", "nop", w=[(s, s.n)])
    b.emit()
    return nc


def mamba_weight_maps(inputs):
    cw = np.asarray(inputs["L0_attn_conv_weight"], np.float32)
    cwq = np.ascontiguousarray(cw.reshape(4, 32, 128).transpose(2, 0, 1))
    cbv = np.ascontiguousarray(
        np.asarray(inputs["L0_attn_conv_bias"], np.float32).reshape(32, 128).T)
    nsv = np.ascontiguousarray(
        np.asarray(inputs["L0_attn_norm_scale"], np.float32).reshape(32, 128).T)
    return {
        "ipw": np.asarray(inputs["L0_attn_in_proj_w"], np.float32),
        "opw": np.asarray(inputs["L0_attn_out_proj_w"], np.float32),
        "cwq": cwq, "cbv": cbv, "nsv": nsv,
        "alog": np.asarray(inputs["L0_attn_A_log"], np.float32).reshape(64, 1),
        "dtb": np.asarray(inputs["L0_attn_dt_bias"], np.float32).reshape(64, 1),
        "drow": np.asarray(inputs["L0_attn_D_param"], np.float32).reshape(1, 64),
    }


GOLDEN_CENTER = 1.0 / np.e
GOLDEN_LOWER = 0.5 - np.log(4.0 / 3.0)
GOLDEN_UPPER = 0.5


def emit_moe(b, C, tag, resin, resout, Wm):
    nc = b.nc
    accd = Wm["accd"]
    sL = b.sem(f"{tag}_L")
    sW = b.sem(f"{tag}_W")
    sE = b.sem(f"{tag}_E")
    sA = b.sem(f"{tag}_A")
    sM = b.sem(f"{tag}_M")
    sS = b.sem(f"{tag}_S")
    sems = (sL, sE, sA, sM, sS)

    XT = nc.alloc_sbuf_tensor(f"{tag}_xt", [128, 16 * 1024], BF16)
    MID = nc.alloc_sbuf_tensor(f"{tag}_mid", [128, 16 * 1024], BF16)
    SG = nc.alloc_sbuf_tensor(f"{tag}_sg", [128, 512], BF16)
    WGB = [nc.alloc_sbuf_tensor(f"{tag}_wgb{i}", [128, 16 * 256], BF16) for i in range(2)]
    WUB = [nc.alloc_sbuf_tensor(f"{tag}_wub{i}", [128, 16 * 256], BF16) for i in range(2)]
    WDB = WUB
    GAB = nc.alloc_sbuf_tensor(f"{tag}_gab", [128, 32 * 64], BF16)
    UAB = nc.alloc_sbuf_tensor(f"{tag}_uab", [128, 32 * 64], BF16)
    DAB = nc.alloc_sbuf_tensor(f"{tag}_dab", [128, 32 * 64], BF16)
    GBB = nc.alloc_sbuf_tensor(f"{tag}_gbb", [64, 2 * 2048], BF16)
    UBB = nc.alloc_sbuf_tensor(f"{tag}_ubb", [64, 2 * 2048], BF16)
    DBB = nc.alloc_sbuf_tensor(f"{tag}_dbb", [64, 2 * 2048], BF16)
    T64 = nc.alloc_sbuf_tensor(f"{tag}_t64", [64, 1024], BF16)
    T64b = nc.alloc_sbuf_tensor(f"{tag}_t64b", [64, 1024], BF16)
    WRB = nc.alloc_sbuf_tensor(f"{tag}_wrb", [128, 16 * 8], BF16)
    LG = nc.alloc_sbuf_tensor(f"{tag}_lg", [8, 1024], F32)
    LGT = nc.alloc_sbuf_tensor(f"{tag}_lgt", [128, 64], F32)
    RT = [nc.alloc_sbuf_tensor(f"{tag}_rt{i}", [128, 64], F32) for i in range(4)]
    RM = nc.alloc_sbuf_tensor(f"{tag}_rm", [128, 64], mybir.dt.uint8)
    RRED = nc.alloc_sbuf_tensor(f"{tag}_rred", [128, 32], F32)
    INVT = nc.alloc_sbuf_tensor(f"{tag}_invt", [128, 1], F32)
    WEXP = nc.alloc_sbuf_tensor(f"{tag}_wexp", [8, 1024], BF16)
    WROW1 = nc.alloc_sbuf_tensor(f"{tag}_wrow1", [1, 1024], BF16)
    WBC = nc.alloc_sbuf_tensor(f"{tag}_wbc", [128, 1024], BF16)
    RESCH = nc.alloc_sbuf_tensor(f"{tag}_resch", [128, 1024], F32)
    ACCB = nc.alloc_sbuf_tensor(f"{tag}_accb", [128, 1024], F32)
    ROCH = [nc.alloc_sbuf_tensor(f"{tag}_roch{i}", [128, 512], F32)
            for i in range(2)]
    sDA = b.sem(f"{tag}_DA")
    l2 = LN2State(nc, tag)

    PG = nc.alloc_psum_tensor(f"{tag}_pg", [128, 512], F32)
    PU = nc.alloc_psum_tensor(f"{tag}_pu", [128, 512], F32)
    PD = [nc.alloc_psum_tensor(f"{tag}_pd{i}", [128, 512], F32) for i in range(2)]
    PR = nc.alloc_psum_tensor(f"{tag}_pr", [64, 512], F32)
    PL = nc.alloc_psum_tensor(f"{tag}_pl", [128, 1024], F32)
    b.op("gpsimd", "dma_start", out=WRB[:].rearrange("p (k e) -> p k e", e=8),
         in_=Wm["wr"].rearrange("(k p) e -> p k e", p=128), i16=sW)
    for (dst, key) in ((GAB, "ga"), (UAB, "ua"), (DAB, "da")):
        b.op("gpsimd", "dma_start",
             out=dst[:].rearrange("p (ek r) -> p ek r", r=64),
             in_=Wm[key].rearrange("e (k p) r -> p (e k) r", p=128), i16=sW)
    for (dst, key) in ((GBB, "gb"), (UBB, "ub"), (DBB, "db")):
        b.op("gpsimd", "dma_start",
             out=dst[:].rearrange("r (e ko) -> r e ko", e=2),
             in_=Wm[key].rearrange("e r ko -> r e ko"), i16=sW)
    b.op("gpsimd", "dma_start", out=INVT[:],
         in_=Wm["temp"][0:1, 0:1].broadcast_to((128, 1)), i16=sW)
    swt = sW.n
    b.op("vector", "reciprocal", INVT[:], INVT[:], w=[(sW, swt)], i=sE)
    invt_t = sE.n

    subs2 = ((0, 512), (512, 512))
    wexd = Wm["wexd"]
    roch_tick = {}
    prev_store = 0
    for tt in (0, 1024):
        sXT = b.sem(f"{tag}_sxt{tt}")
        emit_ln2(b, C, l2, sems, PG[:, :], PU[:, :], resin, tt, 1024,
                 subs2, XT, 1024, 0, sXT)
        xt_tick = (sXT, sXT.n)
        for si, (so, sw) in enumerate(subs2):
            for kk in range(KCH):
                b.raw("tensor",
                      (lambda e, _o=PL[0:8, 0:sw],
                       _l=WRB[:, kk * 8:(kk + 1) * 8],
                       _r=XT[:, kk * 1024 + so:kk * 1024 + so + sw],
                       _s=(kk == 0), _t=(kk == KCH - 1):
                       e.matmul(_o, _l, _r, start=_s, stop=_t)),
                      w=([(sW, swt), xt_tick, (sE, sE.n), (sA, sA.n)]
                         if kk == 0 else []),
                      i=(sM if kk == KCH - 1 else None))
            b.op("vector", "tensor_copy", LG[0:8, so:so + sw], PL[0:8, 0:sw],
                 w=[(sM, sM.n)], i=sE)
        lg_t = sE.n
        for tc8 in range(8):
            b.raw("tensor",
                  (lambda e, _o=PL[0:128, tc8 * 8:(tc8 + 1) * 8],
                   _i=LG[0:8, tc8 * 128:(tc8 + 1) * 128]:
                   e.transpose(_o, _i, C.id_f[0:8, 0:8])),
                  w=([(sE, lg_t), (sE, sE.n)] if tc8 == 0 else []),
                  i=(sM if tc8 == 7 else None))
        b.op("vector", "tensor_copy", LGT[:, 0:64], PL[0:128, 0:64],
             w=[(sM, sM.n)], i=sE)
        lgt_t = sE.n
        LGT3 = lambda t: t[:, 0:64].rearrange("p (t e) -> p t e", e=8)
        RRED3 = RRED[:, 0:32].rearrange("p (t r) -> p t r", r=4)
        NT8 = 8
        b.op("scalar", "activation", RT[0][:, 0:64], LGT[:, 0:64], AF.Sigmoid,
             scale=INVT[:, 0:1], w=[(sE, lgt_t), (sE, invt_t)], i=sA)
        b.op("scalar", "activation", RT[1][:, 0:64], RT[0][:, 0:64], AF.Abs,
             bias=C.scal[:, 7:8], i=sA)
        sg_t = sA.n
        b.op("vector", "tensor_scalar", RT[2][:, 0:64], RT[0][:, 0:64],
             float(GOLDEN_LOWER), None, ALU.is_ge, w=[(sA, sg_t)])
        b.op("vector", "tensor_scalar", RT[0][:, 0:64], RT[0][:, 0:64],
             float(GOLDEN_UPPER), None, ALU.is_le, i=sE)
        b.op("vector", "tensor_mul", RT[2][:, 0:64], RT[2][:, 0:64],
             RT[0][:, 0:64], w=[(sE, sE.n)], i=sE)
        zone_t = sE.n
        b.op("scalar", "activation", RT[3][:, 0:64], RT[1][:, 0:64], AF.Exp,
             scale=-10.0, w=[(sE, zone_t)])
        b.op("scalar", "activation", RT[1][:, 0:64], RT[1][:, 0:64], AF.Exp,
             scale=-1.0 / 0.3, i=sA)
        exp_t = sA.n
        b.op("vector", "tensor_mul", RT[3][:, 0:64], RT[3][:, 0:64],
             RT[2][:, 0:64], w=[(sA, exp_t)], i=sE)
        b.op("vector", "tensor_reduce", RRED3[:, :, 0:1], LGT3(RT[3]),
             mybir.AxisListType.X, ALU.add, w=[(sE, sE.n)], i=sE)
        b.op("vector", "reduce_max", RRED3[:, :, 1:2], LGT3(RT[1]),
             mybir.AxisListType.X, w=[(sE, sE.n)], i=sE)
        m1_t = sE.n
        b.op("vector", "tensor_tensor", LGT3(RT[2]), LGT3(RT[1]),
             RRED3[:, :, 1:2].broadcast_to((128, NT8, 8)), ALU.is_equal,
             w=[(sE, m1_t)], i=sE)
        b.op("vector", "tensor_scalar", RT[0][:, 0:64], RT[2][:, 0:64],
             -1.0, 1.0, ALU.mult, ALU.add, w=[(sE, sE.n)])
        b.op("vector", "tensor_mul", RT[0][:, 0:64], RT[0][:, 0:64],
             RT[1][:, 0:64], i=sE)
        b.op("vector", "reduce_max", RRED3[:, :, 2:3], LGT3(RT[0]),
             mybir.AxisListType.X, w=[(sE, sE.n)], i=sE)
        m2_t = sE.n
        b.op("vector", "tensor_tensor", LGT3(RT[0]), LGT3(RT[1]),
             RRED3[:, :, 2:3].broadcast_to((128, NT8, 8)), ALU.is_equal,
             w=[(sE, m2_t)], i=sE)
        b.op("vector", "tensor_max", RT[0][:, 0:64], RT[0][:, 0:64],
             RT[2][:, 0:64], w=[(sE, sE.n)])
        b.op("vector", "tensor_mul", RT[1][:, 0:64], RT[1][:, 0:64],
             RT[0][:, 0:64], i=sE)
        fbw_t = sE.n
        b.op("vector", "tensor_reduce", RRED3[:, :, 3:4], LGT3(RT[1]),
             mybir.AxisListType.X, ALU.add, w=[(sE, fbw_t)], i=sE)
        b.op("vector", "tensor_scalar_max", RRED3[:, :, 3:4],
             RRED3[:, :, 3:4], 1e-8, w=[(sE, sE.n)], i=sE)
        b.op("vector", "reciprocal", RRED3[:, :, 3:4], RRED3[:, :, 3:4],
             w=[(sE, sE.n)], i=sE)
        b.op("vector", "tensor_tensor", LGT3(RT[1]), LGT3(RT[1]),
             RRED3[:, :, 3:4].broadcast_to((128, NT8, 8)), ALU.mult,
             w=[(sE, sE.n)], i=sE)
        b.op("vector", "tensor_scalar",
             RM[:, 0:64].rearrange("p (t e) -> p t e", e=8),
             RRED3[:, :, 0:1].broadcast_to((128, NT8, 8)), 1e-8, None,
             ALU.is_lt, w=[(sE, sE.n)], i=sE)
        b.op("vector", "copy_predicated", RT[3][:, 0:64], RM[:, 0:64],
             RT[1][:, 0:64], w=[(sE, sE.n)], i=sE)
        wf_t = sE.n
        b.op("vector", "tensor_reduce", RRED3[:, :, 0:1], LGT3(RT[3]),
             mybir.AxisListType.X, ALU.add, w=[(sE, wf_t)], i=sE)
        b.op("vector", "tensor_scalar_max", RRED3[:, :, 0:1],
             RRED3[:, :, 0:1], 1e-8, w=[(sE, sE.n)], i=sE)
        b.op("vector", "reciprocal", RRED3[:, :, 0:1], RRED3[:, :, 0:1],
             w=[(sE, sE.n)], i=sE)
        b.op("vector", "tensor_tensor", LGT3(RT[3]), LGT3(RT[3]),
             RRED3[:, :, 0:1].broadcast_to((128, NT8, 8)), ALU.mult,
             w=[(sE, sE.n)], i=sE)
        b.op("vector", "tensor_scalar", RT[3][:, 0:64], RT[3][:, 0:64], 0.5,
             None, ALU.mult, w=[(sE, sE.n)], i=sE)
        wtok_t = sE.n
        for tc8 in range(8):
            b.raw("tensor",
                  (lambda e, _o=PL[0:8, tc8 * 128:(tc8 + 1) * 128],
                   _i=RT[3][:, tc8 * 8:(tc8 + 1) * 8]:
                   e.transpose(_o, _i, C.id_f[:])),
                  w=([(sE, wtok_t), (sE, sE.n)] if tc8 == 0 else []),
                  i=(sM if tc8 == 7 else None))
        b.op("vector", "tensor_copy", WEXP[0:8, 0:1024], PL[0:8, 0:1024],
             w=[(sM, sM.n)], i=sE)
        wex_t = sE.n
        b.op("gpsimd", "dma_start", out=wexd[:, tt:tt + 1024],
             in_=WEXP[0:8, 0:1024], w=[(sE, wex_t)], i16=sS)
        wrow_t = sS.n
        for e_i in range(8):
            lowrank = e_i < 2
            b.op("sync", "dma_start", out=WROW1[:, :],
                 in_=wexd[e_i:e_i + 1, tt:tt + 1024],
                 w=[(sS, wrow_t), (sM, sM.n)], i16=sL)
            wr1_t = sL.n
            for i2 in range(2):
                b.raw("tensor",
                      (lambda e, _o=PL[:, i2 * 512:(i2 + 1) * 512],
                       _r=WROW1[0:1, i2 * 512:(i2 + 1) * 512]:
                       e.matmul(_o, C.ones_bf[:], _r, start=True, stop=True)),
                      w=([(sL, wr1_t), (sE, sE.n)] if i2 == 0 else []),
                      i=(sM if i2 == 1 else None))
            b.op("vector", "tensor_copy", WBC[:], PL[:, 0:1024],
                 w=[(sM, sM.n)], i=sE)
            wbc_t = sE.n
            if lowrank:
                for (aw, dst) in ((GAB, T64), (UAB, T64b)):
                    for (so, sw) in subs2:
                        for kk in range(KCH):
                            b.raw("tensor",
                                  (lambda e, _o=PR[0:64, 0:sw],
                                   _l=aw[:, (e_i * 16 + kk) * 64:
                                         (e_i * 16 + kk + 1) * 64],
                                   _r=XT[:, kk * 1024 + so:
                                         kk * 1024 + so + sw],
                                   _s=(kk == 0), _t=(kk == KCH - 1):
                                   e.matmul(_o, _l, _r, start=_s, stop=_t)),
                                  w=([(sW, swt), (sE, sE.n)] if kk == 0 else []),
                                  i=(sM if kk == KCH - 1 else None))
                        b.op("vector", "tensor_copy", dst[0:64, so:so + sw],
                             PR[0:64, 0:sw], w=[(sM, sM.n)], i=sE)
                lr_t = sE.n
            for fb2 in range(8):
                par = fb2 % 2
                wv = [(sM, sM.n)] if not (tt == 0 and e_i == 0 and fb2 < 2) else []
                b.op("gpsimd", "dma_start",
                     out=WGB[par][:].rearrange("p (k o) -> p k o", o=256),
                     in_=Wm["wg"][e_i, :, fb2 * 256:(fb2 + 1) * 256].rearrange(
                         "(k p) o -> p k o", p=128), w=wv, i16=sW)
                b.op("gpsimd", "dma_start",
                     out=WUB[par][:].rearrange("p (k o) -> p k o", o=256),
                     in_=Wm["wu"][e_i, :, fb2 * 256:(fb2 + 1) * 256].rearrange(
                         "(k p) o -> p k o", p=128), i16=sW)
                wt = sW.n
                for fh in range(2):
                    fc = fb2 * 2 + fh
                    for (so, sw) in subs2:
                        for kk in range(KCH):
                            b.raw("tensor",
                                  (lambda e, _o=PG[:, 0:sw],
                                   _l=WGB[par][:, kk * 256 + fh * 128:
                                               kk * 256 + (fh + 1) * 128],
                                   _r=XT[:, kk * 1024 + so:
                                         kk * 1024 + so + sw],
                                   _s=(kk == 0),
                                   _t=(kk == KCH - 1 and not lowrank):
                                   e.matmul(_o, _l, _r, start=_s, stop=_t)),
                                  w=([(sW, wt), (sA, sA.n), (sE, sE.n)]
                                     if kk == 0 else []))
                        if lowrank:
                            b.raw("tensor",
                                  (lambda e, _o=PG[:, 0:sw],
                                   _l=GBB[0:64, e_i * 2048 + fc * 128:
                                          e_i * 2048 + (fc + 1) * 128],
                                   _r=T64[0:64, so:so + sw]:
                                   e.matmul(_o, _l, _r, start=False, stop=True)),
                                  w=[(sE, lr_t)])
                        for kk in range(KCH):
                            b.raw("tensor",
                                  (lambda e, _o=PU[:, 0:sw],
                                   _l=WUB[par][:, kk * 256 + fh * 128:
                                               kk * 256 + (fh + 1) * 128],
                                   _r=XT[:, kk * 1024 + so:
                                         kk * 1024 + so + sw],
                                   _s=(kk == 0),
                                   _t=(kk == KCH - 1 and not lowrank):
                                   e.matmul(_o, _l, _r, start=_s, stop=_t)),
                                  i=(sM if (kk == KCH - 1 and not lowrank)
                                     else None))
                        if lowrank:
                            b.raw("tensor",
                                  (lambda e, _o=PU[:, 0:sw],
                                   _l=UBB[0:64, e_i * 2048 + fc * 128:
                                          e_i * 2048 + (fc + 1) * 128],
                                   _r=T64b[0:64, so:so + sw]:
                                   e.matmul(_o, _l, _r, start=False, stop=True)),
                                  i=sM)
                        mmt = sM.n
                        b.op("scalar", "activation", SG[:, 0:sw], PG[:, 0:sw],
                             AF.Silu, w=[(sM, mmt)], i=sA)
                        b.op("vector", "tensor_mul",
                             MID[:, fc * 1024 + so:fc * 1024 + so + sw],
                             SG[:, 0:sw], PU[:, 0:sw], w=[(sA, sA.n)], i=sE)
                        b.op("vector", "tensor_mul",
                             MID[:, fc * 1024 + so:fc * 1024 + so + sw],
                             MID[:, fc * 1024 + so:fc * 1024 + so + sw],
                             WBC[:, so:so + sw],
                             w=[(sE, wbc_t), (sE, sE.n)], i=sE)
            mid_t = sE.n
            if lowrank:
                for (so, sw) in subs2:
                    for kk in range(KCH):
                        b.raw("tensor",
                              (lambda e, _o=PR[0:64, 0:sw],
                               _l=DAB[:, (e_i * 16 + kk) * 64:
                                      (e_i * 16 + kk + 1) * 64],
                               _r=MID[:, kk * 1024 + so:kk * 1024 + so + sw],
                               _s=(kk == 0), _t=(kk == KCH - 1):
                               e.matmul(_o, _l, _r, start=_s, stop=_t)),
                              w=([(sE, mid_t), (sE, sE.n)] if kk == 0 else []),
                              i=(sM if kk == KCH - 1 else None))
                    b.op("vector", "tensor_copy", T64[0:64, so:so + sw],
                         PR[0:64, 0:sw], w=[(sM, sM.n)], i=sE)
                lrd_t = sE.n
            for mb2 in range(8):
                par = mb2 % 2
                wv = [(sM, sM.n)] if not (tt == 0 and e_i == 0 and mb2 < 2) else []
                b.op("gpsimd", "dma_start",
                     out=WDB[par][:].rearrange("p (k o) -> p k o", o=256),
                     in_=Wm["wd"][e_i, :, mb2 * 256:(mb2 + 1) * 256].rearrange(
                         "(k p) o -> p k o", p=128), w=wv, i16=sW)
                wt = sW.n
                for mh in range(2):
                    m = mb2 * 2 + mh
                    for (so, sw) in subs2:
                        pd = PD[(m * 2 + (so // 512)) % 2]
                        for kk in range(KCH):
                            b.raw("tensor",
                                  (lambda e, _o=pd[:, 0:sw],
                                   _l=WDB[par][:, kk * 256 + mh * 128:
                                               kk * 256 + (mh + 1) * 128],
                                   _r=MID[:, kk * 1024 + so:kk * 1024 + so + sw],
                                   _s=(kk == 0),
                                   _t=(kk == KCH - 1 and not lowrank):
                                   e.matmul(_o, _l, _r, start=_s, stop=_t)),
                                  w=([(sW, wt), (sE, mid_t), (sE, sE.n)]
                                     if kk == 0 else []),
                                  i=(sM if (kk == KCH - 1 and not lowrank)
                                     else None))
                        if lowrank:
                            b.raw("tensor",
                                  (lambda e, _o=pd[:, 0:sw],
                                   _l=DBB[0:64, e_i * 2048 + m * 128:
                                          e_i * 2048 + (m + 1) * 128],
                                   _r=T64[0:64, so:so + sw]:
                                   e.matmul(_o, _l, _r, start=False, stop=True)),
                                  w=[(sE, lrd_t)], i=sM)
                        mmt = sM.n
                        rpar = (m * 2 + so // 512) % 2
                        b.op("vector", "tensor_copy", ROCH[rpar][:, 0:sw],
                             pd[:, 0:sw],
                             w=[(sM, mmt), (sDA, roch_tick.get(rpar, 0))], i=sE)
                        b.op("gpsimd", "dma_start",
                             out=accd[m * 128:(m + 1) * 128, so:so + sw],
                             in_=ROCH[rpar][:, 0:sw],
                             accum_op=(ALU.bypass if e_i == 0 else ALU.add),
                             w=[(sE, sE.n)], i16=sDA)
                        roch_tick[rpar] = sDA.n
        for m in range(KCH):
            b.op("sync", "dma_start", out=RESCH[:],
                 in_=resin[m * 128:(m + 1) * 128, tt:tt + 1024],
                 w=([(sDA, sDA.n)] if m == 0 else []) + [(sE, sE.n)], i16=sL)
            b.op("sync", "dma_start", out=ACCB[:],
                 in_=accd[m * 128:(m + 1) * 128, 0:1024], i16=sL)
            ldt = sL.n
            b.op("vector", "tensor_add", ACCB[:], ACCB[:],
                 RESCH[:], w=[(sL, ldt), (sS, sS.n)], i=sE)
            b.op("sync", "dma_start", out=resout[m * 128:(m + 1) * 128,
                                                 tt:tt + 1024],
                 in_=ACCB[:], w=[(sE, sE.n)], i16=sS)
        prev_store = sS.n
    b.op("sync", "nop", w=[(sS, sS.n)])



def build_moe_stage():
    nc = bass.Bass("TRN2", target_bir_lowering=False, debug=False)
    resin = nc.dram_tensor("resin", [D, TOK], F32, kind="ExternalInput").ap()
    resout = nc.dram_tensor("resout", [D, TOK], F32, kind="ExternalOutput").ap()
    Wm = {
        "wg": nc.dram_tensor("wg", [8, 2048, 2048], F32, kind="ExternalInput").ap(),
        "wu": nc.dram_tensor("wu", [8, 2048, 2048], F32, kind="ExternalInput").ap(),
        "wd": nc.dram_tensor("wd", [8, 2048, 2048], F32, kind="ExternalInput").ap(),
        "ga": nc.dram_tensor("ga", [2, 2048, 64], F32, kind="ExternalInput").ap(),
        "gb": nc.dram_tensor("gb", [2, 64, 2048], F32, kind="ExternalInput").ap(),
        "ua": nc.dram_tensor("ua", [2, 2048, 64], F32, kind="ExternalInput").ap(),
        "ub": nc.dram_tensor("ub", [2, 64, 2048], F32, kind="ExternalInput").ap(),
        "da": nc.dram_tensor("da", [2, 2048, 64], F32, kind="ExternalInput").ap(),
        "db": nc.dram_tensor("db", [2, 64, 2048], F32, kind="ExternalInput").ap(),
        "wr": nc.dram_tensor("wr", [2048, 8], F32, kind="ExternalInput").ap(),
        "temp": nc.dram_tensor("temp", [1, 1], F32, kind="ExternalInput").ap(),
        "wexd": nc.dram_tensor("wexd", [8, 2048], BF16, kind="Internal").ap(),
        "accd": nc.dram_tensor("accd", [2048, 1024], F32, kind="Internal").ap(),
    }
    b = B(nc)
    sC = b.sem("C")
    C = Consts(b, sC)
    emit_moe(b, C, "o", resin, resout, Wm)
    b.emit()
    return nc


def moe_weight_maps(inputs):
    return {
        "wg": np.asarray(inputs["L1_ffn_Wg"], np.float32),
        "wu": np.asarray(inputs["L1_ffn_Wu"], np.float32),
        "wd": np.asarray(inputs["L1_ffn_Wd"], np.float32),
        "ga": np.asarray(inputs["L1_ffn_Ga"], np.float32),
        "gb": np.asarray(inputs["L1_ffn_Gb"], np.float32),
        "ua": np.asarray(inputs["L1_ffn_Ua"], np.float32),
        "ub": np.asarray(inputs["L1_ffn_Ub"], np.float32),
        "da": np.asarray(inputs["L1_ffn_Da"], np.float32),
        "db": np.asarray(inputs["L1_ffn_Db"], np.float32),
        "wr": np.asarray(inputs["L1_ffn_Wr"], np.float32),
        "temp": np.asarray(inputs["L1_ffn_temperature"], np.float32).reshape(1, 1),
    }


def emit_qkv(b, C, tag, resin, qf, kf, vt, Wm):
    nc = b.nc
    sL = b.sem(f"{tag}_L")
    sW = b.sem(f"{tag}_W")
    sE = b.sem(f"{tag}_E")
    sA = b.sem(f"{tag}_A")
    sM = b.sem(f"{tag}_M")
    sS = b.sem(f"{tag}_S")
    sems = (sL, sE, sA, sM, sS)

    XT = nc.alloc_sbuf_tensor(f"{tag}_xt", [128, 16 * 2048], BF16)
    WT = nc.alloc_sbuf_tensor(f"{tag}_wt", [128, 16 * 2048], BF16)
    WS = [nc.alloc_sbuf_tensor(f"{tag}_ws{i}", [128, 2048], F32) for i in range(2)]
    OB = nc.alloc_sbuf_tensor(f"{tag}_ob", [128, 2048], F32)
    OBV = nc.alloc_sbuf_tensor(f"{tag}_obv", [128, 2048], BF16)
    l2 = LN2State(nc, tag)
    PZ = [nc.alloc_psum_tensor(f"{tag}_pz{i}", [128, 512], F32) for i in range(2)]
    PT = nc.alloc_psum_tensor(f"{tag}_pt", [128, 512], F32)

    sXT = b.sem(f"{tag}_sxt")
    emit_ln2(b, C, l2, sems, PZ[0][:, :], PZ[1][:, :], resin, HALO, TOK, SUBS_R,
             XT, TOK, 0, sXT)
    xt_tick = (sXT, sXT.n)

    for wi, (wkey, outap, tmaj) in enumerate(
            (("wq", qf, False), ("wk", kf, False), ("wv", vt, True))):
        for m in range(KCH):
            par = m % 2
            b.op("sync", "dma_start", out=WS[par][:],
                 in_=Wm[wkey][m * 128:(m + 1) * 128, :],
                 w=[(sM, sM.n)], i16=sL)
            wt_ld = sL.n
            for g in range(4):
                for jj in range(4):
                    k = g * 4 + jj
                    b.raw("tensor",
                          (lambda e, _o=PT[:, jj * 128:(jj + 1) * 128],
                           _i=WS[par][:, k * 128:(k + 1) * 128]:
                           e.transpose(_o, _i, C.id_f[:])),
                          w=([(sL, wt_ld), (sE, sE.n)] if jj == 0 else []),
                          i=(sM if jj == 3 else None))
                b.op("vector", "tensor_copy",
                     WT[:, (g * 4) * 2048 // 16 + m * 128:0] if False else
                     WT[:].rearrange("p (k o) -> p k o", o=128)[
                         :, g * 4:g * 4 + 4, m * 128:(m + 1) * 128][:, :, 0:128]
                     if False else
                     WT[:].rearrange("p (kk o) -> p kk o", o=2048)[
                         :, g * 4:g * 4 + 4, m * 128:(m + 1) * 128],
                     PT[:, 0:512].rearrange("p (j o) -> p j o", o=128),
                     w=[(sM, sM.n)], i=sE)
        wt_t = sE.n
        if not tmaj:
            for m in range(KCH):
                for (so, sw) in SUBS_R:
                    for kk in range(KCH):
                        b.raw("tensor",
                              (lambda e, _o=PZ[m % 2][:, 0:sw],
                               _l=WT[:, kk * 2048 + m * 128:
                                      kk * 2048 + (m + 1) * 128],
                               _r=XT[:, kk * TOK + so:kk * TOK + so + sw],
                               _s=(kk == 0), _t=(kk == KCH - 1):
                               e.matmul(_o, _l, _r, start=_s, stop=_t)),
                              w=([(sE, wt_t), xt_tick, (sE, sE.n)]
                                 if kk == 0 else []),
                              i=(sM if kk == KCH - 1 else None))
                    b.op("vector", "tensor_copy", OB[:, so:so + sw],
                         PZ[m % 2][:, 0:sw],
                         w=[(sM, sM.n)] + ([(sS, sS.n)] if so == 0 else []),
                         i=sE)
                b.op("sync", "dma_start",
                     out=outap[m * 128:(m + 1) * 128, :], in_=OB[:],
                     w=[(sE, sE.n)], i16=sS)
        else:
            for m in range(KCH):
                for (so, sw) in SUBS_R:
                    for kk in range(KCH):
                        b.raw("tensor",
                              (lambda e, _o=PZ[m % 2][:, 0:sw],
                               _l=XT[:, kk * TOK + m * 128:
                                     kk * TOK + (m + 1) * 128],
                               _r=WT[:, kk * 2048 + so:kk * 2048 + so + sw],
                               _s=(kk == 0), _t=(kk == KCH - 1):
                               e.matmul(_o, _l, _r, start=_s, stop=_t)),
                              w=([(sE, wt_t), xt_tick, (sE, sE.n)]
                                 if kk == 0 else []),
                              i=(sM if kk == KCH - 1 else None))
                    b.op("vector", "tensor_copy", OB[:, so:so + sw],
                         PZ[m % 2][:, 0:sw],
                         w=[(sM, sM.n)] + ([(sS, sS.n)] if so == 0 else []),
                         i=sE)
                b.op("sync", "dma_start",
                     out=outap[m * 128:(m + 1) * 128, :], in_=OB[:],
                     w=[(sE, sE.n)], i16=sS)
    b.op("sync", "nop", w=[(sS, sS.n)])


def emit_attn(b, C, tag, resin, resout, qf, kfull, vfull, Wm):
    nc = b.nc
    sL = b.sem(f"{tag}_L")
    sW = b.sem(f"{tag}_W")
    sE = b.sem(f"{tag}_E")
    sA = b.sem(f"{tag}_A")
    sM = b.sem(f"{tag}_M")
    sS = b.sem(f"{tag}_S")

    OSB = nc.alloc_sbuf_tensor(f"{tag}_osb", [128, 16 * 2048], BF16)
    EB = nc.alloc_sbuf_tensor(f"{tag}_eb", [128, 32 * 512], F32)
    KH0 = nc.alloc_sbuf_tensor(f"{tag}_kh", [128, 4096], F32)
    KH = [KH0, KH0]
    VH0 = nc.alloc_sbuf_tensor(f"{tag}_vh", [128, 4096], F32)
    VH = [VH0, VH0]
    QH0 = nc.alloc_sbuf_tensor(f"{tag}_qh", [128, 2048], F32)
    QH = [QH0, QH0]
    DEN = nc.alloc_sbuf_tensor(f"{tag}_den", [1, 512], F32)
    DENB = nc.alloc_sbuf_tensor(f"{tag}_denb", [1, 512], BF16)
    DBC = nc.alloc_sbuf_tensor(f"{tag}_dbc", [128, 512], BF16)
    WOS = [nc.alloc_sbuf_tensor(f"{tag}_wos{i}", [128, 2048], F32) for i in range(2)]
    WOT = [nc.alloc_sbuf_tensor(f"{tag}_wot{i}", [128, 2048], BF16) for i in range(2)]
    RESCH = nc.alloc_sbuf_tensor(f"{tag}_resch", [128, 512], F32)
    ROCH = nc.alloc_sbuf_tensor(f"{tag}_roch", [128, 512], F32)
    PST = [nc.alloc_psum_tensor(f"{tag}_pst{i}", [128, 512], F32) for i in range(2)]
    POV = [nc.alloc_psum_tensor(f"{tag}_pov{i}", [128, 512], F32) for i in range(2)]
    PDN = nc.alloc_psum_tensor(f"{tag}_pdn", [128, 512], F32)
    PTT = nc.alloc_psum_tensor(f"{tag}_ptt", [128, 512], F32)

    kh_tick = {}
    vh_tick = {}
    qh_tick = {}
    for h in range(16):
        par = h % 2
        b.op("sync", "dma_start", out=KH[par][:, 0:2048],
             in_=kfull[h * 128:(h + 1) * 128, :],
             w=[(sM, kh_tick.get(h - 1, 0))], i16=sL)
        b.op("sync", "dma_start", out=KH[par][:, 2048:4096],
             in_=kfull[2048 + h * 128:2048 + (h + 1) * 128, :], i16=sL)
        b.op("sync", "dma_start", out=QH[par][:],
             in_=qf[h * 128:(h + 1) * 128, :], i16=sL)
        b.op("gpsimd", "dma_start",
             out=VH[par][:].rearrange("p (c o) -> p c o", o=128),
             in_=vfull[:, h * 128:(h + 1) * 128].rearrange(
                 "(c p) o -> p c o", p=128),
             w=[(sM, vh_tick.get(h - 1, 0))], i16=sL)
        ld_t = sL.n
        for qt in range(4):
            for kt in range(32):
                b.raw("tensor",
                      (lambda e, _o=PST[(qt * 32 + kt) % 2][:, 0:512],
                       _l=KH[par][:, kt * 128:(kt + 1) * 128],
                       _r=QH[par][:, qt * 512:(qt + 1) * 512]:
                       e.matmul(_o, _l, _r, start=True, stop=True)),
                      w=([(sL, ld_t), (sA, sA.n), (sE, sE.n)]
                         if (qt == 0 and kt == 0) else
                         ([(sA, sA.n)] if kt % 8 == 0 else [])),
                      i=(sM if kt % 8 == 7 else None))
                if kt % 8 == 7:
                    for kt2 in range(kt - 7, kt + 1):
                        b.op("scalar", "activation",
                             EB[:, kt2 * 512:(kt2 + 1) * 512],
                             PST[(qt * 32 + kt2) % 2][:, 0:512], AF.Exp,
                             scale=float(1.0 / np.sqrt(128.0)),
                             bias=C.scal[:, 2:3],
                             w=([(sM, sM.n)] if kt2 == kt - 7 else []),
                             i=(sA if kt2 == kt else None))
            ex_t = sA.n
            for kt in range(32):
                b.raw("tensor",
                      (lambda e, _o=POV[qt % 2][:, 0:512],
                       _l=VH[par][:, kt * 128:(kt + 1) * 128],
                       _r=EB[:, kt * 512:(kt + 1) * 512],
                       _s=(kt == 0), _t=(kt == 31):
                       e.matmul(_o, _l, _r, start=_s, stop=_t)),
                      w=([(sA, ex_t), (sE, sE.n)] if kt == 0 else []))
            for kt in range(32):
                b.raw("tensor",
                      (lambda e, _o=PDN[0:1, 0:512],
                       _r=EB[:, kt * 512:(kt + 1) * 512],
                       _s=(kt == 0), _t=(kt == 31):
                       e.matmul(_o, C.ones_f[:], _r, start=_s, stop=_t)),
                      i=(sM if kt == 31 else None))
            ov_t = sM.n
            b.op("vector", "tensor_copy", DEN[0:1, 0:512],
                 PDN[0:1, 0:512], w=[(sM, ov_t), (sM, sM.n)], i=sE)
            b.op("vector", "reciprocal", DEN[0:1, 0:512],
                 DEN[0:1, 0:512], w=[(sE, sE.n)], i=sE)
            b.op("vector", "tensor_copy", DENB[0:1, 0:512],
                 DEN[0:1, 0:512], w=[(sE, sE.n)], i=sE)
            dn_t = sE.n
            b.raw("tensor",
                  (lambda e, _o=PDN[0:128, 0:512],
                   _r=DENB[0:1, 0:512]:
                   e.matmul(_o, C.ones_bf[:], _r, start=True, stop=True)),
                  w=[(sE, dn_t)], i=sM)
            b.op("vector", "tensor_copy", DBC[:], PDN[0:128, 0:512],
                 w=[(sM, sM.n)], i=sE)
            b.op("vector", "tensor_mul",
                 OSB[:, h * 2048 + qt * 512:h * 2048 + (qt + 1) * 512],
                 POV[qt % 2][:, 0:512], DBC[:], w=[(sE, sE.n)], i=sE)
        kh_tick[h] = sM.n
        vh_tick[h] = sM.n
        qh_tick[h] = sM.n
    osb_t = sE.n
    for m in range(KCH):
        par = m % 2
        b.op("sync", "dma_start", out=WOS[par][:],
             in_=Wm["wo"][m * 128:(m + 1) * 128, :],
             w=[(sM, sM.n)], i16=sL)
        wo_ld = sL.n
        for g in range(4):
            for jj in range(4):
                k = g * 4 + jj
                b.raw("tensor",
                      (lambda e, _o=PTT[:, jj * 128:(jj + 1) * 128],
                       _i=WOS[par][:, k * 128:(k + 1) * 128]:
                       e.transpose(_o, _i, C.id_f[:])),
                      w=([(sL, wo_ld), (sE, sE.n)] if jj == 0 else []),
                      i=(sM if jj == 3 else None))
            b.op("vector", "tensor_copy",
                 WOT[par][:].rearrange("p (kk o) -> p kk o", o=128)[
                     :, g * 4:g * 4 + 4, :],
                 PTT[:, 0:512].rearrange("p (j o) -> p j o", o=128),
                 w=[(sM, sM.n)], i=sE)
        wo_t = sE.n
        for (so, sw) in SUBS_R:
            for hk in range(16):
                b.raw("tensor",
                      (lambda e, _o=PST[m % 2][:, 0:sw],
                       _l=WOT[par][:, hk * 128:(hk + 1) * 128],
                       _r=OSB[:, hk * 2048 + so:hk * 2048 + so + sw],
                       _s=(hk == 0), _t=(hk == 15):
                       e.matmul(_o, _l, _r, start=_s, stop=_t)),
                      w=([(sE, wo_t), (sE, osb_t), (sE, sE.n)]
                         if hk == 0 else []),
                      i=(sM if hk == 15 else None))
            mmt = sM.n
            b.op("sync", "dma_start", out=RESCH[:],
                 in_=resin[m * 128:(m + 1) * 128, HALO + so:HALO + so + sw],
                 w=[(sE, sE.n)], i16=sL)
            ldt = sL.n
            b.op("vector", "tensor_add", ROCH[:, 0:sw], PST[m % 2][:, 0:sw],
                 RESCH[:, 0:sw], w=[(sM, mmt), (sL, ldt), (sS, sS.n)], i=sE)
            b.op("sync", "dma_start",
                 out=resout[m * 128:(m + 1) * 128, so:so + sw],
                 in_=ROCH[:, 0:sw], w=[(sE, sE.n)], i16=sS)
    b.op("sync", "nop", w=[(sS, sS.n)])


def build_qkv_stage():
    nc = bass.Bass("TRN2", target_bir_lowering=False, debug=False)
    resin = nc.dram_tensor("resin", [D, TOKH], F32, kind="ExternalInput").ap()
    qf = nc.dram_tensor("qf", [D, TOK], F32, kind="ExternalOutput").ap()
    kf = nc.dram_tensor("kf", [D, TOK], F32, kind="ExternalOutput").ap()
    vt = nc.dram_tensor("vt", [TOK, D], F32, kind="ExternalOutput").ap()
    Wm = {
        "wq": nc.dram_tensor("wq", [D, D], F32, kind="ExternalInput").ap(),
        "wk": nc.dram_tensor("wk", [D, D], F32, kind="ExternalInput").ap(),
        "wv": nc.dram_tensor("wv", [D, D], F32, kind="ExternalInput").ap(),
    }
    b = B(nc)
    sC = b.sem("C")
    C = Consts(b, sC)
    emit_qkv(b, C, "q", resin, qf, kf, vt, Wm)
    for s in b.sems_all:
        b.op("sync", "nop", w=[(s, s.n)])
    b.emit()
    return nc


def build_attn_stage():
    nc = bass.Bass("TRN2", target_bir_lowering=False, debug=False)
    resin = nc.dram_tensor("resin", [D, TOKH], F32, kind="ExternalInput").ap()
    resout = nc.dram_tensor("resout", [D, TOK], F32, kind="ExternalOutput").ap()
    qf = nc.dram_tensor("qf", [D, TOK], F32, kind="ExternalInput").ap()
    kfull = nc.dram_tensor("kfull", [2 * D, TOK], F32, kind="ExternalInput").ap()
    vfull = nc.dram_tensor("vfull", [2 * TOK, D], F32, kind="ExternalInput").ap()
    Wm = {"wo": nc.dram_tensor("wo", [D, D], F32, kind="ExternalInput").ap()}
    b = B(nc)
    sC = b.sem("C")
    C = Consts(b, sC)
    emit_attn(b, C, "a", resin, resout, qf, kfull, vfull, Wm)
    for s in b.sems_all:
        b.op("sync", "nop", w=[(s, s.n)])
    b.emit()
    return nc


def _gather_res(per_core, ncol=TOK, off=0):
    out = np.zeros((4, 4096, D), np.float32)
    for c in range(NCORE):
        bb, h = c // 2, c % 2
        out[bb, h * TOK:(h + 1) * TOK, :] = per_core[c][:, off:off + ncol].T
    return out


def kernel(**inputs):
    consts = mk_consts()
    cores = list(range(NCORE))

    wgt, wut, wdt = run_ghostw(inputs, consts)

    res1s = _shard_x0(np.asarray(inputs["L0_attn_u"], np.float32))
    nc1 = build_ffn0_stage(TOKH)
    r1 = run_bass_kernel_spmd(
        nc1, [dict(consts, resin=res1s[c], wgt=wgt, wut=wut, wdt=wdt)
              for c in cores], core_ids=cores)
    res1 = [np.asarray(r1.results[c]["resout"], np.float32) for c in cores]

    Wmm = mamba_weight_maps(inputs)
    ncm = build_mamba_stage()
    zeros_hp = np.zeros((64, 4096), np.float32)
    r2a = run_bass_kernel_spmd(
        ncm, [dict(consts, resin=res1[c], hpin=zeros_hp, **Wmm)
              for c in cores], core_ids=cores)
    maps2 = []
    for c in cores:
        hp = zeros_hp if c % 2 == 0 else np.asarray(
            r2a.results[c - 1]["hpout"], np.float32)
        maps2.append(dict(consts, resin=res1[c], hpin=hp, **Wmm))
    r2 = run_bass_kernel_spmd(ncm, maps2, core_ids=cores)
    res2 = [np.asarray(r2.results[c]["resout"], np.float32) for c in cores]
    res2s = _shard_x0(_gather_res(res2))

    r3 = run_bass_kernel_spmd(
        nc1, [dict(consts, resin=res2s[c], wgt=wgt, wut=wut, wdt=wdt)
              for c in cores], core_ids=cores)
    res3 = [np.asarray(r3.results[c]["resout"], np.float32) for c in cores]
    res3s = [np.ascontiguousarray(r[:, HALO:]) for r in res3]

    Wmo = moe_weight_maps(inputs)
    nco = build_moe_stage()
    r4 = run_bass_kernel_spmd(
        nco, [dict(consts, resin=res3s[c], **Wmo) for c in cores],
        core_ids=cores)
    res4 = [np.asarray(r4.results[c]["resout"], np.float32) for c in cores]
    res4s = _shard_x0(_gather_res(res4))

    ncq = build_qkv_stage()
    rq = run_bass_kernel_spmd(
        ncq, [dict(consts, resin=res4s[c],
                   wq=np.asarray(inputs["L1_attn_Wq"], np.float32),
                   wk=np.asarray(inputs["L1_attn_Wk"], np.float32),
                   wv=np.asarray(inputs["L1_attn_Wv"], np.float32))
              for c in cores], core_ids=cores)
    nca = build_attn_stage()
    mapsa = []
    for c in cores:
        p = (c // 2) * 2
        kfull = np.concatenate([rq.results[p]["kf"], rq.results[p + 1]["kf"]], 0)
        vfull = np.concatenate([rq.results[p]["vt"], rq.results[p + 1]["vt"]], 0)
        mapsa.append(dict(consts, resin=res4s[c], qf=rq.results[c]["qf"],
                          kfull=kfull, vfull=vfull,
                          wo=np.asarray(inputs["L1_attn_Wo"], np.float32)))
    r5 = run_bass_kernel_spmd(nca, mapsa, core_ids=cores)
    res5 = [np.asarray(r5.results[c]["resout"], np.float32) for c in cores]

    r6 = run_bass_kernel_spmd(
        nco, [dict(consts, resin=res5[c], **Wmo) for c in cores],
        core_ids=cores)
    res6 = [np.asarray(r6.results[c]["resout"], np.float32) for c in cores]

    return _gather_res(res6)
